# version 02
# baseline (speedup 1.0000x reference)
from contextlib import ExitStack

import numpy as np
import concourse.bass as bass
import concourse.mybir as mybir
from concourse.bass_utils import run_bass_kernel_spmd

F32 = mybir.dt.float32
BF16 = mybir.dt.bfloat16
F16 = mybir.dt.float16
U32 = mybir.dt.uint32
IOA = bass.IndirectOffsetOnAxis
FP16 = mybir.dt.float16
I16 = mybir.dt.int16
I32 = mybir.dt.int32
AF = mybir.ActivationFunctionType
ALU = mybir.AluOpType
AX = mybir.AxisListType

D = 1024
TC = 2048
NT = 16
HA, DK, MA = 4, 256, 64
HM, HD = 16, 64
FF = 2048
NE, SLOTS = 8, 256
CAP = 1280
SCALE_A = float(DK) ** -0.5

DEBUG_STAGE = 3
RACE_DETECT = True
PH0_MIN = 0
PROBE_SWITCH = None
DUMP = None
DUMPROWS = 0
STOP_AFTER = None


class _StopEmission(Exception):
    pass


def _chunks_of(core):
    g = core % 4
    return (g, 7 - g)


def _host_constants(core):
    cA, cB = _chunks_of(core)
    pos = np.concatenate([np.arange(cA * 1024, (cA + 1) * 1024),
                          np.arange(cB * 1024, (cB + 1) * 1024)])
    half = DK // 2
    inv = 1.0 / 10000.0 ** (np.arange(half, dtype=np.float64) / half)
    ang = pos[None, :] * inv[:, None]
    cosA = np.cos(ang).astype(np.float32)
    sinA = np.sin(ang).astype(np.float32)
    theta = 10000.0 ** (-2.0 * (np.arange(HD // 2, dtype=np.float64) / HD))
    ang2 = pos[None, :] * theta[:, None]
    mc_, ms_ = np.cos(ang2), np.sin(ang2)
    mcos = np.zeros((HD, TC), np.float32)
    msin = np.zeros((HD, TC), np.float32)
    mcos[0::2] = mc_
    mcos[1::2] = mc_
    msin[0::2] = -ms_
    msin[1::2] = ms_
    mcos2 = np.tile(mcos, (2, 1))
    msin2 = np.tile(msin, (2, 1))
    ii, jj = np.indices((128, 128))
    tri = (ii <= jj).astype(np.float32)
    ustrict = (ii < jj).astype(np.float32)
    negm = np.where(ii > jj, -1e30, 0.0).astype(np.float32)
    ident = np.eye(128, dtype=np.float32)
    perm = np.zeros((128, 128), np.float32)
    for i in range(64):
        perm[2 * i, 2 * i + 1] = 1.0
        perm[2 * i + 1, 2 * i] = 1.0
    iota256 = np.tile(np.arange(256, dtype=np.float32)[None, :], (128, 1))
    idp = np.arange(1, TC + 1, dtype=np.int64).reshape(NT, 128).T
    hib = (idp >> 8).astype(np.float32)
    lob = (idp & 255).astype(np.float32)
    konst = np.tile(np.array([[1e-5, 1.0 / 256.0, 0.0, 0.0]], np.float32), (128, 1))
    onesrow = np.ones((1, 128), np.float32)
    onescol = np.ones((128, 1), np.float32)
    maskA = np.zeros((128, 8), np.float32)
    maskAB = np.zeros((128, 8), np.float32)
    maskA[:, :cA] = 1.0
    maskAB[:, cA + 1:cB] = 1.0
    iotah = np.tile(np.arange(256, dtype=np.float16)[None, :], (128, 1))
    tloch = (np.arange(16)[None, :] * 128 + np.arange(128)[:, None]).astype(np.float16)
    dumy = (2048.0 + (np.arange(16)[None, :] % 2) * 128
            + np.arange(128)[:, None]).astype(np.float32)
    moemask = np.zeros((128, 1024), np.float32)
    moemask.reshape(128, 128, 8)[:, :, core] = 1.0
    ee = np.arange(16)[None, :] // 2
    yrows = (ee * 2048 + core * 256 + (np.arange(16)[None, :] % 2) * 128
             + np.arange(128)[:, None]).astype(np.uint32)
    pat8 = np.zeros((8, 128), np.float32)
    pat8.reshape(8, 16, 8)[np.arange(8), :, np.arange(8)] = 1.0
    return dict(iotah=iotah, tloch=tloch, dumy=dumy, moemask=moemask,
                yrows=yrows, pat8=pat8,
                cosA=cosA, sinA=sinA, mcos=mcos, msin=msin, tri=tri,
                ustrict=ustrict, negm=negm, ident=ident, perm=perm,
                iota256=iota256, hib=hib, lob=lob, onesrow=onesrow,
                onescol=onescol, maskA=maskA, maskAB=maskAB, konst=konst,
                mcos2=mcos2, msin2=msin2)


def make_in_maps(inputs):
    x = np.asarray(inputs["L0_attn_hidden_states"], np.float32)
    maps = []
    for core in range(8):
        b = core // 4
        cA, cB = _chunks_of(core)
        xt = np.concatenate([x[b, cA * 1024:(cA + 1) * 1024],
                             x[b, cB * 1024:(cB + 1) * 1024]], axis=0)
        m = {"xt": np.ascontiguousarray(xt), "xT": np.ascontiguousarray(xt.T)}
        ren = {"wq": "L0_attn_Wq", "wk": "L0_attn_Wk", "wv": "L0_attn_Wv",
               "wg": "L0_attn_Wg", "wo": "L0_attn_Wo", "skw1": "L0_attn_sk_w1",
               "skw2": "L0_attn_sk_w2", "svw1": "L0_attn_sv_w1",
               "svw2": "L0_attn_sv_w2", "wqkv": "L0_ffn_w_qkv",
               "wo2": "L0_ffn_w_o", "wrout": "L0_ffn_w_router",
               "w1": "L0_ffn_w1", "w2": "L0_ffn_w2"}
        for k, v in ren.items():
            m[k] = np.asarray(inputs[v], np.float32)
        m["skb2"] = np.asarray(inputs["L0_attn_sk_b2"], np.float32).reshape(1, 256)
        m["svb2"] = np.asarray(inputs["L0_attn_sv_b2"], np.float32).reshape(1, 256)
        m["gnwc"] = np.ascontiguousarray(np.asarray(inputs["L0_attn_gnorm_w"], np.float32).reshape(2, 128).T)
        m["g1c"] = np.ascontiguousarray(np.asarray(inputs["L0_ffn_g1"], np.float32).reshape(8, 128).T)
        m["g2c"] = np.ascontiguousarray(np.asarray(inputs["L0_ffn_g2"], np.float32).reshape(8, 128).T)
        m["w1c"] = np.ascontiguousarray(np.asarray(inputs["L0_ffn_w1"], np.float32)[core])
        m["w2c"] = np.ascontiguousarray(np.asarray(inputs["L0_ffn_w2"], np.float32)[core])
        for k, v in _host_constants(core).items():
            m["c_" + k] = v
        maps.append(m)
    return maps


def assemble_output(results):
    out = np.zeros((2, 8192, D), np.float32)
    for core in range(8):
        b = core // 4
        cA, cB = _chunks_of(core)
        o = np.asarray(results[core]["out"], np.float32)
        out[b, cA * 1024:(cA + 1) * 1024] = o[:1024]
        out[b, cB * 1024:(cB + 1) * 1024] = o[1024:]
    return out


class Ev:

    def __init__(self, nc, stack):
        names = ["sd", "hd", "gd", "tm", "vo", "so", "go", "cc", "zd", "fa", "gs"]
        self.h = {n: stack.enter_context(nc.semaphore(n)) for n in names}
        self.n = {n: 0 for n in names}
        self.ev = {}
        self.pre = set()

    def declare(self, name, sem, val):
        assert name not in self.ev, f"declare of existing event {name}"
        self.ev[name] = (sem, val)
        self.pre.add(name)

    def mark(self, inst, sem, name=None, k=1):
        inst.then_inc(self.h[sem], k)
        self.n[sem] += k
        if name is not None:
            if name in self.ev:
                assert getattr(self, "in_switch", False) or name in self.pre, \
                    f"duplicate event {name}"
                assert self.ev[name] == (sem, self.n[sem]), \
                    f"event value mismatch {name}: {self.ev[name]} vs {(sem, self.n[sem])}"
            self.ev[name] = (sem, self.n[sem])
        return self.n[sem]

    def switch_body_begin(self, snap):
        self.in_switch = True
        self.n = dict(snap)

    def switch_end(self, end_counts):
        assert all(e == end_counts[0] for e in end_counts), end_counts
        self.in_switch = False
        self.n = dict(end_counts[0])

    def wait(self, eng, name):
        sem, val = self.ev[name]
        eng.wait_ge(self.h[sem], val)

    def wait_all(self, eng, *names):
        for nm in names:
            self.wait(eng, nm)


def vchain(v, E):
    def w(inst, name=None):
        val = E.mark(inst, "vo", name)
        v.wait_ge(E.h["vo"], val)
        return inst
    return w


def mm_bank_group(t, mms):
    first = {}
    last = {}
    for i, (_, _, _, bk) in enumerate(mms):
        if bk not in first:
            first[bk] = i
        last[bk] = i
    inst = None
    for i, (o, l, r, bk) in enumerate(mms):
        inst = t.matmul(o, l, r, start=(first[bk] == i), stop=(last[bk] == i))
    return inst


def _build_program_inner(cut_point):
    nc = bass.Bass("TRN2", target_bir_lowering=False,
                   detect_race_conditions=RACE_DETECT)
    nc.cache_partition_id()
    st = ExitStack()

    def din(name, shape, dtype=F32):
        return nc.dram_tensor(name, list(shape), dtype, kind="ExternalInput").ap()

    class LazyIn:
        def __init__(self):
            self.shapes = {}
            self.dtypes = {}
            self.declared = {}

        def __getitem__(self, nm):
            if nm not in self.declared:
                self.declared[nm] = din(nm, self.shapes[nm],
                                        self.dtypes.get(nm, F32))
            return self.declared[nm]

    IN = LazyIn()
    for nm, shp in [
        ("xt", (TC, D)), ("xT", (D, TC)),
        ("wq", (D, D)), ("wk", (D, D)), ("wv", (D, D)), ("wg", (D, D)), ("wo", (D, D)),
        ("skw1", (D, 16)), ("skw2", (16, 256)), ("skb2", (1, 256)),
        ("svw1", (D, 16)), ("svw2", (16, 256)), ("svb2", (1, 256)),
        ("gnwc", (128, 2)), ("g1c", (128, 8)), ("wqkv", (D, 3 * D)), ("wo2", (D, D)),
        ("g2c", (128, 8)), ("wrout", (D, NE)), ("w1", (NE, D, FF)), ("w2", (NE, FF, D)),
        ("c_cosA", (128, TC)), ("c_sinA", (128, TC)),
        ("c_mcos", (HD, TC)), ("c_msin", (HD, TC)),
        ("c_tri", (128, 128)), ("c_ustrict", (128, 128)), ("c_negm", (128, 128)),
        ("c_ident", (128, 128)), ("c_perm", (128, 128)),
        ("c_iota256", (128, 256)), ("c_hib", (128, NT)), ("c_lob", (128, NT)),
        ("c_onesrow", (1, 128)), ("c_onescol", (128, 1)),
        ("c_maskA", (128, 8)), ("c_maskAB", (128, 8)), ("c_konst", (128, 4)),
        ("c_mcos2", (128, TC)), ("c_msin2", (128, TC)),
        ("c_iotah", (128, 256)), ("c_tloch", (128, 16)), ("c_dumy", (128, 16)),
        ("c_moemask", (128, 1024)), ("c_yrows", (128, 16)), ("c_pat8", (8, 128)),
        ("w1c", (D, FF)), ("w2c", (FF, D)),
    ]:
        IN.shapes[nm] = shp
    IN.dtypes["c_iotah"] = F16
    IN.dtypes["c_tloch"] = F16
    IN.dtypes["c_yrows"] = U32
    nc._lazy_in = IN

    OUT = nc.dram_tensor("out", [TC, D], F32, kind="ExternalOutput").ap()

    def dint(name, shape, dtype=F32):
        return nc.dram_tensor(name, list(shape), dtype, kind="Internal").ap()

    def dsh(name, shape, dtype=F32):
        return nc.dram_tensor(name, list(shape), dtype, kind="Internal",
                              addr_space="Shared").ap()

    DR = {} if PH0_MIN >= 1 else {
        "st_in": dint("st_in", (256, 2048)),
        "st_out": dint("st_out", (1024, 2048)),
        "vtimed": dint("vtimed", (TC, D), BF16),
        "ofinb": dint("ofinb", (TC, D), BF16),
        "x1b": dint("x1b", (TC, D), BF16),
        "gsild": dint("gsild", (TC, D), BF16),
        "qmhad": dint("qmhad", (D, TC), BF16),
        "kv_in": dint("kv_in", (128, 33024), BF16),
        "kv_out": dint("kv_out", (512, 33024), BF16),
        "ybT": dint("ybT", (D, TC), BF16),
        "PREd": dint("PREd", (TC, D)),
        "xsd": dint("xsd", (TC, D), F16),
        "gdram": dint("gdram", (TC, NE)),
        "gag": dsh("gag", (8 * TC, NE)),
        "taud": dint("taud", (1, 1)),
        "tau8d": dsh("tau8d", (8, 1)),
        "xct": dint("xct", (TC, D), F16),
        "xctg": dsh("xctg", (8 * TC, D), F16),
        "ycd": dint("ycd", (TC, D), F16),
        "ycg": dsh("ycg", (8 * TC, D), F16),
        "ystage": dint("ystage", (4352, D)),
    }

    E = Ev(nc, st)
    SEM = E.h

    CST = st.enter_context(nc.sbuf_tensor("CST", [128, 3200], BF16))
    CV = {
        "tri": CST[:, 0:128], "negm": CST[:, 128:256],
        "perm": CST[:, 256:384], "iota256": CST[:, 384:640],
        "identb": CST[:, 640:768],
        "mcos": CST[0:64, 768:2816], "msin": CST[64:128, 768:2816],
        "hib": CST[:, 2816:2832], "lob": CST[:, 2832:2848],
        "onesrowb": CST[0:1, 2848:2976], "onescolb": CST[:, 2976:2977],
    }
    CSTF = st.enter_context(nc.sbuf_tensor("CSTF", [128, 1312], F32))
    CF = {
        "ident": CSTF[:, 0:128], "ustrict": CSTF[:, 128:256],
        "onesrow": CSTF[0:1, 256:384],
        "gnwc": CSTF[:, 384:386], "g1c": CSTF[:, 386:394], "g2c": CSTF[:, 394:402],
        "skb2": CSTF[0:1, 402:658], "svb2": CSTF[0:1, 658:914],
        "trif": CSTF[:, 914:1042], "onescol": CSTF[:, 1042:1043],
        "maskA": CSTF[:, 1044:1052], "maskAB": CSTF[:, 1052:1060],
        "konst": CSTF[:, 1060:1064],
    }
    nc.const_aps.register(F32, 1e-5, CF["konst"][:, 0:1])
    nc.const_aps.register(F32, 1.0 / 256.0, CF["konst"][:, 1:2])
    SMS = st.enter_context(nc.sbuf_tensor("SMS", [128, 1024], F32))
    STATE = st.enter_context(nc.sbuf_tensor("STATE", [128, 2048], F32))
    STATEB = st.enter_context(nc.sbuf_tensor("STATEB", [128, 1536], BF16))

    psA = st.enter_context(nc.psum_tensor("psA", [128, 2048], F32))
    psB = st.enter_context(nc.psum_tensor("psB", [128, 1024], F32))
    psC = st.enter_context(nc.psum_tensor("psC", [128, 1024], BF16))
    psD = st.enter_context(nc.psum_tensor("psD", [128, 512], F32))

    _cur_blk = [None]
    _stacks = []

    def blk():
        if _cur_blk[0] is not None:
            _cur_blk[0].__exit__(None, None, None)
        bb = nc.Block()
        bb.__enter__()
        _cur_blk[0] = bb
        return bb

    def finish_and_stop():
        bf = blk()

        if DUMP is not None and DUMP.startswith("dr:"):
            dd = DR[DUMP[3:]]

            @bf.gpsimd
            def _(e):
                for snm in E.n:
                    if snm != "gd":
                        e.wait_ge(E.h[snm], E.n[snm])
                e.wait_ge(E.h["gd"], E.n["gd"])
                nr = DUMPROWS if DUMPROWS else dd.shape[0]
                E.mark(e.dma_start(out=OUT[0:nr, :], in_=dd[0:nr, :]), "gd", None, 16)
                e.wait_ge(E.h["gd"], E.n["gd"])
        elif DUMP is not None:
            dump_buf = nc._dumpables[DUMP]

            @bf.gpsimd
            def _(e):
                for snm in E.n:
                    if snm != "gd":
                        e.wait_ge(E.h[snm], E.n[snm])
                e.wait_ge(E.h["gd"], E.n["gd"])
                np_, nf = dump_buf.shape[0], min(dump_buf.shape[1], 16384)
                if nf >= 1024:
                    na = nf // 1024
                    E.mark(e.dma_start(
                        out=OUT[:, :].rearrange("(a p) d -> p a d", p=128)[0:np_, 0:na, :],
                        in_=dump_buf[0:np_, 0:na * 1024].rearrange("p (a d) -> p a d", a=na)),
                        "gd", None, 16)
                else:
                    E.mark(e.dma_start(out=OUT[0:np_, 0:nf],
                                       in_=dump_buf[0:np_, 0:nf]),
                           "gd", None, 16)
                e.wait_ge(E.h["gd"], E.n["gd"])

        @bf.sync
        def _(e):
            for snm in E.n:
                if snm != "sd" and not (PH0_MIN >= 2 and E.n[snm] == 0):
                    e.wait_ge(E.h[snm], E.n[snm])
            if DUMP is None:
                for tt in range(NT):
                    E.mark(e.dma_start(out=OUT[tt * 128:(tt + 1) * 128, :],
                                       in_=SMS[:, 0:1024]), "sd", None, 16)
            e.wait_ge(E.h["sd"], E.n["sd"])
        if _cur_blk[0] is not None:
            _cur_blk[0].__exit__(None, None, None)
            _cur_blk[0] = None
        for stk in reversed(_stacks):
            stk.close()
        st.close()
        raise _StopEmission(nc)

    def cut(name):
        if PROBE_SWITCH == name:
            bps = blk()

            @bps.tensor
            def _(t):
                rk = t.partition_id() % 4
                for g in nc.Switch(engines=[t], index=[rk], n=4):
                    t.matmul(psA[:, 0:128], CV["identb"], CV["negm"],
                             start=True, stop=True)
            print(f"PROBE_SWITCH at {name}: build OK")
            raise _StopEmission(nc)
        if cut_point == name:
            finish_and_stop()

    att = ExitStack()
    _stacks.append(att)
    QT = att.enter_context(nc.sbuf_tensor("QT", [128, 16384], BF16))
    KT = att.enter_context(nc.sbuf_tensor("KT", [128, 16384], BF16))
    WB = att.enter_context(nc.sbuf_tensor("WB", [128, 8192], BF16))
    WKV = att.enter_context(nc.sbuf_tensor("WKV", [128, 8192], BF16))
    SVT = att.enter_context(nc.sbuf_tensor("SVT", [64, 8192], BF16))
    VS = att.enter_context(nc.sbuf_tensor("VS", [128, 2048], BF16))
    SSC = att.enter_context(nc.sbuf_tensor("SSC", [128, 4864], BF16))
    _dumpables = {}
    nc._dumpables = _dumpables
    att2 = ExitStack()
    _stacks.append(att2)
    ROPET = att2.enter_context(nc.sbuf_tensor("ROPET", [128, 4096], BF16))
    XA = att2.enter_context(nc.sbuf_tensor("XA", [128, 16384], BF16))
    S1S = att2.enter_context(nc.sbuf_tensor("S1S", [16, 4096], BF16))

    b = blk()

    @b.sync
    def _(e):
        last = None
        for src, dst in [
            ("c_ident", CF["ident"]), ("c_ustrict", CF["ustrict"]),
            ("c_onesrow", CF["onesrow"]), ("skb2", CF["skb2"]),
            ("svb2", CF["svb2"]), ("c_tri", CF["trif"]),
            ("c_onescol", CF["onescol"]), ("c_maskA", CF["maskA"]),
            ("c_maskAB", CF["maskAB"]), ("c_konst", CF["konst"]),
        ]:
            last = e.dma_start(out=dst, in_=IN[src][:, :])
            E.mark(last, "sd", None, 16)
        E.mark(e.dma_start(out=CF["gnwc"], in_=IN["gnwc"][:, :]), "sd", None, 16)
        E.mark(e.dma_start(out=CF["g1c"], in_=IN["g1c"][:, :]), "sd", None, 16)
        E.mark(e.dma_start(out=CF["g2c"], in_=IN["g2c"][:, :]), "sd", "constsF", 16)

    @b.gpsimd
    def _(e):
        for src, dst in [
            ("c_cosA", ROPET[:, 0:2048]), ("c_sinA", ROPET[:, 2048:4096]),
            ("c_tri", CV["tri"]), ("c_negm", CV["negm"]), ("c_perm", CV["perm"]),
            ("c_iota256", CV["iota256"]), ("c_ident", CV["identb"]),
            ("c_mcos", CV["mcos"]), ("c_msin", CV["msin"]),
            ("c_hib", CV["hib"]), ("c_lob", CV["lob"]),
            ("c_onesrow", CV["onesrowb"]), ("c_onescol", CV["onescolb"]),
        ]:
            E.mark(e.dma_start(out=dst, in_=IN[src][:, :]), "gd", None, 16)
        E.mark(e.dma_start(out=XA[:, :].rearrange("p (a t) -> p a t", a=8),
                           in_=IN["xT"].rearrange("(a p) t -> p a t", p=128)),
               "gd", "xa", 16)

    cut("ph0")
    def load_w(tag, dram_ap, wait_evs=()):
        bw = blk()

        @bw.gpsimd
        def _(e):
            for evn in wait_evs:
                E.wait(e, evn)
            E.mark(e.dma_start(out=WB[:, 0:8192].rearrange("p (a o) -> p a o", a=8),
                               in_=dram_ap.rearrange("(a p) o -> p a o", p=128)),
                   "gd", f"w.{tag}", 16)
        return f"w.{tag}"

    def proj_dim(tag, dst_sb, w_ev, extra_evs=()):
        bq = blk()
        items = [(oc, ts) for oc in range(8) for ts in range(4)]
        for i, (oc, ts) in enumerate(items):
            @bq.tensor
            def _(t, i=i, oc=oc, ts=ts):
                if i == 0:
                    E.wait(t, w_ev)
                    for evn in extra_evs:
                        E.wait(t, evn)
                ps = psA[:, (i % 4) * 512:(i % 4 + 1) * 512]
                if i >= 4:
                    E.wait(t, f"{tag}.ev.{i - 4}")
                last = None
                for dc in range(8):
                    last = t.matmul(
                        ps,
                        WB[:, dc * 1024 + oc * 128: dc * 1024 + oc * 128 + 128],
                        XA[:, dc * 2048 + ts * 512: dc * 2048 + ts * 512 + 512],
                        start=(dc == 0), stop=(dc == 7))
                E.mark(last, "tm", f"{tag}.mm.{i}")

            @bq.vector
            def _(v, i=i, oc=oc, ts=ts):
                E.wait(v, f"{tag}.mm.{i}")
                ps = psA[:, (i % 4) * 512:(i % 4 + 1) * 512]
                E.mark(v.tensor_copy(
                    dst_sb[:, oc * 2048 + ts * 512: oc * 2048 + ts * 512 + 512], ps),
                    "vo", f"{tag}.ev.{i}")
        return f"{tag}.ev.{len(items) - 1}"

    wq_ev = load_w("wq", IN["wq"], wait_evs=("xa",))
    q_done = proj_dim("qproj", QT, wq_ev, extra_evs=("constsF", "xa"))
    wk_ev = load_w("wk", IN["wk"], wait_evs=(q_done,))
    k_done = proj_dim("kproj", KT, wk_ev)

    def rope_attn(tag, buf, ready_ev):
        br = blk()

        @br.vector
        def _(v):
            E.wait(v, ready_ev)
            last = None
            for h in range(HA):
                a = buf[:, (2 * h) * 2048:(2 * h + 1) * 2048]
                bb = buf[:, (2 * h + 1) * 2048:(2 * h + 2) * 2048]
                t1 = SSC[:, 0:2048]
                t2 = SSC[:, 2048:4096]
                v.tensor_mul(t1, a, ROPET[:, 0:2048])
                v.tensor_mul(t2, bb, ROPET[:, 2048:4096])
                v.tensor_mul(bb, bb, ROPET[:, 0:2048])
                v.tensor_mul(a, a, ROPET[:, 2048:4096])
                v.tensor_add(bb, bb, a)
                last = v.tensor_sub(a, t1, t2)
            E.mark(last, "vo", f"rope.{tag}")
        return f"rope.{tag}"

    _dumpables.update(QT=QT, KT=KT, WKV=WKV, SVT=SVT, STATE=STATE, GSILV=VS)
    cut("qk")
    ropeq_ev = rope_attn("q", QT, k_done)
    ropek_ev = rope_attn("k", KT, ropeq_ev)

    def proj_time(tag, w_ev, consumer):
        bt = blk()
        for tt in range(NT):
            @bt.tensor
            def _(t, tt=tt):
                if tt == 0:
                    E.wait(t, w_ev)
                if tt >= 2:
                    E.wait(t, f"{tag}.ev.{tt - 2}")
                ps = psB[:, :] if tt % 2 == 0 else psA[:, 1024:2048]
                last = None
                for dc in range(8):
                    for half in range(2):
                        last = t.matmul(
                            ps[:, half * 512:(half + 1) * 512],
                            XA[:, dc * 2048 + tt * 128: dc * 2048 + tt * 128 + 128],
                            WB[:, dc * 1024 + half * 512: dc * 1024 + (half + 1) * 512],
                            start=(dc == 0), stop=(dc == 7))
                E.mark(last, "tm", f"{tag}.mm.{tt}")

            if consumer == "v":
                @bt.vector
                def _(v, tt=tt):
                    E.wait(v, f"{tag}.mm.{tt}")
                    if tt >= 2:
                        E.wait(v, f"{tag}.st.{tt - 2}")
                    ps = psB[:, :] if tt % 2 == 0 else psA[:, 1024:2048]
                    E.mark(v.tensor_copy(VS[:, (tt % 2) * 1024:(tt % 2 + 1) * 1024], ps),
                           "vo", f"{tag}.ev.{tt}")

                @bt.scalar
                def _(s, tt=tt):
                    E.wait(s, f"{tag}.ev.{tt}")
                    E.mark(s.dma_start(out=DR["vtimed"][tt * 128:(tt + 1) * 128, :],
                                       in_=VS[:, (tt % 2) * 1024:(tt % 2 + 1) * 1024]),
                           "hd", f"{tag}.st.{tt}", 16)
            else:
                @bt.scalar
                def _(s, tt=tt):
                    E.wait(s, f"{tag}.mm.{tt}")
                    if tt >= 2:
                        E.wait(s, f"{tag}.st.{tt - 2}")
                    ps = psB[:, :] if tt % 2 == 0 else psA[:, 1024:2048]
                    E.mark(s.activation(VS[:, (tt % 2) * 1024:(tt % 2 + 1) * 1024],
                                        ps, AF.Silu), "so", f"{tag}.ev.{tt}")

                @bt.sync
                def _(e, tt=tt):
                    E.wait(e, f"{tag}.ev.{tt}")
                    E.mark(e.dma_start(out=DR["gsild"][tt * 128:(tt + 1) * 128, :],
                                       in_=VS[:, (tt % 2) * 1024:(tt % 2 + 1) * 1024]),
                           "sd", f"{tag}.st.{tt}", 16)
        return f"{tag}.ev.{NT - 1}", f"{tag}.st.{NT - 1}"

    cut("rope")
    wv_ev = load_w("wv", IN["wv"], wait_evs=(ropek_ev,))
    _, vt_done = proj_time("vproj", wv_ev, "v")
    wg_ev = load_w("wg", IN["wg"], wait_evs=("vproj.ev.15",))
    gs_done, gst_done = proj_time("gproj", wg_ev, "g")

    cut("ph1b")
    bw = blk()

    @bw.gpsimd
    def _(e):
        E.wait(e, gs_done)
        e1 = e.dma_start(out=WB[:, 0:128].rearrange("p (a o) -> p a o", a=8),
                         in_=IN["skw1"].rearrange("(a p) o -> p a o", p=128))
        E.mark(e1, "gd", None, 16)
        e2 = e.dma_start(out=WB[:, 128:256].rearrange("p (a o) -> p a o", a=8),
                         in_=IN["svw1"].rearrange("(a p) o -> p a o", p=128))
        E.mark(e2, "gd", None, 16)
        E.mark(e.dma_start(out=WB[0:16, 256:512], in_=IN["skw2"][:, :]), "gd", None, 16)
        E.mark(e.dma_start(out=WB[0:16, 512:768], in_=IN["svw2"][:, :]), "gd", "skw", 16)

    bs1 = blk()
    for i in range(8):
        which, ts = i // 4, i % 4

        @bs1.tensor
        def _(t, i=i, which=which, ts=ts):
            if i == 0:
                E.wait(t, "skw")
            if i >= 2:
                E.wait(t, f"s1.ev.{i - 2}")
            ps = psB[0:16, 0:512] if i % 2 == 0 else psA[0:16, 0:512]
            last = None
            for dc in range(8):
                last = t.matmul(
                    ps,
                    WB[:, which * 128 + dc * 16: which * 128 + (dc + 1) * 16],
                    XA[:, dc * 2048 + ts * 512: dc * 2048 + ts * 512 + 512],
                    start=(dc == 0), stop=(dc == 7))
            E.mark(last, "tm", f"s1.mm.{i}")

        @bs1.vector
        def _(v, i=i, which=which, ts=ts):
            E.wait(v, f"s1.mm.{i}")
            ps = psB[0:16, 0:512] if i % 2 == 0 else psA[0:16, 0:512]
            E.mark(v.tensor_copy(
                S1S[0:16, which * 2048 + ts * 512: which * 2048 + ts * 512 + 512], ps),
                "vo", f"s1.ev.{i}")

    bs2 = blk()
    for i in range(2 * NT):
        which, tt = i // NT, i % NT

        @bs2.tensor
        def _(t, i=i, which=which, tt=tt):
            if i == 0:
                E.wait(t, "s1.ev.7")
            if i >= 2:
                E.wait(t, f"s2.clip.{i - 2}")
            ps = psB[:, 0:256] if i % 2 == 0 else psA[:, 0:256]
            t.matmul(ps,
                     S1S[0:16, which * 2048 + tt * 128: which * 2048 + tt * 128 + 128],
                     WB[0:16, 256 + which * 256: 512 + which * 256],
                     start=True, stop=False)
            E.mark(t.matmul(ps, CF["onesrow"][0:1, 0:128],
                            (CF["skb2"] if which == 0 else CF["svb2"]),
                            start=False, stop=True), "tm", f"s2.mm.{i}")

        @bs2.vector
        def _(v, i=i):
            E.wait(v, f"s2.mm.{i}")
            if i >= 2:
                E.wait(v, f"s2.exp.{i - 2}")
            ps = psB[:, 0:256] if i % 2 == 0 else psA[:, 0:256]
            sl = SMS[:, (i % 2) * 256:(i % 2 + 1) * 256]
            v.tensor_scalar_min(sl, ps, 32.0)
            E.mark(v.tensor_scalar_max(sl, sl, -32.0), "vo", f"s2.clip.{i}")

        @bs2.scalar
        def _(s, i=i, which=which, tt=tt):
            E.wait(s, f"s2.clip.{i}")
            sl = SMS[:, (i % 2) * 256:(i % 2 + 1) * 256]
            E.mark(s.activation(
                WKV[:, which * 4096 + tt * 256: which * 4096 + (tt + 1) * 256],
                sl, AF.Exp), "so", f"s2.exp.{i}")

    bwt = blk()
    for bch in range(16):
        @bwt.tensor
        def _(t, bch=bch):
            if bch == 0:
                E.wait(t, f"s2.exp.{2 * NT - 1}")
            else:
                E.wait(t, f"svt.ev.{4 * bch - 1}")
            for j in range(4):
                i = 4 * bch + j
                tt, h = i // 4, i % 4
                E.mark(t.transpose(
                    psC[0:64, j * 128:(j + 1) * 128],
                    WKV[:, 4096 + tt * 256 + h * 64: 4096 + tt * 256 + (h + 1) * 64],
                    CV["identb"]), "tm", f"svt.mm.{i}")

        @bwt.vector
        def _(v, bch=bch):
            E.wait(v, f"svt.mm.{4 * bch + 3}")
            for j in range(4):
                i = 4 * bch + j
                tt, h = i // 4, i % 4
                E.mark(v.tensor_copy(
                    SVT[0:64, h * 2048 + tt * 128: h * 2048 + (tt + 1) * 128],
                    psC[0:64, j * 128:(j + 1) * 128]), "vo", f"svt.ev.{i}")

    cut("ph1c")
    att2.close()
    KTT = att.enter_context(nc.sbuf_tensor("KTT", [128, 2048], BF16))
    SF = att.enter_context(nc.sbuf_tensor("SF", [128, 2304], F32))
    CONT = att.enter_context(nc.sbuf_tensor("CONT", [128, 2048], F32))

    def emit_ktime(tag, b_, T, slot, pre_evs=()):
        @b_.tensor
        def _(t):
            for evn in pre_evs:
                E.wait(t, evn)
            for dc in range(8):
                E.mark(t.transpose(
                    psC[:, dc * 128:(dc + 1) * 128],
                    KT[:, dc * 2048 + T * 128: dc * 2048 + (T + 1) * 128],
                    CV["identb"]), "tm", f"{tag}.ktt.{T}.{dc}")

        @b_.vector
        def _(v):
            E.wait(v, f"{tag}.ktt.{T}.7")
            for dc in range(8):
                E.mark(v.tensor_copy(
                    KTT[:, slot * 1024 + dc * 128: slot * 1024 + (dc + 1) * 128],
                    psC[:, dc * 128:(dc + 1) * 128]), "vo", f"{tag}.ktc.{T}.{dc}")
        return f"{tag}.ktc.{T}.7"

    bcz = blk()

    @bcz.vector
    def _(v):
        E.mark(v.memset(CONT[:, :], 0.0), "vo", "cont0")

    for cc in range(2):
        for tt in range(8):
            T = cc * 8 + tt
            b_ = blk()

            @b_.gpsimd
            def _(e, T=T):
                if T == 0:
                    E.wait(e, "vproj.st.15")
                    E.wait(e, "gproj.st.15")
                if T >= 2:
                    E.wait(e, f"ca.bundle.{T - 2}")
                E.mark(e.dma_start(out=VS[:, (T % 2) * 1024:(T % 2 + 1) * 1024],
                                   in_=DR["vtimed"][T * 128:(T + 1) * 128, :]),
                       "gd", f"ca.v.{T}", 16)

            pre = ["svt.ev.63"] if T == 0 else [f"ca.ktc.{T - 1}.7"]
            kt_ev = emit_ktime("ca", b_, T, T % 2, pre_evs=pre)

            @b_.tensor
            def _(t, T=T, cc=cc, tt=tt):
                E.wait(t, f"ca.ktc.{T}.7")
                E.wait(t, f"ca.v.{T}")
                last = None
                for h in range(HA):
                    for dqc in range(2):
                        i = 2 * h + dqc
                        last = t.matmul(
                            psD[:, i * 64:(i + 1) * 64],
                            KTT[:, (T % 2) * 1024 + h * 256 + dqc * 128:
                                (T % 2) * 1024 + h * 256 + dqc * 128 + 128],
                            WKV[:, T * 256 + h * 64: T * 256 + (h + 1) * 64],
                            start=(tt == 0 and i == 0), stop=(tt == 7 and i == 7))
                for h in range(HA):
                    bank_first = (h % 2 == 0)
                    bank_last = (h % 2 == 1)
                    last = t.matmul(
                        psB[0:64, h * 256:(h + 1) * 256],
                        WKV[:, 4096 + T * 256 + h * 64: 4096 + T * 256 + (h + 1) * 64],
                        VS[:, (T % 2) * 1024 + h * 256:(T % 2) * 1024 + (h + 1) * 256],
                        start=(tt == 0 and bank_first), stop=(tt == 7 and bank_last))
                t.matmul(psA[0:1, 1024:1280], CV["onescolb"],
                         WKV[:, T * 256:(T + 1) * 256],
                         start=(tt == 0), stop=False)
                last = t.matmul(psA[0:1, 1280:1536], CV["onescolb"],
                                WKV[:, 4096 + T * 256: 4096 + (T + 1) * 256],
                                start=False, stop=(tt == 7))
                E.mark(last, "tm", f"ca.bundle.{T}")

        bce = blk()

        @bce.vector
        def _(v, cc=cc):
            E.wait(v, f"ca.bundle.{cc * 8 + 7}")
            E.wait(v, "cont0")
            if cc == 1:
                E.wait(v, "ca.st.0")
            v.tensor_copy(CONT[:, 0:512], psD[:, 0:512])
            v.tensor_copy(CONT[0:64, 512:1536], psB[0:64, 0:1024])
            v.tensor_copy(CONT[0:1, 1536:1792], psA[0:1, 1024:1280])
            E.mark(v.tensor_copy(CONT[0:1, 1792:2048], psA[0:1, 1280:1536]),
                   "vo", f"ca.cont.{cc}")

        @bce.scalar
        def _(s, cc=cc):
            E.wait(s, f"ca.cont.{cc}")
            E.mark(s.dma_start(out=DR["st_in"][cc * 128:(cc + 1) * 128, :],
                               in_=CONT[:, :]), "hd", f"ca.st.{cc}", 16)

    cut("ph2a")
    bag = blk()

    @bag.gpsimd
    def _(e):
        E.wait(e, "ca.st.1")
        E.mark(e.collective_compute(
            "AllGather", ALU.bypass,
            replica_groups=[[0, 1, 2, 3], [4, 5, 6, 7]],
            ins=[DR["st_in"].tensor.ap()],
            outs=[DR["st_out"].tensor.ap()],
        ), "cc", "ag.state")

    def prefix_add(tag, mask_ap, first_ev):
        for j in range(8):
            r, sl = (j, 0) if j < 4 else (7 - j, 1)
            row = 256 * r + 128 * sl
            bp = blk()

            @bp.gpsimd
            def _(e, j=j, row=row):
                if j == 0:
                    E.wait(e, first_ev)
                    if tag == "pfb":
                        E.wait(e, "w0.7.zrow.1")
                        E.wait(e, "w0.7.ptil")
                if j >= 2:
                    E.wait(e, f"{tag}.add.{j - 2}")
                E.mark(e.dma_start(out=(CONT[:, :] if j % 2 == 0 else SF[:, 0:2048]),
                                   in_=DR["st_out"][row:row + 128, :]),
                       "gd", f"{tag}.ld.{j}", 16)

            @bp.vector
            def _(v, j=j):
                E.wait(v, f"{tag}.ld.{j}")
                if tag == "pfa" and j == 0:
                    v.memset(STATE[:, :], 0.0)
                src = CONT[:, :] if j % 2 == 0 else SF[:, 0:2048]
                E.mark(v.scalar_tensor_tensor(
                    STATE[:, :], src, mask_ap[:, j:j + 1], STATE[:, :],
                    ALU.mult, ALU.add), "vo", f"{tag}.add.{j}")
        bp2 = blk()

        @bp2.vector
        def _(v):
            E.wait(v, f"{tag}.add.7")
            v.tensor_copy(STATEB[:, 0:512], STATE[:, 0:512])
            v.tensor_copy(STATEB[0:64, 512:1536], STATE[0:64, 512:1536])
            v.tensor_copy(SMS[0:1, 0:256], STATE[0:1, 1536:1792])
            E.mark(v.tensor_copy(SMS[0:1, 256:512], STATE[0:1, 1792:2048]),
                   "vo", f"{tag}.done")
        return f"{tag}.done"

    pfa_done = prefix_add("pfa", CF["maskA"], "ag.state")
    cut("pfa")

    def chunk_walk(cc, ready_ev):
        for tt in range(8):
            T = cc * 8 + tt
            P = f"w{cc}.{tt}"
            PM = f"w{cc}.{tt - 1}"
            b_ = blk()

            @b_.gpsimd
            def _(e, T=T, tt=tt, cc=cc, P=P):
                if tt == 0:
                    E.wait(e, ready_ev)
                if tt >= 2:
                    E.wait(e, f"w{cc}.{tt - 2}.oupd")
                E.mark(e.dma_start(out=VS[:, (tt % 2) * 1024:(tt % 2 + 1) * 1024],
                                   in_=DR["vtimed"][T * 128:(T + 1) * 128, :]),
                       "gd", f"{P}.v", 16)
                if tt >= 2:
                    E.wait(e, f"w{cc}.{tt - 2}.ofin")
                E.mark(e.dma_start(
                    out=SSC[:, 2816 + (tt % 2) * 1024: 2816 + (tt % 2 + 1) * 1024],
                    in_=DR["gsild"][T * 128:(T + 1) * 128, :]), "gd", f"{P}.gs", 16)

            pre = [ready_ev] if tt == 0 else [f"{PM}.ktc.7"]
            @b_.tensor
            def _(t, T=T, P=P, pre=pre, tt=tt, cc=cc):
                for evn in pre:
                    E.wait(t, evn)
                if tt > 0:
                    E.wait(t, f"w{cc}.{tt - 1}.ptc.3")
                for dc in range(8):
                    E.mark(t.transpose(
                        psC[:, dc * 128:(dc + 1) * 128],
                        KT[:, dc * 2048 + T * 128: dc * 2048 + (T + 1) * 128],
                        CV["identb"]), "tm", f"{P}.ktt.{dc}")

            @b_.vector
            def _(v, P=P, tt=tt):
                E.wait(v, f"{P}.ktt.7")
                for dc in range(8):
                    E.mark(v.tensor_copy(
                        KTT[:, (tt % 2) * 1024 + dc * 128:(tt % 2) * 1024 + (dc + 1) * 128],
                        psC[:, dc * 128:(dc + 1) * 128]), "vo", f"{P}.ktc.{dc}")

            @b_.tensor
            def _(t, T=T, P=P, tt=tt, cc=cc):
                if tt > 0:
                    E.wait(t, f"w{cc}.{tt - 1}.sevac")
                mms = []
                for h in range(HA):
                    for dqc in range(2):
                        mms.append((psA[:, h * 128:(h + 1) * 128],
                                    KT[:, (2 * h + dqc) * 2048 + T * 128:
                                       (2 * h + dqc) * 2048 + (T + 1) * 128],
                                    QT[:, (2 * h + dqc) * 2048 + T * 128:
                                       (2 * h + dqc) * 2048 + (T + 1) * 128], "b0"))
                E.mark(mm_bank_group(t, mms), "tm", f"{P}.sT")

            @b_.vector
            def _(v, P=P):
                E.wait(v, f"{P}.sT")
                last = None
                for h in range(HA):
                    last = v.scalar_tensor_tensor(
                        SSC[:, h * 128:(h + 1) * 128],
                        psA[:, h * 128:(h + 1) * 128], 1.0, CF["trif"],
                        ALU.mult, ALU.mult)
                E.mark(last, "vo", f"{P}.sevac")

            @b_.tensor
            def _(t, T=T, P=P, tt=tt, cc=cc):
                E.wait(t, f"{P}.sevac")
                if tt > 0:
                    E.wait(t, f"w{cc}.{tt - 1}.stb")
                    E.wait(t, f"w{cc}.{tt - 1}.zrow.1")
                    E.wait(t, f"w{cc}.{tt - 1}.praw")
                    E.wait(t, f"w{cc}.{tt - 1}.zcopy")
                mms = []
                for h in range(HA):
                    sl = psA[:, 512 + h * 64: 512 + (h + 1) * 64]
                    mms.append((sl, SSC[:, h * 128:(h + 1) * 128],
                                WKV[:, T * 256 + h * 64: T * 256 + (h + 1) * 64], "b1"))
                    mms.append((sl,
                                QT[:, 2 * h * 2048 + T * 128: 2 * h * 2048 + (T + 1) * 128],
                                STATEB[:, 2 * h * 64:(2 * h + 1) * 64], "b1"))
                    mms.append((sl,
                                QT[:, (2 * h + 1) * 2048 + T * 128:
                                   (2 * h + 1) * 2048 + (T + 1) * 128],
                                STATEB[:, (2 * h + 1) * 64:(2 * h + 2) * 64], "b1"))
                mms.append((psA[:, 1024:1280], CV["tri"],
                            WKV[:, T * 256:(T + 1) * 256], "b2"))
                mms.append((psA[:, 1024:1280], CF["onesrow"], SMS[0:1, 0:256], "b2"))
                mms.append((psA[:, 1280:1536], CV["tri"],
                            WKV[:, 4096 + T * 256: 4096 + (T + 1) * 256], "b2"))
                mms.append((psA[:, 1280:1536], CF["onesrow"], SMS[0:1, 256:512], "b2"))
                E.mark(mm_bank_group(t, mms), "tm", f"{P}.log")

            @b_.vector
            def _(v, P=P, tt=tt, cc=cc):
                E.wait(v, f"{P}.log")
                if tt > 0:
                    E.wait(v, f"w{cc}.{tt - 1}.zrow.1")
                    E.wait(v, f"w{cc}.{tt - 1}.exp")
                v.reciprocal(SF[:, 256:512], psA[:, 1024:1280])
                E.mark(v.scalar_tensor_tensor(
                    SF[:, 0:256], psA[:, 512:768], SCALE_A, SF[:, 256:512],
                    ALU.mult, ALU.mult), "vo", f"{P}.praw")
                v.reciprocal(SF[:, 512:768], psA[:, 1280:1536])
                v.tensor_copy(SF[:, 1040:1296], psA[:, 1024:1280])
                E.mark(v.tensor_copy(SF[:, 1296:1552], psA[:, 1280:1536]),
                       "vo", f"{P}.zcopy")
                E.mark(v.tensor_reduce(
                    SF[:, 1032:1036],
                    SF[:, 0:256].rearrange("p (a b) -> p a b", a=4),
                    AX.X, ALU.max, negate=True), "vo", f"{P}.nmax")

            @b_.scalar
            def _(s, P=P, tt=tt, cc=cc):
                E.wait(s, f"{P}.nmax")
                if tt > 0:
                    E.wait(s, f"w{cc}.{tt - 1}.ptil")
                last = None
                for h in range(HA):
                    last = s.activation(
                        SF[:, 768 + h * 64: 768 + (h + 1) * 64],
                        SF[:, h * 64:(h + 1) * 64],
                        AF.Exp, bias=SF[:, 1032 + h: 1033 + h],
                        accum_out=SF[:, 1024 + h: 1025 + h])
                E.mark(last, "so", f"{P}.exp")

            @b_.scalar
            def _(s, P=P):
                E.wait(s, f"{P}.zcopy")
                E.mark(s.dma_start(out=SMS[0:1, 0:256], in_=SF[127:128, 1040:1296]),
                       "zd", f"{P}.zrow.0", 16)
                E.mark(s.dma_start(out=SMS[0:1, 256:512], in_=SF[127:128, 1296:1552]),
                       "zd", f"{P}.zrow.1", 16)

            @b_.vector
            def _(v, P=P, tt=tt, cc=cc):
                E.wait(v, f"{P}.exp")
                if tt > 0:
                    E.wait(v, f"w{cc}.{tt - 1}.ptT.1")
                rc = E.mark(v.reciprocal(SF[:, 1028:1032], SF[:, 1024:1028]), "vo")
                v.wait_ge(E.h["vo"], rc)
                last = None
                for h in range(HA):
                    last = v.scalar_tensor_tensor(
                        SSC[:, 1536 + h * 64: 1536 + (h + 1) * 64],
                        SF[:, 768 + h * 64: 768 + (h + 1) * 64],
                        SF[:, 1028 + h: 1029 + h],
                        SF[:, 512 + h * 64: 512 + (h + 1) * 64],
                        ALU.mult, ALU.mult)
                E.mark(last, "vo", f"{P}.ptil")

            @b_.tensor
            def _(t, P=P):
                E.wait(t, f"{P}.ptil")
                E.wait(t, f"{P}.ktc.7")
                for h in range(HA):
                    E.mark(t.transpose(psC[0:64, h * 128:(h + 1) * 128],
                                       SSC[:, 1536 + h * 64: 1536 + (h + 1) * 64],
                                       CV["identb"]), "tm", f"{P}.ptT.{h}")

            @b_.vector
            def _(v, P=P):
                E.wait(v, f"{P}.ptT.3")
                for h in range(HA):
                    E.mark(v.tensor_copy(
                        SSC[0:64, 512 + h * 128: 512 + (h + 1) * 128],
                        psC[0:64, h * 128:(h + 1) * 128]), "vo", f"{P}.ptc.{h}")

            @b_.tensor
            def _(t, T=T, P=P, tt=tt, cc=cc):
                E.wait(t, f"{P}.ptc.3")
                if tt > 0:
                    E.wait(t, f"w{cc}.{tt - 1}.aevac")
                mms = []
                for h in range(HA):
                    mms.append((psA[:, 1536 + h * 128: 1536 + (h + 1) * 128],
                                SVT[0:64, h * 2048 + T * 128: h * 2048 + (T + 1) * 128],
                                SSC[0:64, 512 + h * 128: 512 + (h + 1) * 128], "b3"))
                E.mark(mm_bank_group(t, mms), "tm", f"{P}.aT")

            @b_.vector
            def _(v, P=P):
                E.wait(v, f"{P}.aT")
                last = None
                for h in range(HA):
                    last = v.scalar_tensor_tensor(
                        SSC[:, 1024 + h * 128: 1024 + (h + 1) * 128],
                        psA[:, 1536 + h * 128: 1536 + (h + 1) * 128], 1.0, CF["trif"],
                        ALU.mult, ALU.mult)
                E.mark(last, "vo", f"{P}.aevac")

            @b_.tensor
            def _(t, P=P, tt=tt, cc=cc):
                E.wait(t, f"{P}.aevac")
                E.wait(t, f"{P}.v")
                if tt > 0:
                    E.wait(t, f"w{cc}.{tt - 1}.ofin")
                mms = []
                for h in range(HA):
                    bk = f"b{4 + h // 2}"
                    mms.append((psB[:, h * 256:(h + 1) * 256],
                                SSC[:, 1024 + h * 128: 1024 + (h + 1) * 128],
                                VS[:, (tt % 2) * 1024 + h * 256:(tt % 2) * 1024 + (h + 1) * 256],
                                bk))
                    mms.append((psB[:, h * 256:(h + 1) * 256],
                                SSC[0:64, 512 + h * 128: 512 + (h + 1) * 128],
                                STATEB[0:64, 512 + h * 256: 512 + (h + 1) * 256],
                                bk))
                E.mark(mm_bank_group(t, mms), "tm", f"{P}.o")

            @b_.scalar
            def _(s, P=P):
                E.wait(s, f"{P}.o")
                for h in range(HA):
                    s.activation(SMS[:, 528:784], psB[:, h * 256:(h + 1) * 256],
                                 AF.Square, accum_out=SMS[:, 512 + h: 513 + h])
                E.mark(s.activation(SMS[:, 516:520], SMS[:, 512:516], AF.Sqrt,
                                    bias=1e-5, scale=1.0 / 256.0),
                       "so", f"{P}.gn")

            @b_.vector
            def _(v, P=P, T=T, tt=tt, cc=cc):
                E.wait(v, f"{P}.gn")
                if tt > 0:
                    E.wait(v, f"w{cc}.{tt - 1}.ost")
                rc = E.mark(v.reciprocal(SMS[:, 520:524], SMS[:, 516:520]), "vo")
                v.wait_ge(E.h["vo"], rc)
                E.wait(v, f"{P}.gs")
                last = None
                for h in range(HA):
                    last = v.scalar_tensor_tensor(
                        SSC[:, 1792 + h * 256: 1792 + (h + 1) * 256],
                        psB[:, h * 256:(h + 1) * 256],
                        SMS[:, 520 + h: 521 + h],
                        SSC[:, 2816 + (tt % 2) * 1024 + h * 256:
                            2816 + (tt % 2) * 1024 + (h + 1) * 256],
                        ALU.mult, ALU.mult)
                E.mark(last, "vo", f"{P}.ofin")

            @b_.scalar
            def _(s, P=P, T=T):
                E.wait(s, f"{P}.ofin")
                E.mark(s.dma_start(out=DR["ofinb"][T * 128:(T + 1) * 128, :],
                                   in_=SSC[:, 1792:2816]), "hd", f"{P}.ost", 16)

            @b_.tensor
            def _(t, P=P, T=T, tt=tt, cc=cc):
                E.wait(t, f"{P}.praw")
                E.wait(t, f"{P}.zcopy")
                if tt > 0:
                    E.wait(t, f"w{cc}.{tt - 1}.supd")
                mms = []
                for h in range(HA):
                    for dqc in range(2):
                        i = 2 * h + dqc
                        mms.append((psD[:, i * 64:(i + 1) * 64],
                                    KTT[:, (tt % 2) * 1024 + h * 256 + dqc * 128:
                                        (tt % 2) * 1024 + h * 256 + dqc * 128 + 128],
                                    WKV[:, T * 256 + h * 64: T * 256 + (h + 1) * 64],
                                    "b7"))
                for h in range(HA):
                    mms.append((psA[0:64, 512 + h * 256: 512 + (h + 1) * 256],
                                WKV[:, 4096 + T * 256 + h * 64: 4096 + T * 256 + (h + 1) * 64],
                                VS[:, (tt % 2) * 1024 + h * 256:(tt % 2) * 1024 + (h + 1) * 256],
                                f"b{1 + h // 2}"))
                E.mark(mm_bank_group(t, mms), "tm", f"{P}.oupd")

            @b_.vector
            def _(v, P=P):
                E.wait(v, f"{P}.oupd")
                v.tensor_add(STATE[:, 0:512], STATE[:, 0:512], psD[:, 0:512])
                E.mark(v.tensor_add(STATE[0:64, 512:1536], STATE[0:64, 512:1536],
                                    psA[0:64, 512:1536]), "vo", f"{P}.supd")
                v.tensor_copy(STATEB[:, 0:512], STATE[:, 0:512])
                E.mark(v.tensor_copy(STATEB[0:64, 512:1536], STATE[0:64, 512:1536]),
                       "vo", f"{P}.stb")
        return f"w{cc}.7.ost"

    w0_done = chunk_walk(0, pfa_done)
    cut("walk0")
    pfb_done = prefix_add("pfb", CF["maskAB"], "w0.7.supd")
    w1_done = chunk_walk(1, pfb_done)

    cut("walk1")
    wo_ev = load_w("wo", IN["wo"], wait_evs=(w1_done,))
    bfold = blk()

    @bfold.vector
    def _(v):
        E.wait(v, wo_ev)
        last = None
        for dvc in range(8):
            last = v.tensor_scalar(
                WB[:, dvc * 1024:(dvc + 1) * 1024],
                WB[:, dvc * 1024:(dvc + 1) * 1024],
                CF["gnwc"][:, dvc % 2: dvc % 2 + 1], None, ALU.mult)
        E.mark(last, "vo", "wofold")

    OFT = QT
    btr = blk()

    @btr.sync
    def _(e):
        E.wait(e, "w1.7.ost")
        for dvc in range(8):
            E.mark(e.dma_start(out=OFT[:, dvc * 2048:(dvc + 1) * 2048],
                               in_=DR["ofinb"][:, dvc * 128:(dvc + 1) * 128],
                               transpose=True), "sd", f"oft.{dvc}", 16)

    cut("oft")
    X1F = dint("x1f", (TC, D))

    for tt in range(NT):
        pi = tt % 2
        P = f"ln.{tt}"
        b_ = blk()

        @b_.gpsimd
        def _(e, tt=tt, pi=pi, P=P):
            if tt < 2:
                E.wait(e, "w1.7.supd")
                E.wait(e, "w1.7.zrow.1")
            if tt >= 2:
                E.wait(e, f"ln.{tt - 2}.r1")
            E.mark(e.dma_start(out=SF[:, pi * 1024:(pi + 1) * 1024],
                               in_=IN["xt"][tt * 128:(tt + 1) * 128, :]),
                   "gd", f"{P}.x0", 16)

        @b_.tensor
        def _(t, tt=tt, P=P):
            if tt == 0:
                E.wait(t, "wofold")
                E.wait(t, "oft.7")
            if tt >= 2:
                E.wait(t, f"ln.{tt - 2}.r1")
            ps = psB[:, :] if tt % 2 == 0 else psA[:, 0:1024]
            mms = []
            for dvc in range(8):
                for half in range(2):
                    mms.append((ps[:, half * 512:(half + 1) * 512],
                                OFT[:, dvc * 2048 + tt * 128: dvc * 2048 + tt * 128 + 128],
                                WB[:, dvc * 1024 + half * 512: dvc * 1024 + (half + 1) * 512],
                                f"h{half}"))
            E.mark(mm_bank_group(t, mms), "tm", f"{P}.mm")

        @b_.vector
        def _(v, tt=tt, pi=pi, P=P):
            E.wait(v, f"{P}.mm")
            E.wait(v, f"{P}.x0")
            if tt >= 2:
                E.wait(v, f"ln.{tt - 2}.x1done")
            ps = psB[:, :] if tt % 2 == 0 else psA[:, 0:1024]
            x0t = SF[:, pi * 1024:(pi + 1) * 1024]
            E.mark(v.tensor_add(CONT[:, pi * 1024:(pi + 1) * 1024], ps, x0t),
                   "vo", f"{P}.r1")
            E.mark(v.reduce_sum(SMS[:, 790 + pi * 16: 791 + pi * 16],
                                CONT[:, pi * 1024:(pi + 1) * 1024], AX.X),
                   "vo", f"{P}.sum")

        @b_.scalar
        def _(s, tt=tt, pi=pi, P=P):
            E.wait(s, f"{P}.r1")
            E.mark(s.activation(SSC[:, 0:1024],
                                CONT[:, pi * 1024:(pi + 1) * 1024], AF.Square,
                                accum_out=SMS[:, 791 + pi * 16: 792 + pi * 16]),
                   "so", f"{P}.sq")

        @b_.vector
        def _(v, tt=tt, pi=pi, P=P):
            E.wait(v, f"{P}.sq")
            base = 790 + pi * 16
            v.tensor_scalar(SMS[:, base + 2: base + 3], SMS[:, base: base + 1],
                            1.0 / 1024.0, None, ALU.mult)
            v.tensor_scalar(SMS[:, base + 3: base + 4], SMS[:, base + 1: base + 2],
                            1.0 / 1024.0, None, ALU.mult)
            v.tensor_mul(SMS[:, base + 4: base + 5], SMS[:, base + 2: base + 3],
                         SMS[:, base + 2: base + 3])
            v.tensor_sub(SMS[:, base + 3: base + 4], SMS[:, base + 3: base + 4],
                         SMS[:, base + 4: base + 5])
            E.mark(v.tensor_scalar(SMS[:, base + 3: base + 4],
                                   SMS[:, base + 3: base + 4], 1.0, 1e-5,
                                   ALU.mult, ALU.add), "vo", f"{P}.var")

        @b_.scalar
        def _(s, tt=tt, pi=pi, P=P):
            E.wait(s, f"{P}.var")
            base = 790 + pi * 16
            E.mark(s.activation(SMS[:, base + 5: base + 6],
                                SMS[:, base + 3: base + 4], AF.Sqrt),
                   "so", f"{P}.sd")

        @b_.vector
        def _(v, tt=tt, pi=pi, P=P):
            E.wait(v, f"{P}.sd")
            base = 790 + pi * 16
            v.reciprocal(SMS[:, base + 6: base + 7], SMS[:, base + 5: base + 6])
            v.tensor_mul(SMS[:, base + 7: base + 8], SMS[:, base + 2: base + 3],
                         SMS[:, base + 6: base + 7])
            E.mark(v.tensor_scalar(SMS[:, base + 7: base + 8],
                                   SMS[:, base + 7: base + 8], -1.0, None,
                                   ALU.mult), "vo", f"{P}.nmb")

        @b_.vector
        def _(v, tt=tt, pi=pi, P=P):
            E.wait(v, f"{P}.nmb")
            if tt >= 1:
                E.wait(v, f"ln.{tt - 1}.x1f")
                E.wait(v, f"ln.{tt - 1}.x1st")
            base = 790 + pi * 16
            E.mark(v.tensor_scalar(CONT[:, (1 - pi) * 1024:(2 - pi) * 1024],
                                   CONT[:, pi * 1024:(pi + 1) * 1024],
                                   SMS[:, base + 2: base + 3],
                                   SMS[:, base + 6: base + 7],
                                   ALU.subtract, ALU.mult),
                   "vo", f"{P}.x1")

        @b_.vector
        def _(v, tt=tt, pi=pi, P=P):
            E.wait(v, f"{P}.x1")
            if tt >= 2:
                E.wait(v, f"ln.{tt - 2}.x1st")
            E.mark(v.tensor_copy(
                SSC[:, 2816 + pi * 1024: 2816 + (pi + 1) * 1024],
                CONT[:, (1 - pi) * 1024:(2 - pi) * 1024]), "vo", f"{P}.x1b")

        @b_.sync
        def _(e, tt=tt, pi=pi, P=P):
            E.wait(e, f"{P}.x1b")
            E.mark(e.dma_start(out=X1F[tt * 128:(tt + 1) * 128, :],
                               in_=CONT[:, (1 - pi) * 1024:(2 - pi) * 1024]),
                   "sd", f"{P}.x1f", 16)

        @b_.scalar
        def _(s, tt=tt, pi=pi, P=P):
            E.wait(s, f"{P}.x1b")
            E.mark(s.dma_start(out=DR["x1b"][tt * 128:(tt + 1) * 128, :],
                               in_=SSC[:, 2816 + pi * 1024: 2816 + (pi + 1) * 1024]),
                   "hd", f"{P}.x1st", 16)

        @b_.vector
        def _(v, tt=tt, P=P):
            E.wait(v, f"{P}.x1b")
            E.mark(v.engine_nop(), "vo", f"{P}.x1done")

    att.close()
    mh = ExitStack()
    _stacks.append(mh)
    X1T = mh.enter_context(nc.sbuf_tensor("X1T", [128, 16384], BF16))
    WB3 = mh.enter_context(nc.sbuf_tensor("WB3", [128, 8192], BF16))
    SC3 = mh.enter_context(nc.sbuf_tensor("SC3", [128, 4096], BF16))
    MR = mh.enter_context(nc.sbuf_tensor("MR", [128, 4096], BF16))
    VSTG = mh.enter_context(nc.sbuf_tensor("VSTG", [128, 1040], BF16))
    KB = mh.enter_context(nc.sbuf_tensor("KB", [128, 8192], BF16))
    VB = mh.enter_context(nc.sbuf_tensor("VB", [128, 8320], BF16))
    QH = mh.enter_context(nc.sbuf_tensor("QH", [128, 2048], BF16))
    PB = mh.enter_context(nc.sbuf_tensor("PB", [128, 2048], BF16))
    YPAIR = mh.enter_context(nc.sbuf_tensor("YPAIR", [128, 2048], BF16))
    YTS = mh.enter_context(nc.sbuf_tensor("YTS", [128, 2048], BF16))
    FSC = mh.enter_context(nc.sbuf_tensor("FSC", [128, 64], F32))

    b3 = blk()

    @b3.sync
    def _(e):
        E.wait(e, "ln.15.x1st")
        for dc in range(8):
            E.mark(e.dma_start(out=X1T[:, dc * 2048:(dc + 1) * 2048],
                               in_=DR["x1b"][:, dc * 128:(dc + 1) * 128],
                               transpose=True), "sd", f"x1t.{dc}", 16)

    @b3.gpsimd
    def _(e):
        E.mark(e.dma_start(out=MR[:, 0:2048], in_=IN["c_mcos2"][:, :]), "gd", None, 16)
        E.mark(e.dma_start(out=MR[:, 2048:4096], in_=IN["c_msin2"][:, :]), "gd", "mr", 16)
        E.wait(e, "ln.15.x1st")
        E.mark(e.dma_start(out=WB3[:, :].rearrange("p (a o) -> p a o", a=8),
                           in_=IN["wqkv"][:, 0:1024].rearrange("(a p) o -> p a o", p=128)),
               "gd", "w3.q", 16)

    bfold = blk()

    @bfold.vector
    def _(v):
        E.wait(v, "w3.q")
        last = None
        for dc in range(8):
            last = v.tensor_scalar(WB3[:, dc * 1024:(dc + 1) * 1024],
                                   WB3[:, dc * 1024:(dc + 1) * 1024],
                                   CF["g1c"][:, dc:dc + 1], None, ALU.mult)
        E.mark(last, "vo", "w3.qf")

    def qk_proj(tag, dst_is_q):
        for oc in range(8):
            P = f"{tag}.{oc}"
            b_ = blk()
            for ts in range(4):
                @b_.tensor
                def _(t, oc=oc, P=P, tag=tag, ts=ts):
                    if oc == 0 and ts == 0:
                        E.wait(t, f"w3.{tag}f")
                        E.wait(t, "x1t.7")
                        E.wait(t, "mr")
                    if oc > 0:
                        E.wait(t, f"{tag}.{oc - 1}.ev.{ts}")
                    last = None
                    for dc in range(8):
                        last = t.matmul(
                            psA[:, ts * 512:(ts + 1) * 512],
                            WB3[:, dc * 1024 + oc * 128: dc * 1024 + oc * 128 + 128],
                            X1T[:, dc * 2048 + ts * 512: dc * 2048 + ts * 512 + 512],
                            start=(dc == 0), stop=(dc == 7))
                    E.mark(last, "tm", f"{P}.mm.{ts}")

                @b_.vector
                def _(v, oc=oc, P=P, ts=ts):
                    E.wait(v, f"{P}.mm.{ts}")
                    E.mark(v.tensor_copy(SC3[:, ts * 512:(ts + 1) * 512],
                                         psA[:, ts * 512:(ts + 1) * 512]),
                           "vo", f"{P}.ev.{ts}")

                @b_.tensor
                def _(t, oc=oc, P=P, ts=ts):
                    E.wait(t, f"{P}.ev.{ts}")
                    if ts >= 2:
                        E.wait(t, f"{P}.rp.{ts - 2}")
                    E.mark(t.matmul(psB[:, (ts % 2) * 512:(ts % 2 + 1) * 512],
                                    CV["perm"],
                                    SC3[:, ts * 512:(ts + 1) * 512],
                                    start=True, stop=True), "tm", f"{P}.pm.{ts}")

                @b_.vector
                def _(v, oc=oc, P=P, ts=ts, tag=tag):
                    E.wait(v, f"{P}.pm.{ts}")
                    if ts == 0:
                        if oc > 0:
                            E.wait(v, f"{tag}.{oc - 1}.st")
                        elif tag == "k":
                            E.wait(v, "q.7.st")
                    v.scalar_tensor_tensor(
                        SC3[:, 2048 + ts * 512: 2048 + (ts + 1) * 512],
                        psB[:, (ts % 2) * 512:(ts % 2 + 1) * 512], 1.0,
                        MR[:, 2048 + ts * 512: 2048 + (ts + 1) * 512],
                        ALU.mult, ALU.mult)
                    v.tensor_mul(SC3[:, ts * 512:(ts + 1) * 512],
                                 SC3[:, ts * 512:(ts + 1) * 512],
                                 MR[:, ts * 512:(ts + 1) * 512])
                    E.mark(v.tensor_add(SC3[:, 2048 + ts * 512: 2048 + (ts + 1) * 512],
                                        SC3[:, 2048 + ts * 512: 2048 + (ts + 1) * 512],
                                        SC3[:, ts * 512:(ts + 1) * 512]),
                           "vo", f"{P}.rp.{ts}")

            @b_.scalar
            def _(s, oc=oc, P=P, dst_is_q=dst_is_q):
                E.wait(s, f"{P}.rp.3")
                if dst_is_q:
                    dst = DR["qmhad"][oc * 128:(oc + 1) * 128, :]
                else:
                    dst = DR["kv_in"][:, oc * 2048:(oc + 1) * 2048]
                E.mark(s.dma_start(out=dst, in_=SC3[:, 2048:4096]), "hd", f"{P}.st", 16)

    qk_proj("q", True)

    bwk = blk()

    @bwk.gpsimd
    def _(e):
        E.wait(e, "q.7.rp.3")
        E.mark(e.dma_start(out=WB3[:, :].rearrange("p (a o) -> p a o", a=8),
                           in_=IN["wqkv"][:, 1024:2048].rearrange("(a p) o -> p a o", p=128)),
               "gd", "w3.kraw", 16)

    bfold2 = blk()

    @bfold2.vector
    def _(v):
        E.wait(v, "w3.kraw")
        last = None
        for dc in range(8):
            last = v.tensor_scalar(WB3[:, dc * 1024:(dc + 1) * 1024],
                                   WB3[:, dc * 1024:(dc + 1) * 1024],
                                   CF["g1c"][:, dc:dc + 1], None, ALU.mult)
        E.mark(last, "vo", "w3.kf")

    qk_proj("k", False)

    bwv = blk()

    @bwv.gpsimd
    def _(e):
        E.wait(e, "k.7.rp.3")
        E.mark(e.dma_start(out=WB3[:, :].rearrange("p (a o) -> p a o", a=8),
                           in_=IN["wqkv"][:, 2048:3072].rearrange("(a p) o -> p a o", p=128)),
               "gd", "w3.vraw", 16)

    bfold3 = blk()

    @bfold3.vector
    def _(v):
        E.wait(v, "w3.vraw")
        last = None
        for dc in range(8):
            last = v.tensor_scalar(WB3[:, dc * 1024:(dc + 1) * 1024],
                                   WB3[:, dc * 1024:(dc + 1) * 1024],
                                   CF["g1c"][:, dc:dc + 1], None, ALU.mult)
        E.mark(last, "vo", "w3.vf")
        E.mark(v.memset(VSTG[:, :], 1.0), "vo", "vstg1")

    for tt in range(NT):
        P = f"v3.{tt}"
        b_ = blk()

        @b_.tensor
        def _(t, tt=tt, P=P):
            if tt == 0:
                E.wait(t, "w3.vf")
            if tt >= 2:
                E.wait(t, f"v3.{tt - 2}.ev")
            ps = psB[:, :] if tt % 2 == 0 else psA[:, 1024:2048]
            last = None
            for dc in range(8):
                for half in range(2):
                    last = t.matmul(
                        ps[:, half * 512:(half + 1) * 512],
                        X1T[:, dc * 2048 + tt * 128: dc * 2048 + tt * 128 + 128],
                        WB3[:, dc * 1024 + half * 512: dc * 1024 + (half + 1) * 512],
                        start=(dc == 0), stop=(dc == 7))
            E.mark(last, "tm", f"{P}.mm")

        @b_.vector
        def _(v, tt=tt, P=P):
            E.wait(v, f"{P}.mm")
            if tt == 0:
                E.wait(v, "vstg1")
            if tt >= 1:
                E.wait(v, f"v3.{tt - 1}.st")
            ps = psB[:, :] if tt % 2 == 0 else psA[:, 1024:2048]
            last = None
            for h in range(HM):
                last = v.tensor_copy(VSTG[:, h * 65: h * 65 + 64],
                                     ps[:, h * 64:(h + 1) * 64])
            E.mark(last, "vo", f"{P}.ev")

        @b_.scalar
        def _(s, tt=tt, P=P):
            E.wait(s, f"{P}.ev")
            E.mark(s.dma_start(
                out=DR["kv_in"][:, 16384:33024].rearrange(
                    "p (h c) -> p h c", h=16)[:, :, tt * 65:(tt + 1) * 65],
                in_=VSTG[:, :].rearrange("p (h c) -> p h c", h=16)),
                "hd", f"{P}.st", 16)

    cut("qkv")

    bag3 = blk()

    @bag3.gpsimd
    def _(e):
        E.wait(e, "v3.15.st")
        E.wait(e, "k.7.st")
        E.mark(e.collective_compute(
            "AllGather", ALU.bypass,
            replica_groups=[[0, 1, 2, 3], [4, 5, 6, 7]],
            ins=[DR["kv_in"].tensor.ap()],
            outs=[DR["kv_out"].tensor.ap()],
        ), "cc", "ag.kv")

    cut("ag")

    LENA = {g: 8 * g + 8 for g in range(4)}

    def kv_loads(e, h):
        hp = h % 2
        if h == 0:
            E.wait(e, "ag.kv")
        if h >= 2:
            E.wait(e, f"fl{h - 2}.ev1")
        last = None
        for r in range(4):
            for sl in range(2):
                gc = r if sl == 0 else 7 - r
                last = e.dma_start(
                    out=KB[hp * 64:(hp + 1) * 64, gc * 1024:(gc + 1) * 1024],
                    in_=DR["kv_out"][128 * r + hp * 64: 128 * r + hp * 64 + 64,
                                     (h // 2) * 2048 + sl * 1024:
                                     (h // 2) * 2048 + (sl + 1) * 1024])
                E.mark(last, "sd", None, 16)
                last = e.dma_start(
                    out=VB[:, hp * 4160 + gc * 520: hp * 4160 + (gc + 1) * 520],
                    in_=DR["kv_out"][128 * r:128 * (r + 1),
                                     16384 + h * 1040 + sl * 520:
                                     16384 + h * 1040 + (sl + 1) * 520])
                E.mark(last, "sd", None, 16)
        last = e.dma_start(out=QH[hp * 64:(hp + 1) * 64, :],
                           in_=DR["qmhad"][h * 64:(h + 1) * 64, :])
        E.mark(last, "sd", f"kv{h}.ld", 16)

    TM0, SO0, VO0, GD0 = E.n["tm"], E.n["so"], E.n["vo"], E.n["gd"]
    base_tm, base_so, base_vo = TM0, SO0, VO0
    for h in range(HM):
        P = f"fl{h}"
        for i in range(72):
            E.declare(f"{P}.s{i}", "tm", base_tm + (i + 1 if i < 2 else 2 * i))
            E.declare(f"{P}.p{i}", "tm", base_tm + 3 + i + min(i, 70))
            E.declare(f"{P}.x{i}", "so", base_so + i + 1)
        E.declare(f"{P}.ev0", "vo", base_vo + 2)
        E.declare(f"{P}.ev1", "vo", base_vo + 4)
        E.declare(f"{P}.pA", "fa", h + 1)
        base_tm += 144
        base_so += 72
        base_vo += 4
        if h % 2 == 1:
            pair = h // 2
            for qt16 in range(16):
                E.declare(f"yt{pair}.t{qt16}", "tm", base_tm + qt16 + 1)
            base_tm += 16
            for bch in range(4):
                E.declare(f"yt{pair}.c{bch}", "vo", base_vo + bch + 1)
            base_vo += 4
            E.declare(f"yt{pair}.st", "gd", GD0 + 16 * (pair + 1))

    bfl = blk()

    @bfl.sync
    def _(e):
        for h in range(HM):
            kv_loads(e, h)

    @bfl.tensor
    def _(t):
        rk = t.partition_id() % 4
        snap = dict(E.n)
        ends = []
        for g in nc.Switch(engines=[t], index=[rk], n=4):
            E.switch_body_begin(snap)
            sched = ([(0, j) for j in range(8 * g + 8)]
                     + [(1, j) for j in range(64 - 8 * g)])
            assert len(sched) == 72
            cpos = {0: g, 1: 7 - g}
            chunk_start = {0: 0, 1: 8 * g + 8}
            for h in range(HM):
                hp = h % 2
                P = f"fl{h}"
                fa_pending = [False]

                def emit_s(i, h=h, hp=hp, P=P, fa_pending=fa_pending):
                    cs, j = sched[i]
                    c = cpos[cs]
                    diag = (j >= 8 * c)
                    d = j - 8 * c
                    buf = i % 2
                    if i == 0:
                        E.wait(t, f"kv{h}.ld")
                    if i < 2:
                        if h >= 1:
                            E.wait(t, f"fl{h - 1}.x{70 + i}")
                    else:
                        E.wait(t, f"{P}.x{i - 2}")
                    mms = []
                    for halfq in range(2):
                        mms.append((
                            psA[:, buf * 1024 + halfq * 512:
                                buf * 1024 + (halfq + 1) * 512],
                            KB[hp * 64:(hp + 1) * 64, j * 128:(j + 1) * 128],
                            QH[hp * 64:(hp + 1) * 64,
                               cs * 1024 + halfq * 512:
                               cs * 1024 + (halfq + 1) * 512],
                            "lo" if halfq == 0 else "hi"))
                    if diag:
                        mms.append((
                            psA[:, buf * 1024 + d * 128:
                                buf * 1024 + (d + 1) * 128],
                            CV["identb"], CV["negm"], "lo" if d < 4 else "hi"))
                    first = {}
                    last = {}
                    for k, (_, _, _, bk) in enumerate(mms):
                        first.setdefault(bk, k)
                        last[bk] = k
                    inst = None
                    for k, (o, l, r, bk) in enumerate(mms):
                        inst = t.matmul(o, l, r, start=(first[bk] == k),
                                        stop=(last[bk] == k))
                        if k == 0 and fa_pending[0]:
                            E.mark(inst, "fa", f"{P}.pA")
                            fa_pending[0] = False
                    E.mark(inst, "tm", f"{P}.s{i}")

                emit_s(0)
                emit_s(1)
                for i in range(72):
                    cs, j = sched[i]
                    c = cpos[cs]
                    diag = (j >= 8 * c)
                    d = j - 8 * c if diag else -1
                    buf = i % 2
                    S0 = chunk_start[cs]
                    E.wait(t, f"{P}.x{i}")
                    if i == S0:
                        if cs == 1:
                            E.wait(t, f"{P}.ev0")
                        elif h >= 1:
                            E.wait(t, f"fl{h - 1}.ev1")
                    b4_last = S0 + 8 * c + 3
                    b5_last = S0 + 8 * c + 7
                    inst = None
                    for qt in range(8):
                        if diag and qt < d:
                            continue
                        start = (i == S0) and (qt == 0 or qt == 4)
                        stop = ((i == b4_last and qt == 3)
                                or (i == b5_last and qt == 7))
                        inst = t.matmul(
                            psB[:, qt * 128: qt * 128 + 65],
                            PB[:, buf * 1024 + qt * 128:
                               buf * 1024 + (qt + 1) * 128],
                            VB[:, hp * 4160 + j * 65: hp * 4160 + (j + 1) * 65],
                            start=start, stop=stop)
                    E.mark(inst, "tm", f"{P}.p{i}")
                    if cs == 0 and i == S0 + 8 * c + 7:
                        fa_pending[0] = True
                    if i + 2 < 72:
                        emit_s(i + 2)
                assert not fa_pending[0]
                if h % 2 == 1:
                    pair = h // 2
                    for bch in range(4):
                        if bch == 0:
                            E.wait(t, f"fl{h}.ev1")
                            if pair > 0:
                                E.wait(t, f"yt{pair - 1}.c3")
                        else:
                            E.wait(t, f"yt{pair}.c{bch - 1}")
                        for jj in range(4):
                            qt16 = bch * 4 + jj
                            E.mark(t.transpose(
                                psC[:, jj * 256: jj * 256 + 128],
                                YPAIR[:, qt16 * 128:(qt16 + 1) * 128],
                                CV["identb"]), "tm", f"yt{pair}.t{qt16}")
            ends.append(dict(E.n))
        E.switch_end(ends)

    @bfl.scalar
    def _(s):
        for h in range(HM):
            P = f"fl{h}"
            for i in range(72):
                E.wait(s, f"{P}.s{i}")
                if i >= 2:
                    E.wait(s, f"{P}.p{i - 2}")
                elif h >= 1:
                    E.wait(s, f"fl{h - 1}.p{70 + i}")
                buf = i % 2
                E.mark(s.activation(PB[:, buf * 1024:(buf + 1) * 1024],
                                    psA[:, buf * 1024:(buf + 1) * 1024],
                                    AF.Exp, scale=0.125), "so", f"{P}.x{i}")

    @bfl.vector
    def _(v):
        for h in range(HM):
            hp = h % 2
            P = f"fl{h}"
            for cs in range(2):
                if cs == 0:
                    E.wait(v, f"{P}.pA")
                    if h % 2 == 0 and h >= 2:
                        E.wait(v, f"yt{h // 2 - 1}.st")
                else:
                    E.wait(v, f"{P}.p71")
                rc = E.mark(v.reciprocal(
                    FSC[:, 0:8].rearrange("p (a b) -> p a b", b=1),
                    psB[:, :].rearrange("p (a b) -> p a b", a=8)[:, :, 64:65]),
                    "vo")
                v.wait_ge(E.h["vo"], rc)
                last = None
                for qt in range(8):
                    last = v.tensor_scalar(
                        YPAIR[:, (cs * 8 + qt) * 128 + hp * 64:
                              (cs * 8 + qt) * 128 + hp * 64 + 64],
                        psB[:, qt * 128: qt * 128 + 64],
                        FSC[:, qt:qt + 1], None, ALU.mult)
                E.mark(last, "vo", f"{P}.ev{cs}")
            if h % 2 == 1:
                pair = h // 2
                for bch in range(4):
                    E.wait(v, f"yt{pair}.t{bch * 4 + 3}")
                    last = None
                    for jj in range(4):
                        qt16 = bch * 4 + jj
                        last = v.tensor_copy(
                            YTS[:, qt16 * 128:(qt16 + 1) * 128],
                            psC[:, jj * 256: jj * 256 + 128])
                    E.mark(last, "vo", f"yt{pair}.c{bch}")

    @bfl.gpsimd
    def _(e):
        for pair in range(8):
            E.wait(e, f"yt{pair}.c3")
            E.mark(e.dma_start(out=DR["ybT"][pair * 128:(pair + 1) * 128, :],
                               in_=YTS[:, :]), "gd", f"yt{pair}.st", 16)

    cut("flash")

    mh.close()
    mo2 = ExitStack()
    _stacks.append(mo2)
    GSB = mo2.enter_context(nc.sbuf_tensor("GSB", [128, 128], F32))
    GEB = mo2.enter_context(nc.sbuf_tensor("GEB", [128, 128], F32))
    MSKB = mo2.enter_context(nc.sbuf_tensor("MSKB", [128, 128], F32))
    TAUL = mo2.enter_context(nc.sbuf_tensor("TAUL", [128, 128], F32))
    KEPTB = mo2.enter_context(nc.sbuf_tensor("KEPTB", [128, 128], F32))
    RKS = mo2.enter_context(nc.sbuf_tensor("RKS", [128, 128], F32))
    TIDG = mo2.enter_context(nc.sbuf_tensor("TIDG", [128, 32], F32))
    PXA = mo2.enter_context(nc.sbuf_tensor("PXA", [128, 16], F32))
    PXB = mo2.enter_context(nc.sbuf_tensor("PXB", [128, 16], F32))
    DMY = mo2.enter_context(nc.sbuf_tensor("DMY", [128, 16], F32))
    BIS = mo2.enter_context(nc.sbuf_tensor("BIS", [1, 16], F32))
    MIDC = mo2.enter_context(nc.sbuf_tensor("MIDC", [128, 1], F32))
    CNTR = mo2.enter_context(nc.sbuf_tensor("CNTR", [1, 128], F32))
    TROW = mo2.enter_context(nc.sbuf_tensor("TROW", [1, 128], F32))
    TAU8 = mo2.enter_context(nc.sbuf_tensor("TAU8", [8, 1], F32))
    PAT8S = mo2.enter_context(nc.sbuf_tensor("PAT8S", [8, 128], F32))
    RTOT = mo2.enter_context(nc.sbuf_tensor("RTOT", [128, 1], F32))
    CARR = mo2.enter_context(nc.sbuf_tensor("CARR", [128, 1], F32))
    IOTH = mo2.enter_context(nc.sbuf_tensor("IOTH", [128, 256], F16))
    TLC = mo2.enter_context(nc.sbuf_tensor("TLC", [128, 16], F16))
    PPB = mo2.enter_context(nc.sbuf_tensor("PPB", [128, 512], F16))
    TGE = mo2.enter_context(nc.sbuf_tensor("TGE", [128, 32], F16))
    XGB = mo2.enter_context(nc.sbuf_tensor("XGB", [128, 2048], F16))
    XIDX = mo2.enter_context(nc.sbuf_tensor("XIDX", [128, 16], U32))
    YIDX = mo2.enter_context(nc.sbuf_tensor("YIDX", [128, 16], U32))
    YRW = mo2.enter_context(nc.sbuf_tensor("YRW", [128, 16], U32))
    mh2p = ExitStack()
    _stacks.append(mh2p)
    PRE = mh2p.enter_context(nc.sbuf_tensor("PRE", [128, 16384], F32))
    mh2 = ExitStack()
    _stacks.append(mh2)
    WO2 = mh2.enter_context(nc.sbuf_tensor("WO2", [128, 8192], BF16))
    YTT = mh2.enter_context(nc.sbuf_tensor("YTT", [128, 2048], BF16))
    XF2 = mh2.enter_context(nc.sbuf_tensor("XF2", [128, 2048], F32))
    _dumpables["PRE"] = PRE

    bwo = blk()

    @bwo.gpsimd
    def _(e):
        E.wait(e, "yt7.st")
        E.mark(e.dma_start(out=WO2[:, :].rearrange("p (a o) -> p a o", a=8),
                           in_=IN["wo2"].rearrange("(a p) o -> p a o", p=128)),
               "gd", "wo2.w", 16)

    b_ = blk()
    for tt in range(NT):
        P = f"wo2.{tt}"

        @b_.sync
        def _(e, tt=tt, P=P):
            if tt == 0:
                E.wait(e, "yt7.st")
            if tt >= 2:
                E.wait(e, f"wo2.{tt - 2}.pre")
            E.mark(e.dma_start(
                out=YTT[:, (tt % 2) * 1024:(tt % 2 + 1) * 1024].rearrange(
                    "p (a c) -> p a c", a=8),
                in_=DR["ybT"][:, tt * 128:(tt + 1) * 128].rearrange(
                    "(a p) c -> p a c", p=128)), "sd", f"{P}.yt", 16)

        @b_.gpsimd
        def _(e, tt=tt, P=P):
            if tt >= 2:
                E.wait(e, f"wo2.{tt - 2}.pre")
            E.wait(e, f"ln.{tt}.x1f")
            E.mark(e.dma_start(out=XF2[:, (tt % 2) * 1024:(tt % 2 + 1) * 1024],
                               in_=X1F[tt * 128:(tt + 1) * 128, :]),
                   "gd", f"{P}.x1", 16)

        @b_.tensor
        def _(t, tt=tt, P=P):
            if tt == 0:
                E.wait(t, "wo2.w")
            E.wait(t, f"{P}.yt")
            if tt >= 2:
                E.wait(t, f"wo2.{tt - 2}.pre")
            ps = psB[:, :] if tt % 2 == 0 else psA[:, 0:1024]
            mms = []
            for dc in range(8):
                for half in range(2):
                    mms.append((ps[:, half * 512:(half + 1) * 512],
                                YTT[:, (tt % 2) * 1024 + dc * 128:
                                    (tt % 2) * 1024 + (dc + 1) * 128],
                                WO2[:, dc * 1024 + half * 512: dc * 1024 + (half + 1) * 512],
                                f"h{half}"))
            E.mark(mm_bank_group(t, mms), "tm", f"{P}.mm")

        @b_.vector
        def _(v, tt=tt, P=P):
            E.wait(v, f"{P}.mm")
            E.wait(v, f"{P}.x1")
            ps = psB[:, :] if tt % 2 == 0 else psA[:, 0:1024]
            E.mark(v.tensor_add(PRE[:, tt * 1024:(tt + 1) * 1024], ps,
                                XF2[:, (tt % 2) * 1024:(tt % 2 + 1) * 1024]),
                   "vo", f"{P}.pre")

    if DEBUG_STAGE == 2:
        bd2 = blk()

        @bd2.sync
        def _(e):
            for tt in range(NT):
                E.wait(e, f"wo2.{tt}.pre")
            for tt in range(NT):
                E.mark(e.dma_start(out=OUT[tt * 128:(tt + 1) * 128, :],
                                   in_=PRE[:, tt * 1024:(tt + 1) * 1024]),
                       "sd", f"dbg2.{tt}", 16)
            E.wait(e, f"dbg2.{NT - 1}")
        if _cur_blk[0] is not None:
            _cur_blk[0].__exit__(None, None, None)
            _cur_blk[0] = None
        mh2.close()
        mh2p.close()
        st.close()
        return nc

    if DEBUG_STAGE == 1:
        bdbg = blk()

        @bdbg.sync
        def _(e):
            for tt in range(NT):
                E.wait(e, f"ln.{tt}.x1f")
            for tt in range(NT):
                E.mark(e.dma_start(out=OUT[tt * 128:(tt + 1) * 128, :],
                                   in_=X1F[tt * 128:(tt + 1) * 128, :]),
                       "sd", f"dbg.{tt}", 16)
            E.wait(e, f"dbg.{NT - 1}")
        if _cur_blk[0] is not None:
            _cur_blk[0].__exit__(None, None, None)
            _cur_blk[0] = None
        mh2.close()
        mh2p.close()
        mo2.close()
        st.close()
        return nc

    mh2.close()
    mo1 = ExitStack()
    _stacks.append(mo1)
    XH = mo1.enter_context(nc.sbuf_tensor("XH", [128, 16384], F16))
    XTR = mo1.enter_context(nc.sbuf_tensor("XTR", [128, 16384], F16))
    JNK = mo1.enter_context(nc.sbuf_tensor("JNK", [128, 1024], F32))
    WRS = mo1.enter_context(nc.sbuf_tensor("WRS", [128, 64], F16))
    GAB = mo1.enter_context(nc.sbuf_tensor("GAB", [128, 1024], F32))
    MSB = mo1.enter_context(nc.sbuf_tensor("MSB", [128, 1024], F32))
    ROUT = mo1.enter_context(nc.sbuf_tensor("ROUT", [128, 64], F32))

    b4a = blk()

    @b4a.sync
    def _(e):
        for tt in range(NT):
            E.wait(e, f"wo2.{tt}.pre")
            E.mark(e.dma_start(out=DR["PREd"][tt * 128:(tt + 1) * 128, :],
                               in_=PRE[:, tt * 1024:(tt + 1) * 1024]),
                   "sd", f"p4.pred{tt}", 16)

    @b4a.scalar
    def _(s):
        for tt in range(NT):
            E.wait(s, f"wo2.{tt}.pre")
            E.mark(s.activation(JNK[:, :], PRE[:, tt * 1024:(tt + 1) * 1024],
                                AF.Square,
                                accum_out=SMS[:, 528 + tt: 529 + tt]),
                   "so", f"p4.sq{tt}")

    @b4a.vector
    def _(v):
        E.wait(v, "p4.sq15")
        E.mark(v.tensor_scalar(SMS[:, 544:560], SMS[:, 528:544],
                               1.0 / 1024.0, 1e-6, ALU.mult, ALU.add),
               "vo", "p4.vv")

    @b4a.scalar
    def _(s):
        E.wait(s, "p4.vv")
        E.mark(s.activation(SMS[:, 560:576], SMS[:, 544:560], AF.Sqrt),
               "so", "p4.sd2")

    @b4a.vector
    def _(v):
        E.wait(v, "p4.sd2")
        rc = E.mark(v.reciprocal(SMS[:, 576:592], SMS[:, 560:576]), "vo")
        v.wait_ge(E.h["vo"], rc)
        for tt in range(NT):
            E.mark(v.tensor_scalar(XH[:, tt * 1024:(tt + 1) * 1024],
                                   PRE[:, tt * 1024:(tt + 1) * 1024],
                                   SMS[:, 576 + tt: 577 + tt], None, ALU.mult),
                   "vo", f"p4.xh{tt}")

    @b4a.scalar
    def _(s):
        for tt in range(NT):
            E.wait(s, f"p4.xh{tt}")
            E.mark(s.dma_start(out=DR["xsd"][tt * 128:(tt + 1) * 128, :],
                               in_=XH[:, tt * 1024:(tt + 1) * 1024]),
                   "hd", f"p4.xsd{tt}", 16)

    @b4a.gpsimd
    def _(e):
        E.mark(e.dma_start(out=IOTH[:, :], in_=IN["c_iotah"][:, :]), "gd", None, 16)
        E.mark(e.dma_start(out=TLC[:, :], in_=IN["c_tloch"][:, :]), "gd", None, 16)
        E.mark(e.dma_start(out=DMY[:, :], in_=IN["c_dumy"][:, :]), "gd", None, 16)
        E.mark(e.dma_start(out=MSB[:, :], in_=IN["c_moemask"][:, :]), "gd", "p4.msb", 16)
        E.mark(e.dma_start(out=YRW[:, :], in_=IN["c_yrows"][:, :]), "gd", "p4.yrw", 16)
        E.mark(e.dma_start(out=PAT8S[:, :], in_=IN["c_pat8"][:, :]), "gd", "p4.p8", 16)
        E.mark(e.dma_start(out=WRS[:, :].rearrange("p (a j) -> p a j", a=8),
                           in_=IN["wrout"].rearrange("(a p) j -> p a j", p=128)),
               "gd", "p4.wrs", 16)
        E.wait(e, "p4.sq15")
        E.mark(e.memset(JNK[:, :], 0.0), "go", "p4.jz")
        E.wait(e, "p4.jz")
        for zi in range(18):
            E.mark(e.dma_start(out=DR["ystage"][zi * 128:(zi + 1) * 128, :],
                               in_=JNK[:, :]),
                   "gd", "p4.yz" if zi == 17 else None, 16)

    @b4a.vector
    def _(v):
        E.wait(v, "p4.wrs")
        last = None
        for a in range(8):
            last = v.tensor_scalar(WRS[:, a * 8:(a + 1) * 8],
                                   WRS[:, a * 8:(a + 1) * 8],
                                   CF["g2c"][:, a:a + 1], None, ALU.mult)
        E.mark(last, "vo", "p4.wrf")

    @b4a.sync
    def _(e):
        for tt in range(NT):
            E.wait(e, f"p4.xsd{tt}")
        for dc in range(8):
            E.mark(e.dma_start(out=XTR[:, dc * 2048:(dc + 1) * 2048],
                               in_=DR["xsd"][:, dc * 128:(dc + 1) * 128],
                               transpose=True), "sd", f"p4.xtr{dc}", 16)

    cut("p4a")
    b4b = blk()
    for ts_ in range(NT):
        @b4b.tensor
        def _(t, ts_=ts_):
            if ts_ == 0:
                E.wait(t, "p4.xtr7")
                E.wait(t, "p4.wrf")
            if ts_ >= 2:
                E.wait(t, f"p4.lge{ts_ - 2}")
            last = None
            for dc in range(8):
                last = t.matmul(
                    psB[:, (ts_ % 2) * 512:(ts_ % 2) * 512 + 8],
                    XTR[:, dc * 2048 + ts_ * 128: dc * 2048 + (ts_ + 1) * 128],
                    WRS[:, dc * 8:(dc + 1) * 8],
                    start=(dc == 0), stop=(dc == 7))
            E.mark(last, "tm", f"p4.lg{ts_}")

        @b4b.vector
        def _(v, ts_=ts_):
            w = vchain(v, E)
            E.wait(v, f"p4.lg{ts_}")
            w(v.tensor_copy(ROUT[:, 0:8],
                            psB[:, (ts_ % 2) * 512:(ts_ % 2) * 512 + 8]),
              f"p4.lge{ts_}")
            w(v.max(ROUT[:, 8:16], ROUT[:, 0:8]), f"p4.nm{ts_}")

        @b4b.scalar
        def _(s, ts_=ts_):
            E.wait(s, f"p4.nm{ts_}")
            s.activation(ROUT[:, 16:24], ROUT[:, 0:8], AF.Exp)
            E.mark(s.activation(ROUT[:, 25:26], ROUT[:, 9:10], AF.Exp),
                   "so", f"p4.ex{ts_}")

        @b4b.vector
        def _(v, ts_=ts_):
            w = vchain(v, E)
            E.wait(v, f"p4.ex{ts_}")
            w(v.reduce_sum(ROUT[:, 26:27],
                           ROUT[:, 16:24].rearrange("p (x a) -> p x a", x=1),
                           AX.X))
            w(v.reciprocal(ROUT[:, 27:28], ROUT[:, 26:27]))
            w(v.tensor_scalar(ROUT[:, 16:24], ROUT[:, 16:24], ROUT[:, 27:28],
                              None, ALU.mult))
            w(v.tensor_mul(ROUT[:, 28:29], ROUT[:, 25:26], ROUT[:, 27:28]))
            w(v.tensor_scalar(ROUT[:, 29:37], ROUT[:, 16:24], ROUT[:, 28:29],
                              None, ALU.is_ge))
            w(v.tensor_mul(GSB[:, ts_ * 8:(ts_ + 1) * 8], ROUT[:, 29:37],
                           ROUT[:, 16:24]), f"p4.g{ts_}")

    @b4b.sync
    def _(e):
        for ts_ in range(NT):
            E.wait(e, f"p4.g{ts_}")
        E.mark(e.dma_start(
            out=DR["gdram"][:, :].rearrange("(a p) j -> p a j", p=128),
            in_=GSB[:, :].rearrange("p (a j) -> p a j", a=16)),
            "sd", "p4.gst", 16)

    @b4b.gpsimd
    def _(e):
        E.wait(e, "p4.gst")
        E.mark(e.collective_compute("AllGather", ALU.bypass,
                                    replica_groups=[[0, 1, 2, 3, 4, 5, 6, 7]],
                                    ins=[DR["gdram"].tensor.ap()],
                                    outs=[DR["gag"].tensor.ap()]), "cc", "p4.gag")

    @b4b.sync
    def _(e):
        E.wait(e, "p4.gag")
        for r in range(8):
            E.mark(e.dma_start(
                out=GAB[:, r * 128:(r + 1) * 128].rearrange(
                    "p (a j) -> p a j", a=16),
                in_=DR["gag"][r * 2048:(r + 1) * 2048, :].rearrange(
                    "(a p) j -> p a j", p=128)),
                "sd", f"p4.ga{r}", 16)

    @b4b.vector
    def _(v):
        for r in range(8):
            E.wait(v, f"p4.ga{r}")
        E.wait(v, "p4.msb")
        v.tensor_mul(GAB[:, :], GAB[:, :], MSB[:, :])
        E.mark(v.reduce_sum(GEB[:, :],
                            GAB[:, :].rearrange("p (a j) -> p a j", j=8), AX.X),
               "vo", "p4.ge")

    _dumpables["GSB"] = GSB
    _dumpables["GEB"] = GEB
    _dumpables["KEPTB"] = KEPTB
    _dumpables["RKS"] = RKS
    _dumpables["TIDG"] = TIDG
    _dumpables["BIS"] = BIS
    _dumpables["TAUL"] = TAUL
    cut("p4b")
    b4c = blk()

    @b4c.vector
    def _(v):
        E.wait(v, "p4.ge")
        v.memset(BIS[0:1, 0:1], 0.0)
        E.mark(v.memset(BIS[0:1, 1:2], 1.0), "vo", "p4.bis0")

    NBIS = 45
    for it in range(NBIS):
        P = f"p4.b{it}"

        @b4c.vector
        def _(v, it=it, P=P):
            w = vchain(v, E)
            E.wait(v, "p4.bis0" if it == 0 else f"p4.b{it - 1}.upd")
            w(v.tensor_add(BIS[0:1, 2:3], BIS[0:1, 0:1], BIS[0:1, 1:2]))
            w(v.tensor_scalar(BIS[0:1, 2:3], BIS[0:1, 2:3], 0.5, None,
                              ALU.mult), f"{P}.mid")

        @b4c.tensor
        def _(t, it=it, P=P):
            E.wait(t, f"{P}.mid")
            E.mark(t.matmul(psD[:, 0:1], CF["onesrow"], BIS[0:1, 2:3],
                            start=True, stop=True), "tm", f"{P}.rep")

        @b4c.vector
        def _(v, it=it, P=P):
            w = vchain(v, E)
            E.wait(v, f"{P}.rep")
            w(v.tensor_copy(MIDC[:, :], psD[:, 0:1]))
            w(v.tensor_scalar(MSKB[:, :], GEB[:, :], MIDC[:, 0:1], None,
                              ALU.is_ge), f"{P}.msk")

        @b4c.tensor
        def _(t, it=it, P=P):
            E.wait(t, f"{P}.msk")
            E.mark(t.matmul(psD[0:1, 256:384], CF["onescol"], MSKB[:, :],
                            start=True, stop=True), "tm", f"{P}.cnt")

        @b4c.vector
        def _(v, it=it, P=P):
            w = vchain(v, E)
            E.wait(v, f"{P}.cnt")
            w(v.tensor_copy(CNTR[:, :], psD[0:1, 256:384]))
            w(v.reduce_sum(BIS[0:1, 3:4],
                           CNTR[0:1, :].rearrange("p (x a) -> p x a", x=1), AX.X))
            w(v.tensor_scalar(BIS[0:1, 4:5], BIS[0:1, 3:4], float(CAP), None,
                              ALU.is_ge))
            w(v.tensor_sub(BIS[0:1, 5:6], BIS[0:1, 2:3], BIS[0:1, 0:1]))
            w(v.tensor_mul(BIS[0:1, 5:6], BIS[0:1, 4:5], BIS[0:1, 5:6]))
            w(v.tensor_add(BIS[0:1, 0:1], BIS[0:1, 0:1], BIS[0:1, 5:6]))
            w(v.tensor_sub(BIS[0:1, 6:7], BIS[0:1, 1:2], BIS[0:1, 2:3]))
            w(v.tensor_mul(BIS[0:1, 6:7], BIS[0:1, 4:5], BIS[0:1, 6:7]))
            w(v.tensor_add(BIS[0:1, 1:2], BIS[0:1, 2:3], BIS[0:1, 6:7]),
              f"{P}.upd")

    cut("p4c")
    b4d = blk()

    @b4d.sync
    def _(e):
        E.wait(e, f"p4.b{NBIS - 1}.upd")
        E.mark(e.dma_start(out=DR["taud"][:, :], in_=BIS[0:1, 0:1]),
               "sd", "p4.tau", 16)

    @b4d.gpsimd
    def _(e):
        E.wait(e, "p4.tau")
        E.mark(e.collective_compute("AllGather", ALU.bypass,
                                    replica_groups=[[0, 1, 2, 3, 4, 5, 6, 7]],
                                    ins=[DR["taud"].tensor.ap()],
                                    outs=[DR["tau8d"].tensor.ap()]), "cc", "p4.tag")

    @b4d.sync
    def _(e):
        E.wait(e, "p4.tag")
        E.mark(e.dma_start(out=TAU8[:, :], in_=DR["tau8d"][:, :]),
               "sd", "p4.t8", 16)

    @b4d.tensor
    def _(t):
        E.wait(t, "p4.t8")
        E.wait(t, "p4.p8")
        E.mark(t.matmul(psD[0:1, 384:512], TAU8[:, :], PAT8S[:, :],
                        start=True, stop=True), "tm", "p4.trow")

    @b4d.vector
    def _(v):
        E.wait(v, "p4.trow")
        E.mark(v.tensor_copy(TROW[:, :], psD[0:1, 384:512]), "vo", "p4.trc")

    @b4d.tensor
    def _(t):
        E.wait(t, "p4.trc")
        E.mark(t.matmul(psD[:, 128:256], CF["onesrow"], TROW[0:1, :],
                        start=True, stop=True), "tm", "p4.tfull")

    @b4d.vector
    def _(v):
        w = vchain(v, E)
        E.wait(v, "p4.tfull")
        w(v.tensor_copy(TAUL[:, :], psD[:, 128:256]))
        w(v.tensor_tensor(KEPTB[:, :], GSB[:, :], TAUL[:, :], ALU.is_ge))
        w(v.tensor_scalar(MSKB[:, :], GSB[:, :], 0.0, None, ALU.is_gt))
        w(v.tensor_mul(KEPTB[:, :], KEPTB[:, :], MSKB[:, :]), "p4.kept")

    KEPT3 = KEPTB[:, :].rearrange("p (a j) -> p a j", j=8)
    for j in range(NE):
        P = f"p4.rk{j}"

        @b4d.vector
        def _(v, j=j, P=P):
            if j == 0:
                E.wait(v, "p4.kept")
            E.mark(v.reduce_sum(
                RTOT[:, :],
                KEPT3[:, :, j].rearrange("p (x a) -> p x a", x=1), AX.X),
                "vo", f"{P}.rt")

        @b4d.tensor
        def _(t, j=j, P=P):
            E.wait(t, f"{P}.rt")
            E.mark(t.matmul(psD[:, 0:1], CF["ustrict"], RTOT[:, :],
                            start=True, stop=True), "tm", f"{P}.ca")

        @b4d.vector
        def _(v, j=j, P=P):
            w = vchain(v, E)
            E.wait(v, f"{P}.ca")
            w(v.tensor_copy(CARR[:, :], psD[:, 0:1]))
            KJ = KEPT3[:, :, j]
            v.memset(PXA[:, 0:1], 0.0)
            w(v.tensor_copy(PXA[:, 1:16], KJ[:, 0:15]))
            cur, oth = PXA, PXB
            for k in (1, 2, 4, 8):
                v.tensor_copy(oth[:, 0:k], cur[:, 0:k])
                w(v.tensor_add(oth[:, k:16], cur[:, k:16], cur[:, 0:16 - k]))
                cur, oth = oth, cur
            w(v.tensor_scalar(cur[:, :], cur[:, :], CARR[:, 0:1], 512.0,
                              ALU.add, ALU.add))
            w(v.scalar_tensor_tensor(RKS[:, j * 16:(j + 1) * 16], KJ,
                                     -512.0, cur[:, :], ALU.mult, ALU.add),
              f"{P}.rk")

    for j in range(NE):
        P = f"p4.tg{j}"

        @b4d.vector
        def _(v, j=j, P=P):
            if j == 0:
                for jj in range(NE):
                    E.wait(v, f"p4.rk{jj}.rk")
            v.tensor_copy(
                TGE[:, :].rearrange("p (a x) -> p a x", x=2)[:, :, 0], TLC[:, :])
            E.mark(v.tensor_copy(
                TGE[:, :].rearrange("p (a x) -> p a x", x=2)[:, :, 1],
                GSB[:, :].rearrange("p (a j) -> p a j", j=8)[:, :, j]),
                "vo", f"{P}.tge")

        for a in range(16):
            @b4d.vector
            def _(v, j=j, a=a, P=P):
                if a == 0:
                    E.wait(v, f"{P}.tge")
                if a >= 2:
                    E.wait(v, f"{P}.mm{a - 2}")
                E.mark(v.tensor_scalar(PPB[:, (a % 2) * 256:(a % 2) * 256 + 256],
                                       IOTH[:, :],
                                       RKS[:, j * 16 + a: j * 16 + a + 1],
                                       None, ALU.is_equal),
                       "vo", f"{P}.pp{a}")

            @b4d.tensor
            def _(t, j=j, a=a, P=P):
                E.wait(t, f"{P}.pp{a}")
                if a == 0 and j > 0:
                    E.wait(t, f"p4.tg{j - 1}.ev")
                t.matmul(psD[:, 0:2], PPB[:, (a % 2) * 256:(a % 2) * 256 + 128],
                         TGE[:, 2 * a: 2 * a + 2],
                         start=(a == 0), stop=(a == 15))
                E.mark(t.matmul(psB[:, 0:2],
                                PPB[:, (a % 2) * 256 + 128:(a % 2) * 256 + 256],
                                TGE[:, 2 * a: 2 * a + 2],
                                start=(a == 0), stop=(a == 15)),
                       "tm", f"{P}.mm{a}")

        @b4d.vector
        def _(v, j=j, P=P):
            E.wait(v, f"{P}.mm15")
            v.tensor_copy(TIDG[:, j * 4: j * 4 + 2], psD[:, 0:2])
            E.mark(v.tensor_copy(TIDG[:, j * 4 + 2: j * 4 + 4], psB[:, 0:2]),
                   "vo", f"{P}.ev")

    @b4d.vector
    def _(v):
        w = vchain(v, E)
        for j in range(NE):
            E.wait(v, f"p4.tg{j}.ev")
        for c in range(16):
            base = (c // 2) * 4 + (c % 2) * 2
            tidc = TIDG[:, base: base + 1]
            gc = TIDG[:, base + 1: base + 2]
            v.tensor_copy(XIDX[:, c:c + 1], tidc)
            w(v.tensor_scalar(ROUT[:, 0:1], gc, 0.0, None, ALU.is_equal))
            w(v.tensor_sub(ROUT[:, 1:2], DMY[:, c:c + 1], tidc))
            w(v.tensor_mul(ROUT[:, 1:2], ROUT[:, 0:1], ROUT[:, 1:2]))
            w(v.tensor_add(ROUT[:, 2:3], tidc, ROUT[:, 1:2]))
            w(v.tensor_copy(YIDX[:, c:c + 1], ROUT[:, 2:3]),
              "p4.idx" if c == 15 else None)

    cut("p4d")
    b4e = blk()
    for c in range(16):
        @b4e.gpsimd
        def _(e, c=c):
            if c == 0:
                E.wait(e, "p4.idx")
                for tt in range(NT):
                    E.wait(e, f"p4.xsd{tt}")
            if c >= 2:
                E.wait(e, f"p4.xc{c - 2}")
            E.mark(e.indirect_dma_start(
                XGB[:, (c % 2) * 1024:(c % 2 + 1) * 1024], None,
                DR["xsd"][:, :], IOA(XIDX[:, c:c + 1], 0)),
                "gs", f"p4.xg{c}", 16)

        @b4e.vector
        def _(v, c=c):
            E.wait(v, f"p4.xg{c}")
            if c == 0:
                E.wait(v, "p4.idx")
            base = (c // 2) * 4 + (c % 2) * 2
            E.mark(v.tensor_scalar(XGB[:, (c % 2) * 1024:(c % 2 + 1) * 1024],
                                   XGB[:, (c % 2) * 1024:(c % 2 + 1) * 1024],
                                   TIDG[:, base + 1: base + 2], None, ALU.mult),
                   "vo", f"p4.xs{c}")

        @b4e.scalar
        def _(s, c=c):
            E.wait(s, f"p4.xs{c}")
            E.mark(s.dma_start(out=DR["xct"][c * 128:(c + 1) * 128, :],
                               in_=XGB[:, (c % 2) * 1024:(c % 2 + 1) * 1024]),
                   "hd", f"p4.xc{c}", 16)

    @b4e.gpsimd
    def _(e):
        for c in range(16):
            E.wait(e, f"p4.xc{c}")
        E.mark(e.collective_compute("AllGather", ALU.bypass,
                                    replica_groups=[[0, 1, 2, 3, 4, 5, 6, 7]],
                                    ins=[DR["xct"].tensor.ap()],
                                    outs=[DR["xctg"].tensor.ap()]), "cc", "p4.xag")

    cut("p4e")
    mo1.close()
    mh2p.close()
    mo3 = ExitStack()
    _stacks.append(mo3)
    XET = mo3.enter_context(nc.sbuf_tensor("XET", [128, 16384], F16))
    W1S = mo3.enter_context(nc.sbuf_tensor("W1S", [128, 16384], F16))
    W2S = mo3.enter_context(nc.sbuf_tensor("W2S", [128, 16384], F16))
    HTB = mo3.enter_context(nc.sbuf_tensor("HTB", [128, 16384], F16))
    YCB = mo3.enter_context(nc.sbuf_tensor("YCB", [128, 2048], F16))
    YB = mo3.enter_context(nc.sbuf_tensor("YB", [128, 2048], F16))
    YBF = mo3.enter_context(nc.sbuf_tensor("YBF", [128, 2048], F32))

    b4f = blk()

    @b4f.gpsimd
    def _(e):
        E.mark(e.dma_start(out=W1S[:, :].rearrange("p (a f) -> p a f", a=8),
                           in_=IN["w1c"].rearrange("(a p) f -> p a f", p=128)),
               "gd", "p4.w1", 16)
        E.mark(e.dma_start(out=W2S[:, :].rearrange("p (a f) -> p a f", a=16),
                           in_=IN["w2c"].rearrange("(a p) f -> p a f", p=128)),
               "gd", "p4.w2", 16)

    @b4f.vector
    def _(v):
        E.wait(v, "p4.w1")
        last = None
        for dc in range(8):
            last = v.tensor_scalar(W1S[:, dc * 2048:(dc + 1) * 2048],
                                   W1S[:, dc * 2048:(dc + 1) * 2048],
                                   CF["g2c"][:, dc:dc + 1], None, ALU.mult)
        E.mark(last, "vo", "p4.w1f")

    @b4f.scalar
    def _(s):
        E.wait(s, "p4.xag")
        rk = s.partition_id() % 8
        snap = dict(E.n)
        ends = []
        for g8 in nc.Switch(engines=[s], index=[rk], n=8):
            E.switch_body_begin(snap)
            for r in range(8):
                for dc in range(8):
                    nm = "p4.xet" if (r == 7 and dc == 7) else None
                    E.mark(s.dma_start(
                        out=XET[:, dc * 2048 + r * 256: dc * 2048 + (r + 1) * 256],
                        in_=DR["xctg"][r * 2048 + g8 * 256:
                                       r * 2048 + (g8 + 1) * 256,
                                       dc * 128:(dc + 1) * 128],
                        transpose=True), "hd", nm, 16)
            ends.append(dict(E.n))
        E.switch_end(ends)

    for Jh in range(2):
        for fc in range(16):
            @b4f.tensor
            def _(t, Jh=Jh, fc=fc):
                if Jh == 0 and fc == 0:
                    E.wait(t, "p4.xet")
                    E.wait(t, "p4.w1f")
                if Jh == 1 and fc == 0:
                    E.wait(t, "p4.y0.7")
                if fc >= 2:
                    E.wait(t, f"p4.r{Jh}.{fc - 2}")
                elif Jh == 1:
                    E.wait(t, f"p4.r0.{14 + fc}")
                last = None
                for dc in range(8):
                    for hv in range(2):
                        last = t.matmul(
                            psA[:, (fc % 2) * 1024 + hv * 512:
                                (fc % 2) * 1024 + (hv + 1) * 512],
                            W1S[:, dc * 2048 + fc * 128: dc * 2048 + (fc + 1) * 128],
                            XET[:, dc * 2048 + Jh * 1024 + hv * 512:
                                dc * 2048 + Jh * 1024 + (hv + 1) * 512],
                            start=(dc == 0), stop=(dc == 7))
                E.mark(last, "tm", f"p4.h{Jh}.{fc}")

            @b4f.scalar
            def _(s, Jh=Jh, fc=fc):
                E.wait(s, f"p4.h{Jh}.{fc}")
                E.mark(s.activation(HTB[:, fc * 1024:(fc + 1) * 1024],
                                    psA[:, (fc % 2) * 1024:(fc % 2 + 1) * 1024],
                                    AF.Relu), "so", f"p4.r{Jh}.{fc}")

        for Js in range(8):
            @b4f.tensor
            def _(t, Jh=Jh, Js=Js):
                if Js == 0:
                    E.wait(t, f"p4.r{Jh}.15")
                    if Jh == 0:
                        E.wait(t, "p4.w2")
                if Jh > 0 or Js > 0:
                    pj, ps_ = (Jh, Js - 1) if Js > 0 else (0, 7)
                    E.wait(t, f"p4.ye{pj}.{ps_}")
                last = None
                for fc in range(16):
                    for hv in range(2):
                        last = t.matmul(
                            psB[:, hv * 512:(hv + 1) * 512],
                            HTB[:, fc * 1024 + Js * 128: fc * 1024 + (Js + 1) * 128],
                            W2S[:, fc * 1024 + hv * 512: fc * 1024 + (hv + 1) * 512],
                            start=(fc == 0), stop=(fc == 15))
                E.mark(last, "tm", f"p4.y{Jh}.{Js}")

            @b4f.vector
            def _(v, Jh=Jh, Js=Js):
                E.wait(v, f"p4.y{Jh}.{Js}")
                c = Jh * 8 + Js
                if c >= 2:
                    E.wait(v, f"p4.yc{c - 2}")
                E.mark(v.tensor_copy(YCB[:, (Js % 2) * 1024:(Js % 2 + 1) * 1024],
                                     psB[:, 0:1024]), "vo", f"p4.ye{Jh}.{Js}")

            @b4f.sync
            def _(e, Jh=Jh, Js=Js):
                E.wait(e, f"p4.ye{Jh}.{Js}")
                c = Jh * 8 + Js
                E.mark(e.dma_start(out=DR["ycd"][c * 128:(c + 1) * 128, :],
                                   in_=YCB[:, (Js % 2) * 1024:(Js % 2 + 1) * 1024]),
                       "sd", f"p4.yc{c}", 16)

    @b4f.gpsimd
    def _(e):
        for c in range(16):
            E.wait(e, f"p4.yc{c}")
        E.mark(e.collective_compute("AllGather", ALU.bypass,
                                    replica_groups=[[0, 1, 2, 3, 4, 5, 6, 7]],
                                    ins=[DR["ycd"].tensor.ap()],
                                    outs=[DR["ycg"].tensor.ap()]), "cc", "p4.yag")

    cut("p4f")
    for c in range(16):
        @b4f.gpsimd
        def _(e, c=c):
            if c == 0:
                E.wait(e, "p4.yag")
                E.wait(e, "p4.yrw")
            if c >= 2:
                E.wait(e, f"p4.ybf{c - 2}")
            E.mark(e.indirect_dma_start(
                YB[:, (c % 2) * 1024:(c % 2 + 1) * 1024], None,
                DR["ycg"][:, :], IOA(YRW[:, c:c + 1], 0)),
                "gs", f"p4.yg{c}", 16)

        @b4f.vector
        def _(v, c=c):
            E.wait(v, f"p4.yg{c}")
            if c >= 2:
                E.wait(v, f"p4.ysc{c - 2}")
            E.mark(v.tensor_copy(YBF[:, (c % 2) * 1024:(c % 2 + 1) * 1024],
                                 YB[:, (c % 2) * 1024:(c % 2 + 1) * 1024]),
                   "vo", f"p4.ybf{c}")

        @b4f.gpsimd
        def _(e, c=c):
            E.wait(e, f"p4.ybf{c}")
            if c == 0:
                E.wait(e, "p4.yz")
            else:
                E.wait(e, f"p4.ysc{c - 1}")
            E.mark(e.indirect_dma_start(
                DR["ystage"][:, :], IOA(YIDX[:, c:c + 1], 0),
                YBF[:, (c % 2) * 1024:(c % 2 + 1) * 1024], None,
                oob_is_err=True, compute_op=ALU.add),
                "gs", f"p4.ysc{c}", 16)

    cut("p4g")
    mo3.close()
    mo4 = ExitStack()
    _stacks.append(mo4)
    F1 = mo4.enter_context(nc.sbuf_tensor("F1", [128, 2048], F32))
    OSB = mo4.enter_context(nc.sbuf_tensor("OSB", [128, 2048], F32))
    JNK2 = mo4.enter_context(nc.sbuf_tensor("JNK2", [128, 1024], F32))

    b5 = blk()
    for tt in range(NT):
        P = f"p5.{tt}"
        buf = tt % 2
        fsl = F1[:, buf * 1024:(buf + 1) * 1024]
        osl = OSB[:, buf * 1024:(buf + 1) * 1024]
        base = 592 + buf * 16

        @b5.sync
        def _(e, tt=tt, P=P, fsl=fsl):
            if tt == 0:
                E.wait(e, "p4.ysc15")
            if tt >= 2:
                E.wait(e, f"p5.{tt - 2}.x")
            E.mark(e.dma_start(out=fsl, in_=X1F[tt * 128:(tt + 1) * 128, :]),
                   "sd", f"{P}.l0", 16)

        @b5.gpsimd
        def _(e, tt=tt, P=P, fsl=fsl):
            E.wait(e, f"{P}.l0")
            E.mark(e.dma_start(out=fsl, in_=DR["PREd"][tt * 128:(tt + 1) * 128, :],
                               accum_op=ALU.add), "gd", f"{P}.l1", 16)
            E.wait(e, f"{P}.l1")
            E.mark(e.dma_start(out=fsl, in_=DR["ystage"][tt * 128:(tt + 1) * 128, :],
                               accum_op=ALU.add), "gd", f"{P}.r", 16)

        @b5.vector
        def _(v, tt=tt, P=P, fsl=fsl, base=base):
            E.wait(v, f"{P}.r")
            E.mark(v.reduce_sum(SMS[:, base:base + 1],
                                fsl.rearrange("p (x a) -> p x a", x=1), AX.X),
                   "vo", f"{P}.sum")

        @b5.scalar
        def _(s, tt=tt, P=P, fsl=fsl, base=base):
            E.wait(s, f"{P}.sum")
            E.mark(s.activation(JNK2[:, :], fsl, AF.Square,
                                accum_out=SMS[:, base + 1: base + 2]),
                   "so", f"{P}.ssq")

        @b5.vector
        def _(v, tt=tt, P=P, base=base):
            w = vchain(v, E)
            E.wait(v, f"{P}.ssq")
            v.tensor_scalar(SMS[:, base + 2: base + 3], SMS[:, base: base + 1],
                            1.0 / 1024.0, None, ALU.mult)
            w(v.tensor_scalar(SMS[:, base + 3: base + 4],
                              SMS[:, base + 1: base + 2], 1.0 / 1024.0, None,
                              ALU.mult))
            w(v.tensor_mul(SMS[:, base + 4: base + 5], SMS[:, base + 2: base + 3],
                           SMS[:, base + 2: base + 3]))
            w(v.tensor_sub(SMS[:, base + 3: base + 4], SMS[:, base + 3: base + 4],
                           SMS[:, base + 4: base + 5]))
            w(v.tensor_scalar(SMS[:, base + 3: base + 4],
                              SMS[:, base + 3: base + 4], 1.0, 1e-5,
                              ALU.mult, ALU.add), f"{P}.var")

        @b5.scalar
        def _(s, tt=tt, P=P, base=base):
            E.wait(s, f"{P}.var")
            E.mark(s.activation(SMS[:, base + 5: base + 6],
                                SMS[:, base + 3: base + 4], AF.Sqrt),
                   "so", f"{P}.sdv")

        @b5.vector
        def _(v, tt=tt, P=P, fsl=fsl, osl=osl, base=base):
            E.wait(v, f"{P}.sdv")
            rc = E.mark(v.reciprocal(SMS[:, base + 6: base + 7],
                                     SMS[:, base + 5: base + 6]), "vo")
            v.wait_ge(E.h["vo"], rc)
            if tt >= 2:
                E.wait(v, f"p5.{tt - 2}.o")
            E.mark(v.tensor_scalar(osl, fsl, SMS[:, base + 2: base + 3],
                                   SMS[:, base + 6: base + 7],
                                   ALU.subtract, ALU.mult), "vo", f"{P}.x")

        @b5.sync
        def _(e, tt=tt, P=P, osl=osl):
            E.wait(e, f"{P}.x")
            E.mark(e.dma_start(out=OUT[tt * 128:(tt + 1) * 128, :], in_=osl),
                   "sd", f"{P}.o", 16)

    bfin = blk()

    @bfin.sync
    def _(e):
        for snm in E.n:
            if snm != "sd":
                e.wait_ge(E.h[snm], E.n[snm])
        e.wait_ge(E.h["sd"], E.n["sd"])

    if _cur_blk[0] is not None:
        _cur_blk[0].__exit__(None, None, None)
        _cur_blk[0] = None
    mo4.close()
    mo2.close()
    st.close()
    return nc


def build_program():
    try:
        return _build_program_inner(STOP_AFTER)
    except _StopEmission as e:
        return e.args[0]


def kernel(**inputs):
    in_maps = make_in_maps(inputs)
    nc = build_program()
    declared = set(nc._lazy_in.declared)
    in_maps = [{k: v for k, v in m.items() if k in declared} for m in in_maps]
    res = run_bass_kernel_spmd(nc, in_maps, core_ids=list(range(8)))
    return assemble_output(res.results)


if __name__ == "__main__":
    build_program()
    print("built OK")


# version 03
# speedup vs baseline: 1.1458x; 1.1458x over previous
from contextlib import ExitStack

import numpy as np
import concourse.bass as bass
import concourse.mybir as mybir
from concourse.bass_utils import run_bass_kernel_spmd

F32 = mybir.dt.float32
BF16 = mybir.dt.bfloat16
F16 = mybir.dt.float16
U32 = mybir.dt.uint32
IOA = bass.IndirectOffsetOnAxis
FP16 = mybir.dt.float16
I16 = mybir.dt.int16
I32 = mybir.dt.int32
AF = mybir.ActivationFunctionType
ALU = mybir.AluOpType
AX = mybir.AxisListType

D = 1024
TC = 2048
NT = 16
HA, DK, MA = 4, 256, 64
HM, HD = 16, 64
FF = 2048
NE, SLOTS = 8, 256
CAP = 1280
SCALE_A = float(DK) ** -0.5

DEBUG_STAGE = 3
RACE_DETECT = True
PH0_MIN = 0
PROBE_SWITCH = None
DUMP = None
DUMPROWS = 0
STOP_AFTER = None


class _StopEmission(Exception):
    pass


def _chunks_of(core):
    g = core % 4
    return (g, 7 - g)


def _host_constants(core):
    cA, cB = _chunks_of(core)
    pos = np.concatenate([np.arange(cA * 1024, (cA + 1) * 1024),
                          np.arange(cB * 1024, (cB + 1) * 1024)])
    half = DK // 2
    inv = 1.0 / 10000.0 ** (np.arange(half, dtype=np.float64) / half)
    ang = pos[None, :] * inv[:, None]
    cosA = np.cos(ang).astype(np.float32)
    sinA = np.sin(ang).astype(np.float32)
    theta = 10000.0 ** (-2.0 * (np.arange(HD // 2, dtype=np.float64) / HD))
    ang2 = pos[None, :] * theta[:, None]
    mc_, ms_ = np.cos(ang2), np.sin(ang2)
    mcos = np.zeros((HD, TC), np.float32)
    msin = np.zeros((HD, TC), np.float32)
    mcos[0::2] = mc_
    mcos[1::2] = mc_
    msin[0::2] = -ms_
    msin[1::2] = ms_
    mcos2 = np.tile(mcos, (2, 1))
    msin2 = np.tile(msin, (2, 1))
    ii, jj = np.indices((128, 128))
    tri = (ii <= jj).astype(np.float32)
    ustrict = (ii < jj).astype(np.float32)
    negm = np.where(ii > jj, -1e30, 0.0).astype(np.float32)
    ident = np.eye(128, dtype=np.float32)
    perm = np.zeros((128, 128), np.float32)
    for i in range(64):
        perm[2 * i, 2 * i + 1] = 1.0
        perm[2 * i + 1, 2 * i] = 1.0
    iota256 = np.tile(np.arange(256, dtype=np.float32)[None, :], (128, 1))
    idp = np.arange(1, TC + 1, dtype=np.int64).reshape(NT, 128).T
    hib = (idp >> 8).astype(np.float32)
    lob = (idp & 255).astype(np.float32)
    konst = np.tile(np.array([[1e-5, 1.0 / 256.0, 0.0, 0.0]], np.float32), (128, 1))
    onesrow = np.ones((1, 128), np.float32)
    onescol = np.ones((128, 1), np.float32)
    maskA = np.zeros((128, 8), np.float32)
    maskAB = np.zeros((128, 8), np.float32)
    maskA[:, :cA] = 1.0
    maskAB[:, cA + 1:cB] = 1.0
    iotah = np.tile(np.arange(256, dtype=np.float16)[None, :], (128, 1))
    tloch = (np.arange(16)[None, :] * 128 + np.arange(128)[:, None]).astype(np.float16)
    dumy = (2048.0 + (np.arange(16)[None, :] % 2) * 128
            + np.arange(128)[:, None]).astype(np.float32)
    moemask = np.zeros((128, 1024), np.float32)
    moemask.reshape(128, 128, 8)[:, :, core] = 1.0
    ee = np.arange(16)[None, :] // 2
    yrows = (ee * 2048 + core * 256 + (np.arange(16)[None, :] % 2) * 128
             + np.arange(128)[:, None]).astype(np.uint32)
    pat8 = np.zeros((8, 128), np.float32)
    pat8.reshape(8, 16, 8)[np.arange(8), :, np.arange(8)] = 1.0
    return dict(iotah=iotah, tloch=tloch, dumy=dumy, moemask=moemask,
                yrows=yrows, pat8=pat8,
                cosA=cosA, sinA=sinA, mcos=mcos, msin=msin, tri=tri,
                ustrict=ustrict, negm=negm, ident=ident, perm=perm,
                iota256=iota256, hib=hib, lob=lob, onesrow=onesrow,
                onescol=onescol, maskA=maskA, maskAB=maskAB, konst=konst,
                mcos2=mcos2, msin2=msin2)


def make_in_maps(inputs):
    x = np.asarray(inputs["L0_attn_hidden_states"], np.float32)
    maps = []
    for core in range(8):
        b = core // 4
        cA, cB = _chunks_of(core)
        xt = np.concatenate([x[b, cA * 1024:(cA + 1) * 1024],
                             x[b, cB * 1024:(cB + 1) * 1024]], axis=0)
        m = {"xt": np.ascontiguousarray(xt), "xT": np.ascontiguousarray(xt.T)}
        ren = {"wq": "L0_attn_Wq", "wk": "L0_attn_Wk", "wv": "L0_attn_Wv",
               "wg": "L0_attn_Wg", "wo": "L0_attn_Wo", "skw1": "L0_attn_sk_w1",
               "skw2": "L0_attn_sk_w2", "svw1": "L0_attn_sv_w1",
               "svw2": "L0_attn_sv_w2", "wqkv": "L0_ffn_w_qkv",
               "wo2": "L0_ffn_w_o", "wrout": "L0_ffn_w_router",
               "w1": "L0_ffn_w1", "w2": "L0_ffn_w2"}
        for k, v in ren.items():
            m[k] = np.asarray(inputs[v], np.float32)
        m["skb2"] = np.asarray(inputs["L0_attn_sk_b2"], np.float32).reshape(1, 256)
        m["svb2"] = np.asarray(inputs["L0_attn_sv_b2"], np.float32).reshape(1, 256)
        m["gnwc"] = np.ascontiguousarray(np.asarray(inputs["L0_attn_gnorm_w"], np.float32).reshape(2, 128).T)
        m["g1c"] = np.ascontiguousarray(np.asarray(inputs["L0_ffn_g1"], np.float32).reshape(8, 128).T)
        m["g2c"] = np.ascontiguousarray(np.asarray(inputs["L0_ffn_g2"], np.float32).reshape(8, 128).T)
        m["w1c"] = np.ascontiguousarray(np.asarray(inputs["L0_ffn_w1"], np.float32)[core])
        m["w2c"] = np.ascontiguousarray(np.asarray(inputs["L0_ffn_w2"], np.float32)[core])
        for k, v in _host_constants(core).items():
            m["c_" + k] = v
        maps.append(m)
    return maps


def assemble_output(results):
    out = np.zeros((2, 8192, D), np.float32)
    for core in range(8):
        b = core // 4
        cA, cB = _chunks_of(core)
        o = np.asarray(results[core]["out"], np.float32)
        out[b, cA * 1024:(cA + 1) * 1024] = o[:1024]
        out[b, cB * 1024:(cB + 1) * 1024] = o[1024:]
    return out


class Ev:

    def __init__(self, nc, stack):
        names = ["sd", "hd", "gd", "tm", "vo", "so", "go", "cc", "zd", "fa", "gs"]
        self.h = {n: stack.enter_context(nc.semaphore(n)) for n in names}
        self.n = {n: 0 for n in names}
        self.ev = {}
        self.pre = set()

    def declare(self, name, sem, val):
        assert name not in self.ev, f"declare of existing event {name}"
        self.ev[name] = (sem, val)
        self.pre.add(name)

    def mark(self, inst, sem, name=None, k=1):
        inst.then_inc(self.h[sem], k)
        self.n[sem] += k
        if name is not None:
            if name in self.ev:
                assert getattr(self, "in_switch", False) or name in self.pre, \
                    f"duplicate event {name}"
                assert self.ev[name] == (sem, self.n[sem]), \
                    f"event value mismatch {name}: {self.ev[name]} vs {(sem, self.n[sem])}"
            self.ev[name] = (sem, self.n[sem])
        return self.n[sem]

    def switch_body_begin(self, snap):
        self.in_switch = True
        self.n = dict(snap)

    def switch_end(self, end_counts):
        assert all(e == end_counts[0] for e in end_counts), end_counts
        self.in_switch = False
        self.n = dict(end_counts[0])

    def wait(self, eng, name):
        sem, val = self.ev[name]
        eng.wait_ge(self.h[sem], val)

    def wait_all(self, eng, *names):
        for nm in names:
            self.wait(eng, nm)


def vchain(v, E):
    def w(inst, name=None):
        val = E.mark(inst, "vo", name)
        v.wait_ge(E.h["vo"], val)
        return inst
    return w


def mm_bank_group(t, mms):
    first = {}
    last = {}
    for i, (_, _, _, bk) in enumerate(mms):
        if bk not in first:
            first[bk] = i
        last[bk] = i
    inst = None
    for i, (o, l, r, bk) in enumerate(mms):
        inst = t.matmul(o, l, r, start=(first[bk] == i), stop=(last[bk] == i))
    return inst


def _build_program_inner(cut_point):
    nc = bass.Bass("TRN2", target_bir_lowering=False,
                   detect_race_conditions=RACE_DETECT)
    nc.cache_partition_id()
    st = ExitStack()

    def din(name, shape, dtype=F32):
        return nc.dram_tensor(name, list(shape), dtype, kind="ExternalInput").ap()

    class LazyIn:
        def __init__(self):
            self.shapes = {}
            self.dtypes = {}
            self.declared = {}

        def __getitem__(self, nm):
            if nm not in self.declared:
                self.declared[nm] = din(nm, self.shapes[nm],
                                        self.dtypes.get(nm, F32))
            return self.declared[nm]

    IN = LazyIn()
    for nm, shp in [
        ("xt", (TC, D)), ("xT", (D, TC)),
        ("wq", (D, D)), ("wk", (D, D)), ("wv", (D, D)), ("wg", (D, D)), ("wo", (D, D)),
        ("skw1", (D, 16)), ("skw2", (16, 256)), ("skb2", (1, 256)),
        ("svw1", (D, 16)), ("svw2", (16, 256)), ("svb2", (1, 256)),
        ("gnwc", (128, 2)), ("g1c", (128, 8)), ("wqkv", (D, 3 * D)), ("wo2", (D, D)),
        ("g2c", (128, 8)), ("wrout", (D, NE)), ("w1", (NE, D, FF)), ("w2", (NE, FF, D)),
        ("c_cosA", (128, TC)), ("c_sinA", (128, TC)),
        ("c_mcos", (HD, TC)), ("c_msin", (HD, TC)),
        ("c_tri", (128, 128)), ("c_ustrict", (128, 128)), ("c_negm", (128, 128)),
        ("c_ident", (128, 128)), ("c_perm", (128, 128)),
        ("c_iota256", (128, 256)), ("c_hib", (128, NT)), ("c_lob", (128, NT)),
        ("c_onesrow", (1, 128)), ("c_onescol", (128, 1)),
        ("c_maskA", (128, 8)), ("c_maskAB", (128, 8)), ("c_konst", (128, 4)),
        ("c_mcos2", (128, TC)), ("c_msin2", (128, TC)),
        ("c_iotah", (128, 256)), ("c_tloch", (128, 16)), ("c_dumy", (128, 16)),
        ("c_moemask", (128, 1024)), ("c_yrows", (128, 16)), ("c_pat8", (8, 128)),
        ("w1c", (D, FF)), ("w2c", (FF, D)),
    ]:
        IN.shapes[nm] = shp
    IN.dtypes["c_iotah"] = F16
    IN.dtypes["c_tloch"] = F16
    IN.dtypes["c_yrows"] = U32
    nc._lazy_in = IN

    OUT = nc.dram_tensor("out", [TC, D], F32, kind="ExternalOutput").ap()

    def dint(name, shape, dtype=F32):
        return nc.dram_tensor(name, list(shape), dtype, kind="Internal").ap()

    def dsh(name, shape, dtype=F32):
        return nc.dram_tensor(name, list(shape), dtype, kind="Internal",
                              addr_space="Shared").ap()

    DR = {} if PH0_MIN >= 1 else {
        "st_in": dint("st_in", (256, 2048)),
        "st_out": dint("st_out", (1024, 2048)),
        "vtimed": dint("vtimed", (TC, D), BF16),
        "ofinb": dint("ofinb", (TC, D), BF16),
        "x1b": dint("x1b", (TC, D), BF16),
        "gsild": dint("gsild", (TC, D), BF16),
        "qmhad": dint("qmhad", (D, TC), BF16),
        "kv_in": dint("kv_in", (128, 33024), BF16),
        "kv_out": dint("kv_out", (512, 33024), BF16),
        "ybT": dint("ybT", (D, TC), BF16),
        "PREd": dint("PREd", (TC, D)),
        "xsd": dint("xsd", (TC, D), F16),
        "gdram": dint("gdram", (TC, NE)),
        "gag": dsh("gag", (8 * TC, NE)),
        "taud": dint("taud", (1, 1)),
        "tau8d": dsh("tau8d", (8, 1)),
        "xct": dint("xct", (TC, D), F16),
        "xctg": dsh("xctg", (8 * TC, D), F16),
        "ycd": dint("ycd", (TC, D), F16),
        "ycg": dsh("ycg", (8 * TC, D), F16),
        "ystage": dint("ystage", (4352, D)),
    }

    E = Ev(nc, st)
    SEM = E.h

    CST = st.enter_context(nc.sbuf_tensor("CST", [128, 3200], BF16))
    CV = {
        "tri": CST[:, 0:128], "negm": CST[:, 128:256],
        "perm": CST[:, 256:384], "iota256": CST[:, 384:640],
        "identb": CST[:, 640:768],
        "mcos": CST[0:64, 768:2816], "msin": CST[64:128, 768:2816],
        "hib": CST[:, 2816:2832], "lob": CST[:, 2832:2848],
        "onesrowb": CST[0:1, 2848:2976], "onescolb": CST[:, 2976:2977],
    }
    CSTF = st.enter_context(nc.sbuf_tensor("CSTF", [128, 1312], F32))
    CF = {
        "ident": CSTF[:, 0:128], "ustrict": CSTF[:, 128:256],
        "onesrow": CSTF[0:1, 256:384],
        "gnwc": CSTF[:, 384:386], "g1c": CSTF[:, 386:394], "g2c": CSTF[:, 394:402],
        "skb2": CSTF[0:1, 402:658], "svb2": CSTF[0:1, 658:914],
        "trif": CSTF[:, 914:1042], "onescol": CSTF[:, 1042:1043],
        "maskA": CSTF[:, 1044:1052], "maskAB": CSTF[:, 1052:1060],
        "konst": CSTF[:, 1060:1064],
    }
    nc.const_aps.register(F32, 1e-5, CF["konst"][:, 0:1])
    nc.const_aps.register(F32, 1.0 / 256.0, CF["konst"][:, 1:2])
    SMS = st.enter_context(nc.sbuf_tensor("SMS", [128, 1024], F32))
    STATE = st.enter_context(nc.sbuf_tensor("STATE", [128, 2048], F32))
    STATEB = st.enter_context(nc.sbuf_tensor("STATEB", [128, 1536], BF16))

    psA = st.enter_context(nc.psum_tensor("psA", [128, 2048], F32))
    psB = st.enter_context(nc.psum_tensor("psB", [128, 1024], F32))
    psC = st.enter_context(nc.psum_tensor("psC", [128, 1024], BF16))
    psD = st.enter_context(nc.psum_tensor("psD", [128, 512], F32))

    _cur_blk = [None]
    _stacks = []

    def blk():
        if _cur_blk[0] is not None:
            _cur_blk[0].__exit__(None, None, None)
        bb = nc.Block()
        bb.__enter__()
        _cur_blk[0] = bb
        return bb

    def finish_and_stop():
        bf = blk()

        if DUMP is not None and DUMP.startswith("dr:"):
            dd = DR[DUMP[3:]]

            @bf.gpsimd
            def _(e):
                for snm in E.n:
                    if snm != "gd":
                        e.wait_ge(E.h[snm], E.n[snm])
                e.wait_ge(E.h["gd"], E.n["gd"])
                nr = DUMPROWS if DUMPROWS else dd.shape[0]
                E.mark(e.dma_start(out=OUT[0:nr, :], in_=dd[0:nr, :]), "gd", None, 16)
                e.wait_ge(E.h["gd"], E.n["gd"])
        elif DUMP is not None:
            dump_buf = nc._dumpables[DUMP]

            @bf.gpsimd
            def _(e):
                for snm in E.n:
                    if snm != "gd":
                        e.wait_ge(E.h[snm], E.n[snm])
                e.wait_ge(E.h["gd"], E.n["gd"])
                np_, nf = dump_buf.shape[0], min(dump_buf.shape[1], 16384)
                if nf >= 1024:
                    na = nf // 1024
                    E.mark(e.dma_start(
                        out=OUT[:, :].rearrange("(a p) d -> p a d", p=128)[0:np_, 0:na, :],
                        in_=dump_buf[0:np_, 0:na * 1024].rearrange("p (a d) -> p a d", a=na)),
                        "gd", None, 16)
                else:
                    E.mark(e.dma_start(out=OUT[0:np_, 0:nf],
                                       in_=dump_buf[0:np_, 0:nf]),
                           "gd", None, 16)
                e.wait_ge(E.h["gd"], E.n["gd"])

        @bf.sync
        def _(e):
            for snm in E.n:
                if snm != "sd" and not (PH0_MIN >= 2 and E.n[snm] == 0):
                    e.wait_ge(E.h[snm], E.n[snm])
            if DUMP is None:
                for tt in range(NT):
                    E.mark(e.dma_start(out=OUT[tt * 128:(tt + 1) * 128, :],
                                       in_=SMS[:, 0:1024]), "sd", None, 16)
            e.wait_ge(E.h["sd"], E.n["sd"])
        if _cur_blk[0] is not None:
            _cur_blk[0].__exit__(None, None, None)
            _cur_blk[0] = None
        for stk in reversed(_stacks):
            stk.close()
        st.close()
        raise _StopEmission(nc)

    def cut(name):
        if PROBE_SWITCH == name:
            bps = blk()

            @bps.tensor
            def _(t):
                rk = t.partition_id() % 4
                for g in nc.Switch(engines=[t], index=[rk], n=4):
                    t.matmul(psA[:, 0:128], CV["identb"], CV["negm"],
                             start=True, stop=True)
            print(f"PROBE_SWITCH at {name}: build OK")
            raise _StopEmission(nc)
        if cut_point == name:
            finish_and_stop()

    att = ExitStack()
    _stacks.append(att)
    QT = att.enter_context(nc.sbuf_tensor("QT", [128, 16384], BF16))
    KT = att.enter_context(nc.sbuf_tensor("KT", [128, 16384], BF16))
    WB = att.enter_context(nc.sbuf_tensor("WB", [128, 8192], BF16))
    WKV = att.enter_context(nc.sbuf_tensor("WKV", [128, 8192], BF16))
    SVT = att.enter_context(nc.sbuf_tensor("SVT", [64, 8192], BF16))
    VS = att.enter_context(nc.sbuf_tensor("VS", [128, 2048], BF16))
    SSC = att.enter_context(nc.sbuf_tensor("SSC", [128, 4864], BF16))
    _dumpables = {}
    nc._dumpables = _dumpables
    att2 = ExitStack()
    _stacks.append(att2)
    ROPET = att2.enter_context(nc.sbuf_tensor("ROPET", [128, 4096], BF16))
    XA = att2.enter_context(nc.sbuf_tensor("XA", [128, 16384], BF16))
    S1S = att2.enter_context(nc.sbuf_tensor("S1S", [16, 4096], BF16))

    b = blk()

    @b.sync
    def _(e):
        last = None
        for src, dst in [
            ("c_ident", CF["ident"]), ("c_ustrict", CF["ustrict"]),
            ("c_onesrow", CF["onesrow"]), ("skb2", CF["skb2"]),
            ("svb2", CF["svb2"]), ("c_tri", CF["trif"]),
            ("c_onescol", CF["onescol"]), ("c_maskA", CF["maskA"]),
            ("c_maskAB", CF["maskAB"]), ("c_konst", CF["konst"]),
        ]:
            last = e.dma_start(out=dst, in_=IN[src][:, :])
            E.mark(last, "sd", None, 16)
        E.mark(e.dma_start(out=CF["gnwc"], in_=IN["gnwc"][:, :]), "sd", None, 16)
        E.mark(e.dma_start(out=CF["g1c"], in_=IN["g1c"][:, :]), "sd", None, 16)
        E.mark(e.dma_start(out=CF["g2c"], in_=IN["g2c"][:, :]), "sd", "constsF", 16)

    @b.gpsimd
    def _(e):
        for src, dst in [
            ("c_cosA", ROPET[:, 0:2048]), ("c_sinA", ROPET[:, 2048:4096]),
            ("c_tri", CV["tri"]), ("c_negm", CV["negm"]), ("c_perm", CV["perm"]),
            ("c_iota256", CV["iota256"]), ("c_ident", CV["identb"]),
            ("c_mcos", CV["mcos"]), ("c_msin", CV["msin"]),
            ("c_hib", CV["hib"]), ("c_lob", CV["lob"]),
            ("c_onesrow", CV["onesrowb"]), ("c_onescol", CV["onescolb"]),
        ]:
            E.mark(e.dma_start(out=dst, in_=IN[src][:, :]), "gd", None, 16)
        E.mark(e.dma_start(out=XA[:, :].rearrange("p (a t) -> p a t", a=8),
                           in_=IN["xT"].rearrange("(a p) t -> p a t", p=128)),
               "gd", "xa", 16)

    cut("ph0")
    def load_w(tag, dram_ap, wait_evs=()):
        bw = blk()

        @bw.gpsimd
        def _(e):
            for evn in wait_evs:
                E.wait(e, evn)
            E.mark(e.dma_start(out=WB[:, 0:8192].rearrange("p (a o) -> p a o", a=8),
                               in_=dram_ap.rearrange("(a p) o -> p a o", p=128)),
                   "gd", f"w.{tag}", 16)
        return f"w.{tag}"

    def proj_dim(tag, dst_sb, w_ev, extra_evs=()):
        bq = blk()
        items = [(oc, ts) for oc in range(8) for ts in range(4)]
        for i, (oc, ts) in enumerate(items):
            @bq.tensor
            def _(t, i=i, oc=oc, ts=ts):
                if i == 0:
                    E.wait(t, w_ev)
                    for evn in extra_evs:
                        E.wait(t, evn)
                ps = psA[:, (i % 4) * 512:(i % 4 + 1) * 512]
                if i >= 4:
                    E.wait(t, f"{tag}.ev.{i - 4}")
                last = None
                for dc in range(8):
                    last = t.matmul(
                        ps,
                        WB[:, dc * 1024 + oc * 128: dc * 1024 + oc * 128 + 128],
                        XA[:, dc * 2048 + ts * 512: dc * 2048 + ts * 512 + 512],
                        start=(dc == 0), stop=(dc == 7))
                E.mark(last, "tm", f"{tag}.mm.{i}")

            @bq.vector
            def _(v, i=i, oc=oc, ts=ts):
                E.wait(v, f"{tag}.mm.{i}")
                ps = psA[:, (i % 4) * 512:(i % 4 + 1) * 512]
                E.mark(v.tensor_copy(
                    dst_sb[:, oc * 2048 + ts * 512: oc * 2048 + ts * 512 + 512], ps),
                    "vo", f"{tag}.ev.{i}")
        return f"{tag}.ev.{len(items) - 1}"

    wq_ev = load_w("wq", IN["wq"], wait_evs=("xa",))
    q_done = proj_dim("qproj", QT, wq_ev, extra_evs=("constsF", "xa"))
    wk_ev = load_w("wk", IN["wk"], wait_evs=(q_done,))
    k_done = proj_dim("kproj", KT, wk_ev)

    def rope_attn(tag, buf, ready_ev):
        br = blk()

        @br.vector
        def _(v):
            E.wait(v, ready_ev)
            last = None
            for h in range(HA):
                a = buf[:, (2 * h) * 2048:(2 * h + 1) * 2048]
                bb = buf[:, (2 * h + 1) * 2048:(2 * h + 2) * 2048]
                t1 = SSC[:, 0:2048]
                t2 = SSC[:, 2048:4096]
                v.tensor_mul(t1, a, ROPET[:, 0:2048])
                v.tensor_mul(t2, bb, ROPET[:, 2048:4096])
                v.tensor_mul(bb, bb, ROPET[:, 0:2048])
                v.tensor_mul(a, a, ROPET[:, 2048:4096])
                v.tensor_add(bb, bb, a)
                last = v.tensor_sub(a, t1, t2)
            E.mark(last, "vo", f"rope.{tag}")
        return f"rope.{tag}"

    _dumpables.update(QT=QT, KT=KT, WKV=WKV, SVT=SVT, STATE=STATE, GSILV=VS)
    cut("qk")
    ropeq_ev = rope_attn("q", QT, k_done)
    ropek_ev = rope_attn("k", KT, ropeq_ev)

    def proj_time(tag, w_ev, consumer):
        bt = blk()
        for tt in range(NT):
            @bt.tensor
            def _(t, tt=tt):
                if tt == 0:
                    E.wait(t, w_ev)
                if tt >= 2:
                    E.wait(t, f"{tag}.ev.{tt - 2}")
                ps = psB[:, :] if tt % 2 == 0 else psA[:, 1024:2048]
                last = None
                for dc in range(8):
                    for half in range(2):
                        last = t.matmul(
                            ps[:, half * 512:(half + 1) * 512],
                            XA[:, dc * 2048 + tt * 128: dc * 2048 + tt * 128 + 128],
                            WB[:, dc * 1024 + half * 512: dc * 1024 + (half + 1) * 512],
                            start=(dc == 0), stop=(dc == 7))
                E.mark(last, "tm", f"{tag}.mm.{tt}")

            if consumer == "v":
                @bt.vector
                def _(v, tt=tt):
                    E.wait(v, f"{tag}.mm.{tt}")
                    if tt >= 2:
                        E.wait(v, f"{tag}.st.{tt - 2}")
                    ps = psB[:, :] if tt % 2 == 0 else psA[:, 1024:2048]
                    E.mark(v.tensor_copy(VS[:, (tt % 2) * 1024:(tt % 2 + 1) * 1024], ps),
                           "vo", f"{tag}.ev.{tt}")

                @bt.scalar
                def _(s, tt=tt):
                    E.wait(s, f"{tag}.ev.{tt}")
                    E.mark(s.dma_start(out=DR["vtimed"][tt * 128:(tt + 1) * 128, :],
                                       in_=VS[:, (tt % 2) * 1024:(tt % 2 + 1) * 1024]),
                           "hd", f"{tag}.st.{tt}", 16)
            else:
                @bt.scalar
                def _(s, tt=tt):
                    E.wait(s, f"{tag}.mm.{tt}")
                    if tt >= 2:
                        E.wait(s, f"{tag}.st.{tt - 2}")
                    ps = psB[:, :] if tt % 2 == 0 else psA[:, 1024:2048]
                    E.mark(s.activation(VS[:, (tt % 2) * 1024:(tt % 2 + 1) * 1024],
                                        ps, AF.Silu), "so", f"{tag}.ev.{tt}")

                @bt.sync
                def _(e, tt=tt):
                    E.wait(e, f"{tag}.ev.{tt}")
                    E.mark(e.dma_start(out=DR["gsild"][tt * 128:(tt + 1) * 128, :],
                                       in_=VS[:, (tt % 2) * 1024:(tt % 2 + 1) * 1024]),
                           "sd", f"{tag}.st.{tt}", 16)
        return f"{tag}.ev.{NT - 1}", f"{tag}.st.{NT - 1}"

    cut("rope")
    wv_ev = load_w("wv", IN["wv"], wait_evs=(ropek_ev,))
    _, vt_done = proj_time("vproj", wv_ev, "v")
    wg_ev = load_w("wg", IN["wg"], wait_evs=("vproj.ev.15",))
    gs_done, gst_done = proj_time("gproj", wg_ev, "g")

    cut("ph1b")
    bw = blk()

    @bw.gpsimd
    def _(e):
        E.wait(e, gs_done)
        e1 = e.dma_start(out=WB[:, 0:128].rearrange("p (a o) -> p a o", a=8),
                         in_=IN["skw1"].rearrange("(a p) o -> p a o", p=128))
        E.mark(e1, "gd", None, 16)
        e2 = e.dma_start(out=WB[:, 128:256].rearrange("p (a o) -> p a o", a=8),
                         in_=IN["svw1"].rearrange("(a p) o -> p a o", p=128))
        E.mark(e2, "gd", None, 16)
        E.mark(e.dma_start(out=WB[0:16, 256:512], in_=IN["skw2"][:, :]), "gd", None, 16)
        E.mark(e.dma_start(out=WB[0:16, 512:768], in_=IN["svw2"][:, :]), "gd", "skw", 16)

    bs1 = blk()
    for i in range(8):
        which, ts = i // 4, i % 4

        @bs1.tensor
        def _(t, i=i, which=which, ts=ts):
            if i == 0:
                E.wait(t, "skw")
            if i >= 2:
                E.wait(t, f"s1.ev.{i - 2}")
            ps = psB[0:16, 0:512] if i % 2 == 0 else psA[0:16, 0:512]
            last = None
            for dc in range(8):
                last = t.matmul(
                    ps,
                    WB[:, which * 128 + dc * 16: which * 128 + (dc + 1) * 16],
                    XA[:, dc * 2048 + ts * 512: dc * 2048 + ts * 512 + 512],
                    start=(dc == 0), stop=(dc == 7))
            E.mark(last, "tm", f"s1.mm.{i}")

        @bs1.vector
        def _(v, i=i, which=which, ts=ts):
            E.wait(v, f"s1.mm.{i}")
            ps = psB[0:16, 0:512] if i % 2 == 0 else psA[0:16, 0:512]
            E.mark(v.tensor_copy(
                S1S[0:16, which * 2048 + ts * 512: which * 2048 + ts * 512 + 512], ps),
                "vo", f"s1.ev.{i}")

    bs2 = blk()
    for i in range(2 * NT):
        which, tt = i // NT, i % NT

        @bs2.tensor
        def _(t, i=i, which=which, tt=tt):
            if i == 0:
                E.wait(t, "s1.ev.7")
            if i >= 2:
                E.wait(t, f"s2.clip.{i - 2}")
            ps = psB[:, 0:256] if i % 2 == 0 else psA[:, 0:256]
            t.matmul(ps,
                     S1S[0:16, which * 2048 + tt * 128: which * 2048 + tt * 128 + 128],
                     WB[0:16, 256 + which * 256: 512 + which * 256],
                     start=True, stop=False)
            E.mark(t.matmul(ps, CF["onesrow"][0:1, 0:128],
                            (CF["skb2"] if which == 0 else CF["svb2"]),
                            start=False, stop=True), "tm", f"s2.mm.{i}")

        @bs2.vector
        def _(v, i=i):
            E.wait(v, f"s2.mm.{i}")
            if i >= 2:
                E.wait(v, f"s2.exp.{i - 2}")
            ps = psB[:, 0:256] if i % 2 == 0 else psA[:, 0:256]
            sl = SMS[:, (i % 2) * 256:(i % 2 + 1) * 256]
            v.tensor_scalar_min(sl, ps, 32.0)
            E.mark(v.tensor_scalar_max(sl, sl, -32.0), "vo", f"s2.clip.{i}")

        @bs2.scalar
        def _(s, i=i, which=which, tt=tt):
            E.wait(s, f"s2.clip.{i}")
            sl = SMS[:, (i % 2) * 256:(i % 2 + 1) * 256]
            E.mark(s.activation(
                WKV[:, which * 4096 + tt * 256: which * 4096 + (tt + 1) * 256],
                sl, AF.Exp), "so", f"s2.exp.{i}")

    bwt = blk()
    for bch in range(16):
        @bwt.tensor
        def _(t, bch=bch):
            if bch == 0:
                E.wait(t, f"s2.exp.{2 * NT - 1}")
            else:
                E.wait(t, f"svt.ev.{4 * bch - 1}")
            for j in range(4):
                i = 4 * bch + j
                tt, h = i // 4, i % 4
                E.mark(t.transpose(
                    psC[0:64, j * 128:(j + 1) * 128],
                    WKV[:, 4096 + tt * 256 + h * 64: 4096 + tt * 256 + (h + 1) * 64],
                    CV["identb"]), "tm", f"svt.mm.{i}")

        @bwt.vector
        def _(v, bch=bch):
            E.wait(v, f"svt.mm.{4 * bch + 3}")
            for j in range(4):
                i = 4 * bch + j
                tt, h = i // 4, i % 4
                E.mark(v.tensor_copy(
                    SVT[0:64, h * 2048 + tt * 128: h * 2048 + (tt + 1) * 128],
                    psC[0:64, j * 128:(j + 1) * 128]), "vo", f"svt.ev.{i}")

    cut("ph1c")
    att2.close()
    KTT = att.enter_context(nc.sbuf_tensor("KTT", [128, 2048], BF16))
    SF = att.enter_context(nc.sbuf_tensor("SF", [128, 2304], F32))
    CONT = att.enter_context(nc.sbuf_tensor("CONT", [128, 2048], F32))

    def emit_ktime(tag, b_, T, slot, pre_evs=()):
        @b_.tensor
        def _(t):
            for evn in pre_evs:
                E.wait(t, evn)
            for dc in range(8):
                E.mark(t.transpose(
                    psC[:, dc * 128:(dc + 1) * 128],
                    KT[:, dc * 2048 + T * 128: dc * 2048 + (T + 1) * 128],
                    CV["identb"]), "tm", f"{tag}.ktt.{T}.{dc}")

        @b_.vector
        def _(v):
            E.wait(v, f"{tag}.ktt.{T}.7")
            for dc in range(8):
                E.mark(v.tensor_copy(
                    KTT[:, slot * 1024 + dc * 128: slot * 1024 + (dc + 1) * 128],
                    psC[:, dc * 128:(dc + 1) * 128]), "vo", f"{tag}.ktc.{T}.{dc}")
        return f"{tag}.ktc.{T}.7"

    bcz = blk()

    @bcz.vector
    def _(v):
        E.mark(v.memset(CONT[:, :], 0.0), "vo", "cont0")

    for cc in range(2):
        for tt in range(8):
            T = cc * 8 + tt
            b_ = blk()

            @b_.gpsimd
            def _(e, T=T):
                if T == 0:
                    E.wait(e, "vproj.st.15")
                    E.wait(e, "gproj.st.15")
                if T >= 2:
                    E.wait(e, f"ca.bundle.{T - 2}")
                E.mark(e.dma_start(out=VS[:, (T % 2) * 1024:(T % 2 + 1) * 1024],
                                   in_=DR["vtimed"][T * 128:(T + 1) * 128, :]),
                       "gd", f"ca.v.{T}", 16)

            pre = ["svt.ev.63"] if T == 0 else [f"ca.ktc.{T - 1}.7"]
            kt_ev = emit_ktime("ca", b_, T, T % 2, pre_evs=pre)

            @b_.tensor
            def _(t, T=T, cc=cc, tt=tt):
                E.wait(t, f"ca.ktc.{T}.7")
                E.wait(t, f"ca.v.{T}")
                last = None
                for h in range(HA):
                    for dqc in range(2):
                        i = 2 * h + dqc
                        last = t.matmul(
                            psD[:, i * 64:(i + 1) * 64],
                            KTT[:, (T % 2) * 1024 + h * 256 + dqc * 128:
                                (T % 2) * 1024 + h * 256 + dqc * 128 + 128],
                            WKV[:, T * 256 + h * 64: T * 256 + (h + 1) * 64],
                            start=(tt == 0 and i == 0), stop=(tt == 7 and i == 7))
                for h in range(HA):
                    bank_first = (h % 2 == 0)
                    bank_last = (h % 2 == 1)
                    last = t.matmul(
                        psB[0:64, h * 256:(h + 1) * 256],
                        WKV[:, 4096 + T * 256 + h * 64: 4096 + T * 256 + (h + 1) * 64],
                        VS[:, (T % 2) * 1024 + h * 256:(T % 2) * 1024 + (h + 1) * 256],
                        start=(tt == 0 and bank_first), stop=(tt == 7 and bank_last))
                t.matmul(psA[0:1, 1024:1280], CV["onescolb"],
                         WKV[:, T * 256:(T + 1) * 256],
                         start=(tt == 0), stop=False)
                last = t.matmul(psA[0:1, 1280:1536], CV["onescolb"],
                                WKV[:, 4096 + T * 256: 4096 + (T + 1) * 256],
                                start=False, stop=(tt == 7))
                E.mark(last, "tm", f"ca.bundle.{T}")

        bce = blk()

        @bce.vector
        def _(v, cc=cc):
            E.wait(v, f"ca.bundle.{cc * 8 + 7}")
            E.wait(v, "cont0")
            if cc == 1:
                E.wait(v, "ca.st.0")
            v.tensor_copy(CONT[:, 0:512], psD[:, 0:512])
            v.tensor_copy(CONT[0:64, 512:1536], psB[0:64, 0:1024])
            v.tensor_copy(CONT[0:1, 1536:1792], psA[0:1, 1024:1280])
            E.mark(v.tensor_copy(CONT[0:1, 1792:2048], psA[0:1, 1280:1536]),
                   "vo", f"ca.cont.{cc}")

        @bce.scalar
        def _(s, cc=cc):
            E.wait(s, f"ca.cont.{cc}")
            E.mark(s.dma_start(out=DR["st_in"][cc * 128:(cc + 1) * 128, :],
                               in_=CONT[:, :]), "hd", f"ca.st.{cc}", 16)

    cut("ph2a")
    bag = blk()

    @bag.gpsimd
    def _(e):
        E.wait(e, "ca.st.1")
        E.mark(e.collective_compute(
            "AllGather", ALU.bypass,
            replica_groups=[[0, 1, 2, 3], [4, 5, 6, 7]],
            ins=[DR["st_in"].tensor.ap()],
            outs=[DR["st_out"].tensor.ap()],
        ), "cc", "ag.state")

    def prefix_add(tag, mask_ap, first_ev):
        for j in range(8):
            r, sl = (j, 0) if j < 4 else (7 - j, 1)
            row = 256 * r + 128 * sl
            bp = blk()

            @bp.gpsimd
            def _(e, j=j, row=row):
                if j == 0:
                    E.wait(e, first_ev)
                    if tag == "pfb":
                        E.wait(e, "w0.7.zrow.1")
                        E.wait(e, "w0.7.ptil")
                if j >= 2:
                    E.wait(e, f"{tag}.add.{j - 2}")
                E.mark(e.dma_start(out=(CONT[:, :] if j % 2 == 0 else SF[:, 0:2048]),
                                   in_=DR["st_out"][row:row + 128, :]),
                       "gd", f"{tag}.ld.{j}", 16)

            @bp.vector
            def _(v, j=j):
                E.wait(v, f"{tag}.ld.{j}")
                if tag == "pfa" and j == 0:
                    v.memset(STATE[:, :], 0.0)
                src = CONT[:, :] if j % 2 == 0 else SF[:, 0:2048]
                E.mark(v.scalar_tensor_tensor(
                    STATE[:, :], src, mask_ap[:, j:j + 1], STATE[:, :],
                    ALU.mult, ALU.add), "vo", f"{tag}.add.{j}")
        bp2 = blk()

        @bp2.vector
        def _(v):
            E.wait(v, f"{tag}.add.7")
            v.tensor_copy(STATEB[:, 0:512], STATE[:, 0:512])
            v.tensor_copy(STATEB[0:64, 512:1536], STATE[0:64, 512:1536])
            v.tensor_copy(SMS[0:1, 0:256], STATE[0:1, 1536:1792])
            E.mark(v.tensor_copy(SMS[0:1, 256:512], STATE[0:1, 1792:2048]),
                   "vo", f"{tag}.done")
        return f"{tag}.done"

    pfa_done = prefix_add("pfa", CF["maskA"], "ag.state")
    cut("pfa")

    def chunk_walk(cc, ready_ev):
        for tt in range(8):
            T = cc * 8 + tt
            P = f"w{cc}.{tt}"
            PM = f"w{cc}.{tt - 1}"
            b_ = blk()

            @b_.gpsimd
            def _(e, T=T, tt=tt, cc=cc, P=P):
                if tt == 0:
                    E.wait(e, ready_ev)
                if tt >= 2:
                    E.wait(e, f"w{cc}.{tt - 2}.oupd")
                E.mark(e.dma_start(out=VS[:, (tt % 2) * 1024:(tt % 2 + 1) * 1024],
                                   in_=DR["vtimed"][T * 128:(T + 1) * 128, :]),
                       "gd", f"{P}.v", 16)
                if tt >= 2:
                    E.wait(e, f"w{cc}.{tt - 2}.ofin")
                E.mark(e.dma_start(
                    out=SSC[:, 2816 + (tt % 2) * 1024: 2816 + (tt % 2 + 1) * 1024],
                    in_=DR["gsild"][T * 128:(T + 1) * 128, :]), "gd", f"{P}.gs", 16)

            pre = [ready_ev] if tt == 0 else [f"{PM}.ktc.7"]
            @b_.tensor
            def _(t, T=T, P=P, pre=pre, tt=tt, cc=cc):
                for evn in pre:
                    E.wait(t, evn)
                if tt > 0:
                    E.wait(t, f"w{cc}.{tt - 1}.ptc.3")
                for dc in range(8):
                    E.mark(t.transpose(
                        psC[:, dc * 128:(dc + 1) * 128],
                        KT[:, dc * 2048 + T * 128: dc * 2048 + (T + 1) * 128],
                        CV["identb"]), "tm", f"{P}.ktt.{dc}")

            @b_.vector
            def _(v, P=P, tt=tt):
                E.wait(v, f"{P}.ktt.7")
                for dc in range(8):
                    E.mark(v.tensor_copy(
                        KTT[:, (tt % 2) * 1024 + dc * 128:(tt % 2) * 1024 + (dc + 1) * 128],
                        psC[:, dc * 128:(dc + 1) * 128]), "vo", f"{P}.ktc.{dc}")

            @b_.tensor
            def _(t, T=T, P=P, tt=tt, cc=cc):
                if tt > 0:
                    E.wait(t, f"w{cc}.{tt - 1}.sevac")
                mms = []
                for h in range(HA):
                    for dqc in range(2):
                        mms.append((psA[:, h * 128:(h + 1) * 128],
                                    KT[:, (2 * h + dqc) * 2048 + T * 128:
                                       (2 * h + dqc) * 2048 + (T + 1) * 128],
                                    QT[:, (2 * h + dqc) * 2048 + T * 128:
                                       (2 * h + dqc) * 2048 + (T + 1) * 128], "b0"))
                E.mark(mm_bank_group(t, mms), "tm", f"{P}.sT")

            @b_.vector
            def _(v, P=P):
                E.wait(v, f"{P}.sT")
                last = None
                for h in range(HA):
                    last = v.scalar_tensor_tensor(
                        SSC[:, h * 128:(h + 1) * 128],
                        psA[:, h * 128:(h + 1) * 128], 1.0, CF["trif"],
                        ALU.mult, ALU.mult)
                E.mark(last, "vo", f"{P}.sevac")

            @b_.tensor
            def _(t, T=T, P=P, tt=tt, cc=cc):
                E.wait(t, f"{P}.sevac")
                if tt > 0:
                    E.wait(t, f"w{cc}.{tt - 1}.stb")
                    E.wait(t, f"w{cc}.{tt - 1}.zrow.1")
                    E.wait(t, f"w{cc}.{tt - 1}.praw")
                    E.wait(t, f"w{cc}.{tt - 1}.zcopy")
                mms = []
                for h in range(HA):
                    sl = psA[:, 512 + h * 64: 512 + (h + 1) * 64]
                    mms.append((sl, SSC[:, h * 128:(h + 1) * 128],
                                WKV[:, T * 256 + h * 64: T * 256 + (h + 1) * 64], "b1"))
                    mms.append((sl,
                                QT[:, 2 * h * 2048 + T * 128: 2 * h * 2048 + (T + 1) * 128],
                                STATEB[:, 2 * h * 64:(2 * h + 1) * 64], "b1"))
                    mms.append((sl,
                                QT[:, (2 * h + 1) * 2048 + T * 128:
                                   (2 * h + 1) * 2048 + (T + 1) * 128],
                                STATEB[:, (2 * h + 1) * 64:(2 * h + 2) * 64], "b1"))
                mms.append((psA[:, 1024:1280], CV["tri"],
                            WKV[:, T * 256:(T + 1) * 256], "b2"))
                mms.append((psA[:, 1024:1280], CF["onesrow"], SMS[0:1, 0:256], "b2"))
                mms.append((psA[:, 1280:1536], CV["tri"],
                            WKV[:, 4096 + T * 256: 4096 + (T + 1) * 256], "b2"))
                mms.append((psA[:, 1280:1536], CF["onesrow"], SMS[0:1, 256:512], "b2"))
                E.mark(mm_bank_group(t, mms), "tm", f"{P}.log")

            @b_.vector
            def _(v, P=P, tt=tt, cc=cc):
                E.wait(v, f"{P}.log")
                if tt > 0:
                    E.wait(v, f"w{cc}.{tt - 1}.zrow.1")
                    E.wait(v, f"w{cc}.{tt - 1}.exp")
                v.reciprocal(SF[:, 256:512], psA[:, 1024:1280])
                E.mark(v.scalar_tensor_tensor(
                    SF[:, 0:256], psA[:, 512:768], SCALE_A, SF[:, 256:512],
                    ALU.mult, ALU.mult), "vo", f"{P}.praw")
                v.reciprocal(SF[:, 512:768], psA[:, 1280:1536])
                v.tensor_copy(SF[:, 1040:1296], psA[:, 1024:1280])
                E.mark(v.tensor_copy(SF[:, 1296:1552], psA[:, 1280:1536]),
                       "vo", f"{P}.zcopy")
                E.mark(v.tensor_reduce(
                    SF[:, 1032:1036],
                    SF[:, 0:256].rearrange("p (a b) -> p a b", a=4),
                    AX.X, ALU.max, negate=True), "vo", f"{P}.nmax")

            @b_.scalar
            def _(s, P=P, tt=tt, cc=cc):
                E.wait(s, f"{P}.nmax")
                if tt > 0:
                    E.wait(s, f"w{cc}.{tt - 1}.ptil")
                last = None
                for h in range(HA):
                    last = s.activation(
                        SF[:, 768 + h * 64: 768 + (h + 1) * 64],
                        SF[:, h * 64:(h + 1) * 64],
                        AF.Exp, bias=SF[:, 1032 + h: 1033 + h],
                        accum_out=SF[:, 1024 + h: 1025 + h])
                E.mark(last, "so", f"{P}.exp")

            @b_.scalar
            def _(s, P=P):
                E.wait(s, f"{P}.zcopy")
                E.mark(s.dma_start(out=SMS[0:1, 0:256], in_=SF[127:128, 1040:1296]),
                       "zd", f"{P}.zrow.0", 16)
                E.mark(s.dma_start(out=SMS[0:1, 256:512], in_=SF[127:128, 1296:1552]),
                       "zd", f"{P}.zrow.1", 16)

            @b_.vector
            def _(v, P=P, tt=tt, cc=cc):
                E.wait(v, f"{P}.exp")
                if tt > 0:
                    E.wait(v, f"w{cc}.{tt - 1}.ptT.1")
                rc = E.mark(v.reciprocal(SF[:, 1028:1032], SF[:, 1024:1028]), "vo")
                v.wait_ge(E.h["vo"], rc)
                last = None
                for h in range(HA):
                    last = v.scalar_tensor_tensor(
                        SSC[:, 1536 + h * 64: 1536 + (h + 1) * 64],
                        SF[:, 768 + h * 64: 768 + (h + 1) * 64],
                        SF[:, 1028 + h: 1029 + h],
                        SF[:, 512 + h * 64: 512 + (h + 1) * 64],
                        ALU.mult, ALU.mult)
                E.mark(last, "vo", f"{P}.ptil")

            @b_.tensor
            def _(t, P=P):
                E.wait(t, f"{P}.ptil")
                E.wait(t, f"{P}.ktc.7")
                for h in range(HA):
                    E.mark(t.transpose(psC[0:64, h * 128:(h + 1) * 128],
                                       SSC[:, 1536 + h * 64: 1536 + (h + 1) * 64],
                                       CV["identb"]), "tm", f"{P}.ptT.{h}")

            @b_.vector
            def _(v, P=P):
                E.wait(v, f"{P}.ptT.3")
                for h in range(HA):
                    E.mark(v.tensor_copy(
                        SSC[0:64, 512 + h * 128: 512 + (h + 1) * 128],
                        psC[0:64, h * 128:(h + 1) * 128]), "vo", f"{P}.ptc.{h}")

            @b_.tensor
            def _(t, T=T, P=P, tt=tt, cc=cc):
                E.wait(t, f"{P}.ptc.3")
                if tt > 0:
                    E.wait(t, f"w{cc}.{tt - 1}.aevac")
                mms = []
                for h in range(HA):
                    mms.append((psA[:, 1536 + h * 128: 1536 + (h + 1) * 128],
                                SVT[0:64, h * 2048 + T * 128: h * 2048 + (T + 1) * 128],
                                SSC[0:64, 512 + h * 128: 512 + (h + 1) * 128], "b3"))
                E.mark(mm_bank_group(t, mms), "tm", f"{P}.aT")

            @b_.vector
            def _(v, P=P):
                E.wait(v, f"{P}.aT")
                last = None
                for h in range(HA):
                    last = v.scalar_tensor_tensor(
                        SSC[:, 1024 + h * 128: 1024 + (h + 1) * 128],
                        psA[:, 1536 + h * 128: 1536 + (h + 1) * 128], 1.0, CF["trif"],
                        ALU.mult, ALU.mult)
                E.mark(last, "vo", f"{P}.aevac")

            @b_.tensor
            def _(t, P=P, tt=tt, cc=cc):
                E.wait(t, f"{P}.aevac")
                E.wait(t, f"{P}.v")
                if tt > 0:
                    E.wait(t, f"w{cc}.{tt - 1}.ofin")
                mms = []
                for h in range(HA):
                    bk = f"b{4 + h // 2}"
                    mms.append((psB[:, h * 256:(h + 1) * 256],
                                SSC[:, 1024 + h * 128: 1024 + (h + 1) * 128],
                                VS[:, (tt % 2) * 1024 + h * 256:(tt % 2) * 1024 + (h + 1) * 256],
                                bk))
                    mms.append((psB[:, h * 256:(h + 1) * 256],
                                SSC[0:64, 512 + h * 128: 512 + (h + 1) * 128],
                                STATEB[0:64, 512 + h * 256: 512 + (h + 1) * 256],
                                bk))
                E.mark(mm_bank_group(t, mms), "tm", f"{P}.o")

            @b_.scalar
            def _(s, P=P):
                E.wait(s, f"{P}.o")
                for h in range(HA):
                    s.activation(SMS[:, 528:784], psB[:, h * 256:(h + 1) * 256],
                                 AF.Square, accum_out=SMS[:, 512 + h: 513 + h])
                E.mark(s.activation(SMS[:, 516:520], SMS[:, 512:516], AF.Sqrt,
                                    bias=1e-5, scale=1.0 / 256.0),
                       "so", f"{P}.gn")

            @b_.vector
            def _(v, P=P, T=T, tt=tt, cc=cc):
                E.wait(v, f"{P}.gn")
                if tt > 0:
                    E.wait(v, f"w{cc}.{tt - 1}.ost")
                rc = E.mark(v.reciprocal(SMS[:, 520:524], SMS[:, 516:520]), "vo")
                v.wait_ge(E.h["vo"], rc)
                E.wait(v, f"{P}.gs")
                last = None
                for h in range(HA):
                    last = v.scalar_tensor_tensor(
                        SSC[:, 1792 + h * 256: 1792 + (h + 1) * 256],
                        psB[:, h * 256:(h + 1) * 256],
                        SMS[:, 520 + h: 521 + h],
                        SSC[:, 2816 + (tt % 2) * 1024 + h * 256:
                            2816 + (tt % 2) * 1024 + (h + 1) * 256],
                        ALU.mult, ALU.mult)
                E.mark(last, "vo", f"{P}.ofin")

            @b_.scalar
            def _(s, P=P, T=T):
                E.wait(s, f"{P}.ofin")
                E.mark(s.dma_start(out=DR["ofinb"][T * 128:(T + 1) * 128, :],
                                   in_=SSC[:, 1792:2816]), "hd", f"{P}.ost", 16)

            @b_.tensor
            def _(t, P=P, T=T, tt=tt, cc=cc):
                E.wait(t, f"{P}.praw")
                E.wait(t, f"{P}.zcopy")
                if tt > 0:
                    E.wait(t, f"w{cc}.{tt - 1}.supd")
                mms = []
                for h in range(HA):
                    for dqc in range(2):
                        i = 2 * h + dqc
                        mms.append((psD[:, i * 64:(i + 1) * 64],
                                    KTT[:, (tt % 2) * 1024 + h * 256 + dqc * 128:
                                        (tt % 2) * 1024 + h * 256 + dqc * 128 + 128],
                                    WKV[:, T * 256 + h * 64: T * 256 + (h + 1) * 64],
                                    "b7"))
                for h in range(HA):
                    mms.append((psA[0:64, 512 + h * 256: 512 + (h + 1) * 256],
                                WKV[:, 4096 + T * 256 + h * 64: 4096 + T * 256 + (h + 1) * 64],
                                VS[:, (tt % 2) * 1024 + h * 256:(tt % 2) * 1024 + (h + 1) * 256],
                                f"b{1 + h // 2}"))
                E.mark(mm_bank_group(t, mms), "tm", f"{P}.oupd")

            @b_.vector
            def _(v, P=P):
                E.wait(v, f"{P}.oupd")
                v.tensor_add(STATE[:, 0:512], STATE[:, 0:512], psD[:, 0:512])
                E.mark(v.tensor_add(STATE[0:64, 512:1536], STATE[0:64, 512:1536],
                                    psA[0:64, 512:1536]), "vo", f"{P}.supd")
                v.tensor_copy(STATEB[:, 0:512], STATE[:, 0:512])
                E.mark(v.tensor_copy(STATEB[0:64, 512:1536], STATE[0:64, 512:1536]),
                       "vo", f"{P}.stb")
        return f"w{cc}.7.ost"

    w0_done = chunk_walk(0, pfa_done)
    cut("walk0")
    pfb_done = prefix_add("pfb", CF["maskAB"], "w0.7.supd")
    w1_done = chunk_walk(1, pfb_done)

    cut("walk1")
    wo_ev = load_w("wo", IN["wo"], wait_evs=(w1_done,))
    bfold = blk()

    @bfold.vector
    def _(v):
        E.wait(v, wo_ev)
        last = None
        for dvc in range(8):
            last = v.tensor_scalar(
                WB[:, dvc * 1024:(dvc + 1) * 1024],
                WB[:, dvc * 1024:(dvc + 1) * 1024],
                CF["gnwc"][:, dvc % 2: dvc % 2 + 1], None, ALU.mult)
        E.mark(last, "vo", "wofold")

    OFT = QT
    btr = blk()

    @btr.sync
    def _(e):
        E.wait(e, "w1.7.ost")
        for dvc in range(8):
            E.mark(e.dma_start(out=OFT[:, dvc * 2048:(dvc + 1) * 2048],
                               in_=DR["ofinb"][:, dvc * 128:(dvc + 1) * 128],
                               transpose=True), "sd", f"oft.{dvc}", 16)

    cut("oft")
    X1F = dint("x1f", (TC, D))

    for tt in range(NT):
        pi = tt % 2
        P = f"ln.{tt}"
        b_ = blk()

        @b_.gpsimd
        def _(e, tt=tt, pi=pi, P=P):
            if tt < 2:
                E.wait(e, "w1.7.supd")
                E.wait(e, "w1.7.zrow.1")
            if tt >= 2:
                E.wait(e, f"ln.{tt - 2}.r1")
            E.mark(e.dma_start(out=SF[:, pi * 1024:(pi + 1) * 1024],
                               in_=IN["xt"][tt * 128:(tt + 1) * 128, :]),
                   "gd", f"{P}.x0", 16)

        @b_.tensor
        def _(t, tt=tt, P=P):
            if tt == 0:
                E.wait(t, "wofold")
                E.wait(t, "oft.7")
            if tt >= 2:
                E.wait(t, f"ln.{tt - 2}.r1")
            ps = psB[:, :] if tt % 2 == 0 else psA[:, 0:1024]
            mms = []
            for dvc in range(8):
                for half in range(2):
                    mms.append((ps[:, half * 512:(half + 1) * 512],
                                OFT[:, dvc * 2048 + tt * 128: dvc * 2048 + tt * 128 + 128],
                                WB[:, dvc * 1024 + half * 512: dvc * 1024 + (half + 1) * 512],
                                f"h{half}"))
            E.mark(mm_bank_group(t, mms), "tm", f"{P}.mm")

        @b_.vector
        def _(v, tt=tt, pi=pi, P=P):
            E.wait(v, f"{P}.mm")
            E.wait(v, f"{P}.x0")
            if tt >= 2:
                E.wait(v, f"ln.{tt - 2}.x1done")
            ps = psB[:, :] if tt % 2 == 0 else psA[:, 0:1024]
            x0t = SF[:, pi * 1024:(pi + 1) * 1024]
            E.mark(v.tensor_add(CONT[:, pi * 1024:(pi + 1) * 1024], ps, x0t),
                   "vo", f"{P}.r1")
            E.mark(v.reduce_sum(SMS[:, 790 + pi * 16: 791 + pi * 16],
                                CONT[:, pi * 1024:(pi + 1) * 1024], AX.X),
                   "vo", f"{P}.sum")

        @b_.scalar
        def _(s, tt=tt, pi=pi, P=P):
            E.wait(s, f"{P}.r1")
            E.mark(s.activation(SSC[:, 0:1024],
                                CONT[:, pi * 1024:(pi + 1) * 1024], AF.Square,
                                accum_out=SMS[:, 791 + pi * 16: 792 + pi * 16]),
                   "so", f"{P}.sq")

        @b_.vector
        def _(v, tt=tt, pi=pi, P=P):
            E.wait(v, f"{P}.sq")
            base = 790 + pi * 16
            v.tensor_scalar(SMS[:, base + 2: base + 3], SMS[:, base: base + 1],
                            1.0 / 1024.0, None, ALU.mult)
            v.tensor_scalar(SMS[:, base + 3: base + 4], SMS[:, base + 1: base + 2],
                            1.0 / 1024.0, None, ALU.mult)
            v.tensor_mul(SMS[:, base + 4: base + 5], SMS[:, base + 2: base + 3],
                         SMS[:, base + 2: base + 3])
            v.tensor_sub(SMS[:, base + 3: base + 4], SMS[:, base + 3: base + 4],
                         SMS[:, base + 4: base + 5])
            E.mark(v.tensor_scalar(SMS[:, base + 3: base + 4],
                                   SMS[:, base + 3: base + 4], 1.0, 1e-5,
                                   ALU.mult, ALU.add), "vo", f"{P}.var")

        @b_.scalar
        def _(s, tt=tt, pi=pi, P=P):
            E.wait(s, f"{P}.var")
            base = 790 + pi * 16
            E.mark(s.activation(SMS[:, base + 5: base + 6],
                                SMS[:, base + 3: base + 4], AF.Sqrt),
                   "so", f"{P}.sd")

        @b_.vector
        def _(v, tt=tt, pi=pi, P=P):
            E.wait(v, f"{P}.sd")
            base = 790 + pi * 16
            v.reciprocal(SMS[:, base + 6: base + 7], SMS[:, base + 5: base + 6])
            v.tensor_mul(SMS[:, base + 7: base + 8], SMS[:, base + 2: base + 3],
                         SMS[:, base + 6: base + 7])
            E.mark(v.tensor_scalar(SMS[:, base + 7: base + 8],
                                   SMS[:, base + 7: base + 8], -1.0, None,
                                   ALU.mult), "vo", f"{P}.nmb")

        @b_.vector
        def _(v, tt=tt, pi=pi, P=P):
            E.wait(v, f"{P}.nmb")
            if tt >= 1:
                E.wait(v, f"ln.{tt - 1}.x1f")
                E.wait(v, f"ln.{tt - 1}.x1st")
            base = 790 + pi * 16
            E.mark(v.tensor_scalar(CONT[:, (1 - pi) * 1024:(2 - pi) * 1024],
                                   CONT[:, pi * 1024:(pi + 1) * 1024],
                                   SMS[:, base + 2: base + 3],
                                   SMS[:, base + 6: base + 7],
                                   ALU.subtract, ALU.mult),
                   "vo", f"{P}.x1")

        @b_.vector
        def _(v, tt=tt, pi=pi, P=P):
            E.wait(v, f"{P}.x1")
            if tt >= 2:
                E.wait(v, f"ln.{tt - 2}.x1st")
            E.mark(v.tensor_copy(
                SSC[:, 2816 + pi * 1024: 2816 + (pi + 1) * 1024],
                CONT[:, (1 - pi) * 1024:(2 - pi) * 1024]), "vo", f"{P}.x1b")

        @b_.sync
        def _(e, tt=tt, pi=pi, P=P):
            E.wait(e, f"{P}.x1b")
            E.mark(e.dma_start(out=X1F[tt * 128:(tt + 1) * 128, :],
                               in_=CONT[:, (1 - pi) * 1024:(2 - pi) * 1024]),
                   "sd", f"{P}.x1f", 16)

        @b_.scalar
        def _(s, tt=tt, pi=pi, P=P):
            E.wait(s, f"{P}.x1b")
            E.mark(s.dma_start(out=DR["x1b"][tt * 128:(tt + 1) * 128, :],
                               in_=SSC[:, 2816 + pi * 1024: 2816 + (pi + 1) * 1024]),
                   "hd", f"{P}.x1st", 16)

        @b_.vector
        def _(v, tt=tt, P=P):
            E.wait(v, f"{P}.x1b")
            E.mark(v.engine_nop(), "vo", f"{P}.x1done")

    att.close()
    mh = ExitStack()
    _stacks.append(mh)
    X1T = mh.enter_context(nc.sbuf_tensor("X1T", [128, 16384], BF16))
    WB3 = mh.enter_context(nc.sbuf_tensor("WB3", [128, 8192], BF16))
    SC3 = mh.enter_context(nc.sbuf_tensor("SC3", [128, 4096], BF16))
    MR = mh.enter_context(nc.sbuf_tensor("MR", [128, 4096], BF16))
    VSTG = mh.enter_context(nc.sbuf_tensor("VSTG", [128, 1040], BF16))
    KB = mh.enter_context(nc.sbuf_tensor("KB", [128, 8192], BF16))
    VB = mh.enter_context(nc.sbuf_tensor("VB", [128, 8320], BF16))
    QH = mh.enter_context(nc.sbuf_tensor("QH", [128, 2048], BF16))
    PB = mh.enter_context(nc.sbuf_tensor("PB", [128, 2048], BF16))
    YPAIR = mh.enter_context(nc.sbuf_tensor("YPAIR", [128, 2048], BF16))
    YTS = mh.enter_context(nc.sbuf_tensor("YTS", [128, 2048], BF16))
    FSC = mh.enter_context(nc.sbuf_tensor("FSC", [128, 64], F32))

    b3 = blk()

    @b3.sync
    def _(e):
        E.wait(e, "ln.15.x1st")
        for dc in range(8):
            E.mark(e.dma_start(out=X1T[:, dc * 2048:(dc + 1) * 2048],
                               in_=DR["x1b"][:, dc * 128:(dc + 1) * 128],
                               transpose=True), "sd", f"x1t.{dc}", 16)

    @b3.gpsimd
    def _(e):
        E.mark(e.dma_start(out=MR[:, 0:2048], in_=IN["c_mcos2"][:, :]), "gd", None, 16)
        E.mark(e.dma_start(out=MR[:, 2048:4096], in_=IN["c_msin2"][:, :]), "gd", "mr", 16)
        E.wait(e, "ln.15.x1st")
        E.mark(e.dma_start(out=WB3[:, :].rearrange("p (a o) -> p a o", a=8),
                           in_=IN["wqkv"][:, 0:1024].rearrange("(a p) o -> p a o", p=128)),
               "gd", "w3.q", 16)

    bfold = blk()

    @bfold.vector
    def _(v):
        E.wait(v, "w3.q")
        last = None
        for dc in range(8):
            last = v.tensor_scalar(WB3[:, dc * 1024:(dc + 1) * 1024],
                                   WB3[:, dc * 1024:(dc + 1) * 1024],
                                   CF["g1c"][:, dc:dc + 1], None, ALU.mult)
        E.mark(last, "vo", "w3.qf")

    def qk_proj(tag, dst_is_q):
        for oc in range(8):
            P = f"{tag}.{oc}"
            b_ = blk()
            for ts in range(4):
                @b_.tensor
                def _(t, oc=oc, P=P, tag=tag, ts=ts):
                    if oc == 0 and ts == 0:
                        E.wait(t, f"w3.{tag}f")
                        E.wait(t, "x1t.7")
                        E.wait(t, "mr")
                    if oc > 0:
                        E.wait(t, f"{tag}.{oc - 1}.ev.{ts}")
                    last = None
                    for dc in range(8):
                        last = t.matmul(
                            psA[:, ts * 512:(ts + 1) * 512],
                            WB3[:, dc * 1024 + oc * 128: dc * 1024 + oc * 128 + 128],
                            X1T[:, dc * 2048 + ts * 512: dc * 2048 + ts * 512 + 512],
                            start=(dc == 0), stop=(dc == 7))
                    E.mark(last, "tm", f"{P}.mm.{ts}")

                @b_.vector
                def _(v, oc=oc, P=P, ts=ts):
                    E.wait(v, f"{P}.mm.{ts}")
                    E.mark(v.tensor_copy(SC3[:, ts * 512:(ts + 1) * 512],
                                         psA[:, ts * 512:(ts + 1) * 512]),
                           "vo", f"{P}.ev.{ts}")

                @b_.tensor
                def _(t, oc=oc, P=P, ts=ts):
                    E.wait(t, f"{P}.ev.{ts}")
                    if ts >= 2:
                        E.wait(t, f"{P}.rp.{ts - 2}")
                    E.mark(t.matmul(psB[:, (ts % 2) * 512:(ts % 2 + 1) * 512],
                                    CV["perm"],
                                    SC3[:, ts * 512:(ts + 1) * 512],
                                    start=True, stop=True), "tm", f"{P}.pm.{ts}")

                @b_.vector
                def _(v, oc=oc, P=P, ts=ts, tag=tag):
                    E.wait(v, f"{P}.pm.{ts}")
                    if ts == 0:
                        if oc > 0:
                            E.wait(v, f"{tag}.{oc - 1}.st")
                        elif tag == "k":
                            E.wait(v, "q.7.st")
                    v.scalar_tensor_tensor(
                        SC3[:, 2048 + ts * 512: 2048 + (ts + 1) * 512],
                        psB[:, (ts % 2) * 512:(ts % 2 + 1) * 512], 1.0,
                        MR[:, 2048 + ts * 512: 2048 + (ts + 1) * 512],
                        ALU.mult, ALU.mult)
                    v.tensor_mul(SC3[:, ts * 512:(ts + 1) * 512],
                                 SC3[:, ts * 512:(ts + 1) * 512],
                                 MR[:, ts * 512:(ts + 1) * 512])
                    E.mark(v.tensor_add(SC3[:, 2048 + ts * 512: 2048 + (ts + 1) * 512],
                                        SC3[:, 2048 + ts * 512: 2048 + (ts + 1) * 512],
                                        SC3[:, ts * 512:(ts + 1) * 512]),
                           "vo", f"{P}.rp.{ts}")

            @b_.scalar
            def _(s, oc=oc, P=P, dst_is_q=dst_is_q):
                E.wait(s, f"{P}.rp.3")
                if dst_is_q:
                    dst = DR["qmhad"][oc * 128:(oc + 1) * 128, :]
                else:
                    dst = DR["kv_in"][:, oc * 2048:(oc + 1) * 2048]
                E.mark(s.dma_start(out=dst, in_=SC3[:, 2048:4096]), "hd", f"{P}.st", 16)

    qk_proj("q", True)

    bwk = blk()

    @bwk.gpsimd
    def _(e):
        E.wait(e, "q.7.rp.3")
        E.mark(e.dma_start(out=WB3[:, :].rearrange("p (a o) -> p a o", a=8),
                           in_=IN["wqkv"][:, 1024:2048].rearrange("(a p) o -> p a o", p=128)),
               "gd", "w3.kraw", 16)

    bfold2 = blk()

    @bfold2.vector
    def _(v):
        E.wait(v, "w3.kraw")
        last = None
        for dc in range(8):
            last = v.tensor_scalar(WB3[:, dc * 1024:(dc + 1) * 1024],
                                   WB3[:, dc * 1024:(dc + 1) * 1024],
                                   CF["g1c"][:, dc:dc + 1], None, ALU.mult)
        E.mark(last, "vo", "w3.kf")

    qk_proj("k", False)

    bwv = blk()

    @bwv.gpsimd
    def _(e):
        E.wait(e, "k.7.rp.3")
        E.mark(e.dma_start(out=WB3[:, :].rearrange("p (a o) -> p a o", a=8),
                           in_=IN["wqkv"][:, 2048:3072].rearrange("(a p) o -> p a o", p=128)),
               "gd", "w3.vraw", 16)

    bfold3 = blk()

    @bfold3.vector
    def _(v):
        E.wait(v, "w3.vraw")
        last = None
        for dc in range(8):
            last = v.tensor_scalar(WB3[:, dc * 1024:(dc + 1) * 1024],
                                   WB3[:, dc * 1024:(dc + 1) * 1024],
                                   CF["g1c"][:, dc:dc + 1], None, ALU.mult)
        E.mark(last, "vo", "w3.vf")
        E.mark(v.memset(VSTG[:, :], 1.0), "vo", "vstg1")

    for tt in range(NT):
        P = f"v3.{tt}"
        b_ = blk()

        @b_.tensor
        def _(t, tt=tt, P=P):
            if tt == 0:
                E.wait(t, "w3.vf")
            if tt >= 2:
                E.wait(t, f"v3.{tt - 2}.ev")
            ps = psB[:, :] if tt % 2 == 0 else psA[:, 1024:2048]
            last = None
            for dc in range(8):
                for half in range(2):
                    last = t.matmul(
                        ps[:, half * 512:(half + 1) * 512],
                        X1T[:, dc * 2048 + tt * 128: dc * 2048 + tt * 128 + 128],
                        WB3[:, dc * 1024 + half * 512: dc * 1024 + (half + 1) * 512],
                        start=(dc == 0), stop=(dc == 7))
            E.mark(last, "tm", f"{P}.mm")

        @b_.vector
        def _(v, tt=tt, P=P):
            E.wait(v, f"{P}.mm")
            if tt == 0:
                E.wait(v, "vstg1")
            if tt >= 1:
                E.wait(v, f"v3.{tt - 1}.st")
            ps = psB[:, :] if tt % 2 == 0 else psA[:, 1024:2048]
            last = None
            for h in range(HM):
                last = v.tensor_copy(VSTG[:, h * 65: h * 65 + 64],
                                     ps[:, h * 64:(h + 1) * 64])
            E.mark(last, "vo", f"{P}.ev")

        @b_.scalar
        def _(s, tt=tt, P=P):
            E.wait(s, f"{P}.ev")
            E.mark(s.dma_start(
                out=DR["kv_in"][:, 16384:33024].rearrange(
                    "p (h c) -> p h c", h=16)[:, :, tt * 65:(tt + 1) * 65],
                in_=VSTG[:, :].rearrange("p (h c) -> p h c", h=16)),
                "hd", f"{P}.st", 16)

    cut("qkv")

    bag3 = blk()

    @bag3.gpsimd
    def _(e):
        E.wait(e, "v3.15.st")
        E.wait(e, "k.7.st")
        E.mark(e.collective_compute(
            "AllGather", ALU.bypass,
            replica_groups=[[0, 1, 2, 3], [4, 5, 6, 7]],
            ins=[DR["kv_in"].tensor.ap()],
            outs=[DR["kv_out"].tensor.ap()],
        ), "cc", "ag.kv")

    cut("ag")

    LENA = {g: 8 * g + 8 for g in range(4)}

    def kv_loads(e, h):
        hp = h % 2
        if h == 0:
            E.wait(e, "ag.kv")
        if h >= 2:
            E.wait(e, f"fl{h - 2}.ev1")
        last = None
        for r in range(4):
            for sl in range(2):
                gc = r if sl == 0 else 7 - r
                last = e.dma_start(
                    out=KB[hp * 64:(hp + 1) * 64, gc * 1024:(gc + 1) * 1024],
                    in_=DR["kv_out"][128 * r + hp * 64: 128 * r + hp * 64 + 64,
                                     (h // 2) * 2048 + sl * 1024:
                                     (h // 2) * 2048 + (sl + 1) * 1024])
                E.mark(last, "sd", None, 16)
                last = e.dma_start(
                    out=VB[:, hp * 4160 + gc * 520: hp * 4160 + (gc + 1) * 520],
                    in_=DR["kv_out"][128 * r:128 * (r + 1),
                                     16384 + h * 1040 + sl * 520:
                                     16384 + h * 1040 + (sl + 1) * 520])
                E.mark(last, "sd", None, 16)
        last = e.dma_start(out=QH[hp * 64:(hp + 1) * 64, :],
                           in_=DR["qmhad"][h * 64:(h + 1) * 64, :])
        E.mark(last, "sd", f"kv{h}.ld", 16)

    TM0, SO0, VO0, GD0 = E.n["tm"], E.n["so"], E.n["vo"], E.n["gd"]
    base_tm, base_so, base_vo = TM0, SO0, VO0
    for h in range(HM):
        P = f"fl{h}"
        for i in range(72):
            E.declare(f"{P}.s{i}", "tm", base_tm + (i + 1 if i < 2 else 2 * i))
            E.declare(f"{P}.p{i}", "tm", base_tm + 3 + i + min(i, 70))
            E.declare(f"{P}.x{i}", "so", base_so + i + 1)
        E.declare(f"{P}.ev0", "vo", base_vo + 2)
        E.declare(f"{P}.ev1", "vo", base_vo + 4)
        E.declare(f"{P}.pA", "fa", h + 1)
        base_tm += 144
        base_so += 72
        base_vo += 4
        if h % 2 == 1:
            pair = h // 2
            for qt16 in range(16):
                E.declare(f"yt{pair}.t{qt16}", "tm", base_tm + qt16 + 1)
            base_tm += 16
            for bch in range(4):
                E.declare(f"yt{pair}.c{bch}", "vo", base_vo + bch + 1)
            base_vo += 4
            E.declare(f"yt{pair}.st", "gd", GD0 + 16 * (pair + 1))

    bfl = blk()

    @bfl.sync
    def _(e):
        for h in range(HM):
            kv_loads(e, h)

    @bfl.tensor
    def _(t):
        rk = t.partition_id() % 4
        snap = dict(E.n)
        ends = []
        for g in nc.Switch(engines=[t], index=[rk], n=4):
            E.switch_body_begin(snap)
            sched = ([(0, j) for j in range(8 * g + 8)]
                     + [(1, j) for j in range(64 - 8 * g)])
            assert len(sched) == 72
            cpos = {0: g, 1: 7 - g}
            chunk_start = {0: 0, 1: 8 * g + 8}
            for h in range(HM):
                hp = h % 2
                P = f"fl{h}"
                fa_pending = [False]

                def emit_s(i, h=h, hp=hp, P=P, fa_pending=fa_pending):
                    cs, j = sched[i]
                    c = cpos[cs]
                    diag = (j >= 8 * c)
                    d = j - 8 * c
                    buf = i % 2
                    if i == 0:
                        E.wait(t, f"kv{h}.ld")
                    if i < 2:
                        if h >= 1:
                            E.wait(t, f"fl{h - 1}.x{70 + i}")
                    else:
                        E.wait(t, f"{P}.x{i - 2}")
                    mms = []
                    for halfq in range(2):
                        mms.append((
                            psA[:, buf * 1024 + halfq * 512:
                                buf * 1024 + (halfq + 1) * 512],
                            KB[hp * 64:(hp + 1) * 64, j * 128:(j + 1) * 128],
                            QH[hp * 64:(hp + 1) * 64,
                               cs * 1024 + halfq * 512:
                               cs * 1024 + (halfq + 1) * 512],
                            "lo" if halfq == 0 else "hi"))
                    if diag:
                        mms.append((
                            psA[:, buf * 1024 + d * 128:
                                buf * 1024 + (d + 1) * 128],
                            CV["identb"], CV["negm"], "lo" if d < 4 else "hi"))
                    first = {}
                    last = {}
                    for k, (_, _, _, bk) in enumerate(mms):
                        first.setdefault(bk, k)
                        last[bk] = k
                    inst = None
                    for k, (o, l, r, bk) in enumerate(mms):
                        inst = t.matmul(o, l, r, start=(first[bk] == k),
                                        stop=(last[bk] == k))
                        if k == 0 and fa_pending[0]:
                            E.mark(inst, "fa", f"{P}.pA")
                            fa_pending[0] = False
                    E.mark(inst, "tm", f"{P}.s{i}")

                emit_s(0)
                emit_s(1)
                for i in range(72):
                    cs, j = sched[i]
                    c = cpos[cs]
                    diag = (j >= 8 * c)
                    d = j - 8 * c if diag else -1
                    buf = i % 2
                    S0 = chunk_start[cs]
                    E.wait(t, f"{P}.x{i}")
                    if i == S0:
                        if cs == 1:
                            E.wait(t, f"{P}.ev0")
                        elif h >= 1:
                            E.wait(t, f"fl{h - 1}.ev1")
                    b4_last = S0 + 8 * c + 3
                    b5_last = S0 + 8 * c + 7
                    inst = None
                    for qt in range(8):
                        if diag and qt < d:
                            continue
                        start = (i == S0) and (qt == 0 or qt == 4)
                        stop = ((i == b4_last and qt == 3)
                                or (i == b5_last and qt == 7))
                        inst = t.matmul(
                            psB[:, qt * 128: qt * 128 + 65],
                            PB[:, buf * 1024 + qt * 128:
                               buf * 1024 + (qt + 1) * 128],
                            VB[:, hp * 4160 + j * 65: hp * 4160 + (j + 1) * 65],
                            start=start, stop=stop)
                    E.mark(inst, "tm", f"{P}.p{i}")
                    if cs == 0 and i == S0 + 8 * c + 7:
                        fa_pending[0] = True
                    if i + 2 < 72:
                        emit_s(i + 2)
                assert not fa_pending[0]
                if h % 2 == 1:
                    pair = h // 2
                    for bch in range(4):
                        if bch == 0:
                            E.wait(t, f"fl{h}.ev1")
                            if pair > 0:
                                E.wait(t, f"yt{pair - 1}.c3")
                        else:
                            E.wait(t, f"yt{pair}.c{bch - 1}")
                        for jj in range(4):
                            qt16 = bch * 4 + jj
                            E.mark(t.transpose(
                                psC[:, jj * 256: jj * 256 + 128],
                                YPAIR[:, qt16 * 128:(qt16 + 1) * 128],
                                CV["identb"]), "tm", f"yt{pair}.t{qt16}")
            ends.append(dict(E.n))
        E.switch_end(ends)

    @bfl.scalar
    def _(s):
        for h in range(HM):
            P = f"fl{h}"
            for i in range(72):
                E.wait(s, f"{P}.s{i}")
                if i >= 2:
                    E.wait(s, f"{P}.p{i - 2}")
                elif h >= 1:
                    E.wait(s, f"fl{h - 1}.p{70 + i}")
                buf = i % 2
                E.mark(s.activation(PB[:, buf * 1024:(buf + 1) * 1024],
                                    psA[:, buf * 1024:(buf + 1) * 1024],
                                    AF.Exp, scale=0.125), "so", f"{P}.x{i}")

    @bfl.vector
    def _(v):
        for h in range(HM):
            hp = h % 2
            P = f"fl{h}"
            for cs in range(2):
                if cs == 0:
                    E.wait(v, f"{P}.pA")
                    if h % 2 == 0 and h >= 2:
                        E.wait(v, f"yt{h // 2 - 1}.st")
                else:
                    E.wait(v, f"{P}.p71")
                rc = E.mark(v.reciprocal(
                    FSC[:, 0:8].rearrange("p (a b) -> p a b", b=1),
                    psB[:, :].rearrange("p (a b) -> p a b", a=8)[:, :, 64:65]),
                    "vo")
                v.wait_ge(E.h["vo"], rc)
                last = None
                for qt in range(8):
                    last = v.tensor_scalar(
                        YPAIR[:, (cs * 8 + qt) * 128 + hp * 64:
                              (cs * 8 + qt) * 128 + hp * 64 + 64],
                        psB[:, qt * 128: qt * 128 + 64],
                        FSC[:, qt:qt + 1], None, ALU.mult)
                E.mark(last, "vo", f"{P}.ev{cs}")
            if h % 2 == 1:
                pair = h // 2
                for bch in range(4):
                    E.wait(v, f"yt{pair}.t{bch * 4 + 3}")
                    last = None
                    for jj in range(4):
                        qt16 = bch * 4 + jj
                        last = v.tensor_copy(
                            YTS[:, qt16 * 128:(qt16 + 1) * 128],
                            psC[:, jj * 256: jj * 256 + 128])
                    E.mark(last, "vo", f"yt{pair}.c{bch}")

    @bfl.gpsimd
    def _(e):
        for pair in range(8):
            E.wait(e, f"yt{pair}.c3")
            E.mark(e.dma_start(out=DR["ybT"][pair * 128:(pair + 1) * 128, :],
                               in_=YTS[:, :]), "gd", f"yt{pair}.st", 16)

    cut("flash")

    mh.close()
    mo2 = ExitStack()
    _stacks.append(mo2)
    GSB = mo2.enter_context(nc.sbuf_tensor("GSB", [128, 128], F32))
    GEB = mo2.enter_context(nc.sbuf_tensor("GEB", [128, 128], F32))
    MSKB = mo2.enter_context(nc.sbuf_tensor("MSKB", [128, 128], F32))
    TAUL = mo2.enter_context(nc.sbuf_tensor("TAUL", [128, 128], F32))
    KEPTB = mo2.enter_context(nc.sbuf_tensor("KEPTB", [128, 128], F32))
    RKS = mo2.enter_context(nc.sbuf_tensor("RKS", [128, 128], F32))
    TIDG = mo2.enter_context(nc.sbuf_tensor("TIDG", [128, 32], F32))
    PXA = mo2.enter_context(nc.sbuf_tensor("PXA", [128, 16], F32))
    PXB = mo2.enter_context(nc.sbuf_tensor("PXB", [128, 16], F32))
    DMY = mo2.enter_context(nc.sbuf_tensor("DMY", [128, 16], F32))
    BIS = mo2.enter_context(nc.sbuf_tensor("BIS", [1, 16], F32))
    MIDC = mo2.enter_context(nc.sbuf_tensor("MIDC", [128, 1], F32))
    CNTR = mo2.enter_context(nc.sbuf_tensor("CNTR", [1, 128], F32))
    TROW = mo2.enter_context(nc.sbuf_tensor("TROW", [1, 128], F32))
    TAU8 = mo2.enter_context(nc.sbuf_tensor("TAU8", [8, 1], F32))
    PAT8S = mo2.enter_context(nc.sbuf_tensor("PAT8S", [8, 128], F32))
    RTOT = mo2.enter_context(nc.sbuf_tensor("RTOT", [128, 1], F32))
    CARR = mo2.enter_context(nc.sbuf_tensor("CARR", [128, 1], F32))
    IOTH = mo2.enter_context(nc.sbuf_tensor("IOTH", [128, 256], F16))
    TLC = mo2.enter_context(nc.sbuf_tensor("TLC", [128, 16], F16))
    PPB = mo2.enter_context(nc.sbuf_tensor("PPB", [128, 512], F16))
    TGE = mo2.enter_context(nc.sbuf_tensor("TGE", [128, 32], F16))
    XGB = mo2.enter_context(nc.sbuf_tensor("XGB", [128, 2048], F16))
    XIDX = mo2.enter_context(nc.sbuf_tensor("XIDX", [128, 16], U32))
    YIDX = mo2.enter_context(nc.sbuf_tensor("YIDX", [128, 16], U32))
    YRW = mo2.enter_context(nc.sbuf_tensor("YRW", [128, 16], U32))
    mh2p = ExitStack()
    _stacks.append(mh2p)
    PRE = mh2p.enter_context(nc.sbuf_tensor("PRE", [128, 16384], F32))
    mh2 = ExitStack()
    _stacks.append(mh2)
    WO2 = mh2.enter_context(nc.sbuf_tensor("WO2", [128, 8192], BF16))
    YTT = mh2.enter_context(nc.sbuf_tensor("YTT", [128, 2048], BF16))
    XF2 = mh2.enter_context(nc.sbuf_tensor("XF2", [128, 2048], F32))
    _dumpables["PRE"] = PRE

    bwo = blk()

    @bwo.gpsimd
    def _(e):
        E.wait(e, "yt7.st")
        E.mark(e.dma_start(out=WO2[:, :].rearrange("p (a o) -> p a o", a=8),
                           in_=IN["wo2"].rearrange("(a p) o -> p a o", p=128)),
               "gd", "wo2.w", 16)

    b_ = blk()
    for tt in range(NT):
        P = f"wo2.{tt}"

        @b_.sync
        def _(e, tt=tt, P=P):
            if tt == 0:
                E.wait(e, "yt7.st")
            if tt >= 2:
                E.wait(e, f"wo2.{tt - 2}.pre")
            E.mark(e.dma_start(
                out=YTT[:, (tt % 2) * 1024:(tt % 2 + 1) * 1024].rearrange(
                    "p (a c) -> p a c", a=8),
                in_=DR["ybT"][:, tt * 128:(tt + 1) * 128].rearrange(
                    "(a p) c -> p a c", p=128)), "sd", f"{P}.yt", 16)

        @b_.gpsimd
        def _(e, tt=tt, P=P):
            if tt >= 2:
                E.wait(e, f"wo2.{tt - 2}.pre")
            E.wait(e, f"ln.{tt}.x1f")
            E.mark(e.dma_start(out=XF2[:, (tt % 2) * 1024:(tt % 2 + 1) * 1024],
                               in_=X1F[tt * 128:(tt + 1) * 128, :]),
                   "gd", f"{P}.x1", 16)

        @b_.tensor
        def _(t, tt=tt, P=P):
            if tt == 0:
                E.wait(t, "wo2.w")
            E.wait(t, f"{P}.yt")
            if tt >= 2:
                E.wait(t, f"wo2.{tt - 2}.pre")
            ps = psB[:, :] if tt % 2 == 0 else psA[:, 0:1024]
            mms = []
            for dc in range(8):
                for half in range(2):
                    mms.append((ps[:, half * 512:(half + 1) * 512],
                                YTT[:, (tt % 2) * 1024 + dc * 128:
                                    (tt % 2) * 1024 + (dc + 1) * 128],
                                WO2[:, dc * 1024 + half * 512: dc * 1024 + (half + 1) * 512],
                                f"h{half}"))
            E.mark(mm_bank_group(t, mms), "tm", f"{P}.mm")

        @b_.vector
        def _(v, tt=tt, P=P):
            E.wait(v, f"{P}.mm")
            E.wait(v, f"{P}.x1")
            ps = psB[:, :] if tt % 2 == 0 else psA[:, 0:1024]
            E.mark(v.tensor_add(PRE[:, tt * 1024:(tt + 1) * 1024], ps,
                                XF2[:, (tt % 2) * 1024:(tt % 2 + 1) * 1024]),
                   "vo", f"{P}.pre")

    if DEBUG_STAGE == 2:
        bd2 = blk()

        @bd2.sync
        def _(e):
            for tt in range(NT):
                E.wait(e, f"wo2.{tt}.pre")
            for tt in range(NT):
                E.mark(e.dma_start(out=OUT[tt * 128:(tt + 1) * 128, :],
                                   in_=PRE[:, tt * 1024:(tt + 1) * 1024]),
                       "sd", f"dbg2.{tt}", 16)
            E.wait(e, f"dbg2.{NT - 1}")
        if _cur_blk[0] is not None:
            _cur_blk[0].__exit__(None, None, None)
            _cur_blk[0] = None
        mh2.close()
        mh2p.close()
        st.close()
        return nc

    if DEBUG_STAGE == 1:
        bdbg = blk()

        @bdbg.sync
        def _(e):
            for tt in range(NT):
                E.wait(e, f"ln.{tt}.x1f")
            for tt in range(NT):
                E.mark(e.dma_start(out=OUT[tt * 128:(tt + 1) * 128, :],
                                   in_=X1F[tt * 128:(tt + 1) * 128, :]),
                       "sd", f"dbg.{tt}", 16)
            E.wait(e, f"dbg.{NT - 1}")
        if _cur_blk[0] is not None:
            _cur_blk[0].__exit__(None, None, None)
            _cur_blk[0] = None
        mh2.close()
        mh2p.close()
        mo2.close()
        st.close()
        return nc

    mh2.close()
    mo1 = ExitStack()
    _stacks.append(mo1)
    XH = mo1.enter_context(nc.sbuf_tensor("XH", [128, 16384], F16))
    XTR = mo1.enter_context(nc.sbuf_tensor("XTR", [128, 16384], F16))
    JNK = mo1.enter_context(nc.sbuf_tensor("JNK", [128, 1024], F32))
    WRS = mo1.enter_context(nc.sbuf_tensor("WRS", [128, 64], F16))
    GAB = mo1.enter_context(nc.sbuf_tensor("GAB", [128, 1024], F32))
    MSB = mo1.enter_context(nc.sbuf_tensor("MSB", [128, 1024], F32))
    ROUT = mo1.enter_context(nc.sbuf_tensor("ROUT", [128, 64], F32))

    b4a = blk()

    @b4a.sync
    def _(e):
        for tt in range(NT):
            E.wait(e, f"wo2.{tt}.pre")
            E.mark(e.dma_start(out=DR["PREd"][tt * 128:(tt + 1) * 128, :],
                               in_=PRE[:, tt * 1024:(tt + 1) * 1024]),
                   "sd", f"p4.pred{tt}", 16)

    @b4a.scalar
    def _(s):
        for tt in range(NT):
            E.wait(s, f"wo2.{tt}.pre")
            E.mark(s.activation(JNK[:, :], PRE[:, tt * 1024:(tt + 1) * 1024],
                                AF.Square,
                                accum_out=SMS[:, 528 + tt: 529 + tt]),
                   "so", f"p4.sq{tt}")

    @b4a.vector
    def _(v):
        E.wait(v, "p4.sq15")
        E.mark(v.tensor_scalar(SMS[:, 544:560], SMS[:, 528:544],
                               1.0 / 1024.0, 1e-6, ALU.mult, ALU.add),
               "vo", "p4.vv")

    @b4a.scalar
    def _(s):
        E.wait(s, "p4.vv")
        E.mark(s.activation(SMS[:, 560:576], SMS[:, 544:560], AF.Sqrt),
               "so", "p4.sd2")

    @b4a.vector
    def _(v):
        E.wait(v, "p4.sd2")
        rc = E.mark(v.reciprocal(SMS[:, 576:592], SMS[:, 560:576]), "vo")
        v.wait_ge(E.h["vo"], rc)
        for tt in range(NT):
            E.mark(v.tensor_scalar(XH[:, tt * 1024:(tt + 1) * 1024],
                                   PRE[:, tt * 1024:(tt + 1) * 1024],
                                   SMS[:, 576 + tt: 577 + tt], None, ALU.mult),
                   "vo", f"p4.xh{tt}")

    @b4a.scalar
    def _(s):
        for tt in range(NT):
            E.wait(s, f"p4.xh{tt}")
            E.mark(s.dma_start(out=DR["xsd"][tt * 128:(tt + 1) * 128, :],
                               in_=XH[:, tt * 1024:(tt + 1) * 1024]),
                   "hd", f"p4.xsd{tt}", 16)

    @b4a.gpsimd
    def _(e):
        E.mark(e.dma_start(out=IOTH[:, :], in_=IN["c_iotah"][:, :]), "gd", None, 16)
        E.mark(e.dma_start(out=TLC[:, :], in_=IN["c_tloch"][:, :]), "gd", None, 16)
        E.mark(e.dma_start(out=DMY[:, :], in_=IN["c_dumy"][:, :]), "gd", None, 16)
        E.mark(e.dma_start(out=MSB[:, :], in_=IN["c_moemask"][:, :]), "gd", "p4.msb", 16)
        E.mark(e.dma_start(out=YRW[:, :], in_=IN["c_yrows"][:, :]), "gd", "p4.yrw", 16)
        E.mark(e.dma_start(out=PAT8S[:, :], in_=IN["c_pat8"][:, :]), "gd", "p4.p8", 16)
        E.mark(e.dma_start(out=WRS[:, :].rearrange("p (a j) -> p a j", a=8),
                           in_=IN["wrout"].rearrange("(a p) j -> p a j", p=128)),
               "gd", "p4.wrs", 16)
        E.wait(e, "p4.sq15")
        E.mark(e.memset(JNK[:, :], 0.0), "go", "p4.jz")
        E.wait(e, "p4.jz")
        for zi in range(18):
            E.mark(e.dma_start(out=DR["ystage"][zi * 128:(zi + 1) * 128, :],
                               in_=JNK[:, :]),
                   "gd", "p4.yz" if zi == 17 else None, 16)

    @b4a.vector
    def _(v):
        E.wait(v, "p4.wrs")
        last = None
        for a in range(8):
            last = v.tensor_scalar(WRS[:, a * 8:(a + 1) * 8],
                                   WRS[:, a * 8:(a + 1) * 8],
                                   CF["g2c"][:, a:a + 1], None, ALU.mult)
        E.mark(last, "vo", "p4.wrf")

    @b4a.sync
    def _(e):
        for tt in range(NT):
            E.wait(e, f"p4.xsd{tt}")
        for dc in range(8):
            E.mark(e.dma_start(out=XTR[:, dc * 2048:(dc + 1) * 2048],
                               in_=DR["xsd"][:, dc * 128:(dc + 1) * 128],
                               transpose=True), "sd", f"p4.xtr{dc}", 16)

    cut("p4a")
    b4b = blk()
    for ts_ in range(NT):
        @b4b.tensor
        def _(t, ts_=ts_):
            if ts_ == 0:
                E.wait(t, "p4.xtr7")
                E.wait(t, "p4.wrf")
            if ts_ >= 2:
                E.wait(t, f"p4.lge{ts_ - 2}")
            last = None
            for dc in range(8):
                last = t.matmul(
                    psB[:, (ts_ % 2) * 512:(ts_ % 2) * 512 + 8],
                    XTR[:, dc * 2048 + ts_ * 128: dc * 2048 + (ts_ + 1) * 128],
                    WRS[:, dc * 8:(dc + 1) * 8],
                    start=(dc == 0), stop=(dc == 7))
            E.mark(last, "tm", f"p4.lg{ts_}")

        @b4b.vector
        def _(v, ts_=ts_):
            w = vchain(v, E)
            E.wait(v, f"p4.lg{ts_}")
            w(v.tensor_copy(ROUT[:, 0:8],
                            psB[:, (ts_ % 2) * 512:(ts_ % 2) * 512 + 8]),
              f"p4.lge{ts_}")
            w(v.max(ROUT[:, 8:16], ROUT[:, 0:8]), f"p4.nm{ts_}")

        @b4b.scalar
        def _(s, ts_=ts_):
            E.wait(s, f"p4.nm{ts_}")
            s.activation(ROUT[:, 16:24], ROUT[:, 0:8], AF.Exp)
            E.mark(s.activation(ROUT[:, 25:26], ROUT[:, 9:10], AF.Exp),
                   "so", f"p4.ex{ts_}")

        @b4b.vector
        def _(v, ts_=ts_):
            w = vchain(v, E)
            E.wait(v, f"p4.ex{ts_}")
            w(v.reduce_sum(ROUT[:, 26:27],
                           ROUT[:, 16:24].rearrange("p (x a) -> p x a", x=1),
                           AX.X))
            w(v.reciprocal(ROUT[:, 27:28], ROUT[:, 26:27]))
            w(v.tensor_scalar(ROUT[:, 16:24], ROUT[:, 16:24], ROUT[:, 27:28],
                              None, ALU.mult))
            w(v.tensor_mul(ROUT[:, 28:29], ROUT[:, 25:26], ROUT[:, 27:28]))
            w(v.tensor_scalar(ROUT[:, 29:37], ROUT[:, 16:24], ROUT[:, 28:29],
                              None, ALU.is_ge))
            w(v.tensor_mul(GSB[:, ts_ * 8:(ts_ + 1) * 8], ROUT[:, 29:37],
                           ROUT[:, 16:24]), f"p4.g{ts_}")

    @b4b.sync
    def _(e):
        for ts_ in range(NT):
            E.wait(e, f"p4.g{ts_}")
        E.mark(e.dma_start(
            out=DR["gdram"][:, :].rearrange("(a p) j -> p a j", p=128),
            in_=GSB[:, :].rearrange("p (a j) -> p a j", a=16)),
            "sd", "p4.gst", 16)

    @b4b.gpsimd
    def _(e):
        E.wait(e, "p4.gst")
        E.mark(e.collective_compute("AllGather", ALU.bypass,
                                    replica_groups=[[0, 1, 2, 3, 4, 5, 6, 7]],
                                    ins=[DR["gdram"].tensor.ap()],
                                    outs=[DR["gag"].tensor.ap()]), "cc", "p4.gag")

    @b4b.sync
    def _(e):
        E.wait(e, "p4.gag")
        for r in range(8):
            E.mark(e.dma_start(
                out=GAB[:, r * 128:(r + 1) * 128].rearrange(
                    "p (a j) -> p a j", a=16),
                in_=DR["gag"][r * 2048:(r + 1) * 2048, :].rearrange(
                    "(a p) j -> p a j", p=128)),
                "sd", f"p4.ga{r}", 16)

    @b4b.vector
    def _(v):
        for r in range(8):
            E.wait(v, f"p4.ga{r}")
        E.wait(v, "p4.msb")
        v.tensor_mul(GAB[:, :], GAB[:, :], MSB[:, :])
        E.mark(v.reduce_sum(GEB[:, :],
                            GAB[:, :].rearrange("p (a j) -> p a j", j=8), AX.X),
               "vo", "p4.ge")

    _dumpables["GSB"] = GSB
    _dumpables["GEB"] = GEB
    _dumpables["KEPTB"] = KEPTB
    _dumpables["RKS"] = RKS
    _dumpables["TIDG"] = TIDG
    _dumpables["BIS"] = BIS
    _dumpables["TAUL"] = TAUL
    cut("p4b")
    b4c = blk()

    @b4c.vector
    def _(v):
        E.wait(v, "p4.ge")
        v.memset(BIS[0:1, 0:1], 0.0)
        E.mark(v.memset(BIS[0:1, 1:2], 1.0), "vo", "p4.bis0")

    NBIS = 40
    for it in range(NBIS):
        P = f"p4.b{it}"

        @b4c.vector
        def _(v, it=it, P=P):
            w = vchain(v, E)
            E.wait(v, "p4.bis0" if it == 0 else f"p4.b{it - 1}.upd")
            w(v.tensor_add(BIS[0:1, 2:3], BIS[0:1, 0:1], BIS[0:1, 1:2]))
            w(v.tensor_scalar(BIS[0:1, 2:3], BIS[0:1, 2:3], 0.5, None,
                              ALU.mult), f"{P}.mid")

        @b4c.tensor
        def _(t, it=it, P=P):
            E.wait(t, f"{P}.mid")
            E.mark(t.matmul(psD[:, 0:1], CF["onesrow"], BIS[0:1, 2:3],
                            start=True, stop=True), "tm", f"{P}.rep")

        @b4c.vector
        def _(v, it=it, P=P):
            w = vchain(v, E)
            E.wait(v, f"{P}.rep")
            w(v.tensor_copy(MIDC[:, :], psD[:, 0:1]))
            w(v.tensor_scalar(MSKB[:, :], GEB[:, :], MIDC[:, 0:1], None,
                              ALU.is_ge), f"{P}.msk")

        @b4c.tensor
        def _(t, it=it, P=P):
            E.wait(t, f"{P}.msk")
            E.mark(t.matmul(psD[0:1, 256:384], CF["onescol"], MSKB[:, :],
                            start=True, stop=True), "tm", f"{P}.cnt")

        @b4c.vector
        def _(v, it=it, P=P):
            w = vchain(v, E)
            E.wait(v, f"{P}.cnt")
            w(v.tensor_copy(CNTR[:, :], psD[0:1, 256:384]))
            w(v.reduce_sum(BIS[0:1, 3:4],
                           CNTR[0:1, :].rearrange("p (x a) -> p x a", x=1), AX.X))
            w(v.tensor_scalar(BIS[0:1, 4:5], BIS[0:1, 3:4], float(CAP), None,
                              ALU.is_ge))
            w(v.tensor_sub(BIS[0:1, 5:6], BIS[0:1, 2:3], BIS[0:1, 0:1]))
            w(v.tensor_mul(BIS[0:1, 5:6], BIS[0:1, 4:5], BIS[0:1, 5:6]))
            w(v.tensor_add(BIS[0:1, 0:1], BIS[0:1, 0:1], BIS[0:1, 5:6]))
            w(v.tensor_sub(BIS[0:1, 6:7], BIS[0:1, 1:2], BIS[0:1, 2:3]))
            w(v.tensor_mul(BIS[0:1, 6:7], BIS[0:1, 4:5], BIS[0:1, 6:7]))
            w(v.tensor_add(BIS[0:1, 1:2], BIS[0:1, 2:3], BIS[0:1, 6:7]),
              f"{P}.upd")

    cut("p4c")
    b4d = blk()

    @b4d.sync
    def _(e):
        E.wait(e, f"p4.b{NBIS - 1}.upd")
        E.mark(e.dma_start(out=DR["taud"][:, :], in_=BIS[0:1, 0:1]),
               "sd", "p4.tau", 16)

    @b4d.gpsimd
    def _(e):
        E.wait(e, "p4.tau")
        E.mark(e.collective_compute("AllGather", ALU.bypass,
                                    replica_groups=[[0, 1, 2, 3, 4, 5, 6, 7]],
                                    ins=[DR["taud"].tensor.ap()],
                                    outs=[DR["tau8d"].tensor.ap()]), "cc", "p4.tag")

    @b4d.sync
    def _(e):
        E.wait(e, "p4.tag")
        E.mark(e.dma_start(out=TAU8[:, :], in_=DR["tau8d"][:, :]),
               "sd", "p4.t8", 16)

    @b4d.tensor
    def _(t):
        E.wait(t, "p4.t8")
        E.wait(t, "p4.p8")
        E.mark(t.matmul(psD[0:1, 384:512], TAU8[:, :], PAT8S[:, :],
                        start=True, stop=True), "tm", "p4.trow")

    @b4d.vector
    def _(v):
        E.wait(v, "p4.trow")
        E.mark(v.tensor_copy(TROW[:, :], psD[0:1, 384:512]), "vo", "p4.trc")

    @b4d.tensor
    def _(t):
        E.wait(t, "p4.trc")
        E.mark(t.matmul(psD[:, 128:256], CF["onesrow"], TROW[0:1, :],
                        start=True, stop=True), "tm", "p4.tfull")

    @b4d.vector
    def _(v):
        w = vchain(v, E)
        E.wait(v, "p4.tfull")
        w(v.tensor_copy(TAUL[:, :], psD[:, 128:256]))
        w(v.tensor_tensor(KEPTB[:, :], GSB[:, :], TAUL[:, :], ALU.is_ge))
        w(v.tensor_scalar(MSKB[:, :], GSB[:, :], 0.0, None, ALU.is_gt))
        w(v.tensor_mul(KEPTB[:, :], KEPTB[:, :], MSKB[:, :]), "p4.kept")

    KEPT3 = KEPTB[:, :].rearrange("p (a j) -> p a j", j=8)
    for j in range(NE):
        P = f"p4.rk{j}"

        @b4d.vector
        def _(v, j=j, P=P):
            if j == 0:
                E.wait(v, "p4.kept")
            E.mark(v.reduce_sum(
                RTOT[:, :],
                KEPT3[:, :, j].rearrange("p (x a) -> p x a", x=1), AX.X),
                "vo", f"{P}.rt")

        @b4d.tensor
        def _(t, j=j, P=P):
            E.wait(t, f"{P}.rt")
            E.mark(t.matmul(psD[:, 0:1], CF["ustrict"], RTOT[:, :],
                            start=True, stop=True), "tm", f"{P}.ca")

        @b4d.vector
        def _(v, j=j, P=P):
            w = vchain(v, E)
            E.wait(v, f"{P}.ca")
            w(v.tensor_copy(CARR[:, :], psD[:, 0:1]))
            KJ = KEPT3[:, :, j]
            v.memset(PXA[:, 0:1], 0.0)
            w(v.tensor_copy(PXA[:, 1:16], KJ[:, 0:15]))
            cur, oth = PXA, PXB
            for k in (1, 2, 4, 8):
                v.tensor_copy(oth[:, 0:k], cur[:, 0:k])
                w(v.tensor_add(oth[:, k:16], cur[:, k:16], cur[:, 0:16 - k]))
                cur, oth = oth, cur
            w(v.tensor_scalar(cur[:, :], cur[:, :], CARR[:, 0:1], 512.0,
                              ALU.add, ALU.add))
            w(v.scalar_tensor_tensor(RKS[:, j * 16:(j + 1) * 16], KJ,
                                     -512.0, cur[:, :], ALU.mult, ALU.add),
              f"{P}.rk")

    for j in range(NE):
        P = f"p4.tg{j}"

        @b4d.vector
        def _(v, j=j, P=P):
            if j == 0:
                for jj in range(NE):
                    E.wait(v, f"p4.rk{jj}.rk")
            v.tensor_copy(
                TGE[:, :].rearrange("p (a x) -> p a x", x=2)[:, :, 0], TLC[:, :])
            E.mark(v.tensor_copy(
                TGE[:, :].rearrange("p (a x) -> p a x", x=2)[:, :, 1],
                GSB[:, :].rearrange("p (a j) -> p a j", j=8)[:, :, j]),
                "vo", f"{P}.tge")

        for a in range(16):
            @b4d.vector
            def _(v, j=j, a=a, P=P):
                if a == 0:
                    E.wait(v, f"{P}.tge")
                if a >= 2:
                    E.wait(v, f"{P}.mm{a - 2}")
                E.mark(v.tensor_scalar(PPB[:, (a % 2) * 256:(a % 2) * 256 + 256],
                                       IOTH[:, :],
                                       RKS[:, j * 16 + a: j * 16 + a + 1],
                                       None, ALU.is_equal),
                       "vo", f"{P}.pp{a}")

            @b4d.tensor
            def _(t, j=j, a=a, P=P):
                E.wait(t, f"{P}.pp{a}")
                if a == 0 and j > 0:
                    E.wait(t, f"p4.tg{j - 1}.ev")
                t.matmul(psD[:, 0:2], PPB[:, (a % 2) * 256:(a % 2) * 256 + 128],
                         TGE[:, 2 * a: 2 * a + 2],
                         start=(a == 0), stop=(a == 15))
                E.mark(t.matmul(psB[:, 0:2],
                                PPB[:, (a % 2) * 256 + 128:(a % 2) * 256 + 256],
                                TGE[:, 2 * a: 2 * a + 2],
                                start=(a == 0), stop=(a == 15)),
                       "tm", f"{P}.mm{a}")

        @b4d.vector
        def _(v, j=j, P=P):
            E.wait(v, f"{P}.mm15")
            v.tensor_copy(TIDG[:, j * 4: j * 4 + 2], psD[:, 0:2])
            E.mark(v.tensor_copy(TIDG[:, j * 4 + 2: j * 4 + 4], psB[:, 0:2]),
                   "vo", f"{P}.ev")

    @b4d.vector
    def _(v):
        w = vchain(v, E)
        for j in range(NE):
            E.wait(v, f"p4.tg{j}.ev")
        for c in range(16):
            base = (c // 2) * 4 + (c % 2) * 2
            tidc = TIDG[:, base: base + 1]
            gc = TIDG[:, base + 1: base + 2]
            v.tensor_copy(XIDX[:, c:c + 1], tidc)
            w(v.tensor_scalar(ROUT[:, 0:1], gc, 0.0, None, ALU.is_equal))
            w(v.tensor_sub(ROUT[:, 1:2], DMY[:, c:c + 1], tidc))
            w(v.tensor_mul(ROUT[:, 1:2], ROUT[:, 0:1], ROUT[:, 1:2]))
            w(v.tensor_add(ROUT[:, 2:3], tidc, ROUT[:, 1:2]))
            w(v.tensor_copy(YIDX[:, c:c + 1], ROUT[:, 2:3]),
              "p4.idx" if c == 15 else None)

    cut("p4d")
    b4e = blk()
    for c in range(16):
        @b4e.gpsimd
        def _(e, c=c):
            if c == 0:
                E.wait(e, "p4.idx")
                for tt in range(NT):
                    E.wait(e, f"p4.xsd{tt}")
            if c >= 2:
                E.wait(e, f"p4.xc{c - 2}")
            E.mark(e.indirect_dma_start(
                XGB[:, (c % 2) * 1024:(c % 2 + 1) * 1024], None,
                DR["xsd"][:, :], IOA(XIDX[:, c:c + 1], 0)),
                "gs", f"p4.xg{c}", 16)

        @b4e.vector
        def _(v, c=c):
            E.wait(v, f"p4.xg{c}")
            if c == 0:
                E.wait(v, "p4.idx")
            base = (c // 2) * 4 + (c % 2) * 2
            E.mark(v.tensor_scalar(XGB[:, (c % 2) * 1024:(c % 2 + 1) * 1024],
                                   XGB[:, (c % 2) * 1024:(c % 2 + 1) * 1024],
                                   TIDG[:, base + 1: base + 2], None, ALU.mult),
                   "vo", f"p4.xs{c}")

        @b4e.scalar
        def _(s, c=c):
            E.wait(s, f"p4.xs{c}")
            E.mark(s.dma_start(out=DR["xct"][c * 128:(c + 1) * 128, :],
                               in_=XGB[:, (c % 2) * 1024:(c % 2 + 1) * 1024]),
                   "hd", f"p4.xc{c}", 16)

    @b4e.gpsimd
    def _(e):
        for c in range(16):
            E.wait(e, f"p4.xc{c}")
        E.mark(e.collective_compute("AllGather", ALU.bypass,
                                    replica_groups=[[0, 1, 2, 3, 4, 5, 6, 7]],
                                    ins=[DR["xct"].tensor.ap()],
                                    outs=[DR["xctg"].tensor.ap()]), "cc", "p4.xag")

    cut("p4e")
    mo1.close()
    mh2p.close()
    mo3 = ExitStack()
    _stacks.append(mo3)
    XET = mo3.enter_context(nc.sbuf_tensor("XET", [128, 16384], F16))
    W1S = mo3.enter_context(nc.sbuf_tensor("W1S", [128, 16384], F16))
    W2S = mo3.enter_context(nc.sbuf_tensor("W2S", [128, 16384], F16))
    HTB = mo3.enter_context(nc.sbuf_tensor("HTB", [128, 16384], F16))
    YCB = mo3.enter_context(nc.sbuf_tensor("YCB", [128, 2048], F16))
    YB = mo3.enter_context(nc.sbuf_tensor("YB", [128, 2048], F16))
    YBF = mo3.enter_context(nc.sbuf_tensor("YBF", [128, 2048], F32))

    b4f = blk()

    @b4f.gpsimd
    def _(e):
        E.mark(e.dma_start(out=W1S[:, :].rearrange("p (a f) -> p a f", a=8),
                           in_=IN["w1c"].rearrange("(a p) f -> p a f", p=128)),
               "gd", "p4.w1", 16)
        E.mark(e.dma_start(out=W2S[:, :].rearrange("p (a f) -> p a f", a=16),
                           in_=IN["w2c"].rearrange("(a p) f -> p a f", p=128)),
               "gd", "p4.w2", 16)

    @b4f.vector
    def _(v):
        E.wait(v, "p4.w1")
        last = None
        for dc in range(8):
            last = v.tensor_scalar(W1S[:, dc * 2048:(dc + 1) * 2048],
                                   W1S[:, dc * 2048:(dc + 1) * 2048],
                                   CF["g2c"][:, dc:dc + 1], None, ALU.mult)
        E.mark(last, "vo", "p4.w1f")

    @b4f.scalar
    def _(s):
        E.wait(s, "p4.xag")
        rk = s.partition_id() % 8
        snap = dict(E.n)
        ends = []
        for g8 in nc.Switch(engines=[s], index=[rk], n=8):
            E.switch_body_begin(snap)
            for r in range(8):
                for dc in range(8):
                    nm = "p4.xet" if (r == 7 and dc == 7) else None
                    E.mark(s.dma_start(
                        out=XET[:, dc * 2048 + r * 256: dc * 2048 + (r + 1) * 256],
                        in_=DR["xctg"][r * 2048 + g8 * 256:
                                       r * 2048 + (g8 + 1) * 256,
                                       dc * 128:(dc + 1) * 128],
                        transpose=True), "hd", nm, 16)
            ends.append(dict(E.n))
        E.switch_end(ends)

    for Jh in range(2):
        for fc in range(16):
            @b4f.tensor
            def _(t, Jh=Jh, fc=fc):
                if Jh == 0 and fc == 0:
                    E.wait(t, "p4.xet")
                    E.wait(t, "p4.w1f")
                if Jh == 1 and fc == 0:
                    E.wait(t, "p4.y0.7")
                if fc >= 2:
                    E.wait(t, f"p4.r{Jh}.{fc - 2}")
                elif Jh == 1:
                    E.wait(t, f"p4.r0.{14 + fc}")
                last = None
                for dc in range(8):
                    for hv in range(2):
                        last = t.matmul(
                            psA[:, (fc % 2) * 1024 + hv * 512:
                                (fc % 2) * 1024 + (hv + 1) * 512],
                            W1S[:, dc * 2048 + fc * 128: dc * 2048 + (fc + 1) * 128],
                            XET[:, dc * 2048 + Jh * 1024 + hv * 512:
                                dc * 2048 + Jh * 1024 + (hv + 1) * 512],
                            start=(dc == 0), stop=(dc == 7))
                E.mark(last, "tm", f"p4.h{Jh}.{fc}")

            @b4f.scalar
            def _(s, Jh=Jh, fc=fc):
                E.wait(s, f"p4.h{Jh}.{fc}")
                E.mark(s.activation(HTB[:, fc * 1024:(fc + 1) * 1024],
                                    psA[:, (fc % 2) * 1024:(fc % 2 + 1) * 1024],
                                    AF.Relu), "so", f"p4.r{Jh}.{fc}")

        for Js in range(8):
            @b4f.tensor
            def _(t, Jh=Jh, Js=Js):
                if Js == 0:
                    E.wait(t, f"p4.r{Jh}.15")
                    if Jh == 0:
                        E.wait(t, "p4.w2")
                if Jh > 0 or Js > 0:
                    pj, ps_ = (Jh, Js - 1) if Js > 0 else (0, 7)
                    E.wait(t, f"p4.ye{pj}.{ps_}")
                last = None
                for fc in range(16):
                    for hv in range(2):
                        last = t.matmul(
                            psB[:, hv * 512:(hv + 1) * 512],
                            HTB[:, fc * 1024 + Js * 128: fc * 1024 + (Js + 1) * 128],
                            W2S[:, fc * 1024 + hv * 512: fc * 1024 + (hv + 1) * 512],
                            start=(fc == 0), stop=(fc == 15))
                E.mark(last, "tm", f"p4.y{Jh}.{Js}")

            @b4f.vector
            def _(v, Jh=Jh, Js=Js):
                E.wait(v, f"p4.y{Jh}.{Js}")
                c = Jh * 8 + Js
                if c >= 2:
                    E.wait(v, f"p4.yc{c - 2}")
                E.mark(v.tensor_copy(YCB[:, (Js % 2) * 1024:(Js % 2 + 1) * 1024],
                                     psB[:, 0:1024]), "vo", f"p4.ye{Jh}.{Js}")

            @b4f.sync
            def _(e, Jh=Jh, Js=Js):
                E.wait(e, f"p4.ye{Jh}.{Js}")
                c = Jh * 8 + Js
                E.mark(e.dma_start(out=DR["ycd"][c * 128:(c + 1) * 128, :],
                                   in_=YCB[:, (Js % 2) * 1024:(Js % 2 + 1) * 1024]),
                       "sd", f"p4.yc{c}", 16)

    @b4f.gpsimd
    def _(e):
        for c in range(16):
            E.wait(e, f"p4.yc{c}")
        E.mark(e.collective_compute("AllGather", ALU.bypass,
                                    replica_groups=[[0, 1, 2, 3, 4, 5, 6, 7]],
                                    ins=[DR["ycd"].tensor.ap()],
                                    outs=[DR["ycg"].tensor.ap()]), "cc", "p4.yag")

    cut("p4f")
    for c in range(16):
        @b4f.gpsimd
        def _(e, c=c):
            if c == 0:
                E.wait(e, "p4.yag")
                E.wait(e, "p4.yrw")
            if c >= 2:
                E.wait(e, f"p4.ybf{c - 2}")
            E.mark(e.indirect_dma_start(
                YB[:, (c % 2) * 1024:(c % 2 + 1) * 1024], None,
                DR["ycg"][:, :], IOA(YRW[:, c:c + 1], 0)),
                "gs", f"p4.yg{c}", 16)

        @b4f.vector
        def _(v, c=c):
            E.wait(v, f"p4.yg{c}")
            if c >= 2:
                E.wait(v, f"p4.ysc{c - 2}")
            E.mark(v.tensor_copy(YBF[:, (c % 2) * 1024:(c % 2 + 1) * 1024],
                                 YB[:, (c % 2) * 1024:(c % 2 + 1) * 1024]),
                   "vo", f"p4.ybf{c}")

        @b4f.gpsimd
        def _(e, c=c):
            E.wait(e, f"p4.ybf{c}")
            if c == 0:
                E.wait(e, "p4.yz")
            else:
                E.wait(e, f"p4.ysc{c - 1}")
            E.mark(e.indirect_dma_start(
                DR["ystage"][:, :], IOA(YIDX[:, c:c + 1], 0),
                YBF[:, (c % 2) * 1024:(c % 2 + 1) * 1024], None,
                oob_is_err=True, compute_op=ALU.add),
                "gs", f"p4.ysc{c}", 16)

    cut("p4g")
    mo3.close()
    mo4 = ExitStack()
    _stacks.append(mo4)
    F1 = mo4.enter_context(nc.sbuf_tensor("F1", [128, 2048], F32))
    F2 = mo4.enter_context(nc.sbuf_tensor("F2", [128, 2048], F32))
    F3 = mo4.enter_context(nc.sbuf_tensor("F3", [128, 2048], F32))
    OSB = mo4.enter_context(nc.sbuf_tensor("OSB", [128, 2048], F32))
    JNK2 = mo4.enter_context(nc.sbuf_tensor("JNK2", [128, 1024], F32))

    b5 = blk()
    for tt in range(NT):
        P = f"p5.{tt}"
        buf = tt % 2
        fsl = F1[:, buf * 1024:(buf + 1) * 1024]
        osl = OSB[:, buf * 1024:(buf + 1) * 1024]
        base = 592 + buf * 16

        f2sl = F2[:, buf * 1024:(buf + 1) * 1024]
        f3sl = F3[:, buf * 1024:(buf + 1) * 1024]

        @b5.sync
        def _(e, tt=tt, P=P, fsl=fsl):
            if tt == 0:
                E.wait(e, "p4.ysc15")
            if tt >= 2:
                E.wait(e, f"p5.{tt - 2}.x")
            E.mark(e.dma_start(out=fsl, in_=X1F[tt * 128:(tt + 1) * 128, :]),
                   "sd", f"{P}.l0", 16)

        @b5.scalar
        def _(s, tt=tt, P=P, f2sl=f2sl):
            if tt == 0:
                E.wait(s, "p4.ysc15")
            if tt >= 2:
                E.wait(s, f"p5.{tt - 2}.x")
            E.mark(s.dma_start(out=f2sl,
                               in_=DR["PREd"][tt * 128:(tt + 1) * 128, :]),
                   "hd", f"{P}.l1", 16)

        @b5.gpsimd
        def _(e, tt=tt, P=P, f3sl=f3sl):
            if tt == 0:
                E.wait(e, "p4.ysc15")
            if tt >= 2:
                E.wait(e, f"p5.{tt - 2}.x")
            E.mark(e.dma_start(out=f3sl,
                               in_=DR["ystage"][tt * 128:(tt + 1) * 128, :]),
                   "gd", f"{P}.l2", 16)

        @b5.vector
        def _(v, tt=tt, P=P, fsl=fsl, f2sl=f2sl, f3sl=f3sl, base=base):
            w = vchain(v, E)
            E.wait(v, f"{P}.l0")
            E.wait(v, f"{P}.l1")
            E.wait(v, f"{P}.l2")
            w(v.tensor_add(fsl, fsl, f2sl))
            w(v.tensor_add(fsl, fsl, f3sl), f"{P}.r")
            E.mark(v.reduce_sum(SMS[:, base:base + 1],
                                fsl.rearrange("p (x a) -> p x a", x=1), AX.X),
                   "vo", f"{P}.sum")

        @b5.scalar
        def _(s, tt=tt, P=P, fsl=fsl, base=base):
            E.wait(s, f"{P}.sum")
            E.mark(s.activation(JNK2[:, :], fsl, AF.Square,
                                accum_out=SMS[:, base + 1: base + 2]),
                   "so", f"{P}.ssq")

        @b5.vector
        def _(v, tt=tt, P=P, base=base):
            w = vchain(v, E)
            E.wait(v, f"{P}.ssq")
            v.tensor_scalar(SMS[:, base + 2: base + 3], SMS[:, base: base + 1],
                            1.0 / 1024.0, None, ALU.mult)
            w(v.tensor_scalar(SMS[:, base + 3: base + 4],
                              SMS[:, base + 1: base + 2], 1.0 / 1024.0, None,
                              ALU.mult))
            w(v.tensor_mul(SMS[:, base + 4: base + 5], SMS[:, base + 2: base + 3],
                           SMS[:, base + 2: base + 3]))
            w(v.tensor_sub(SMS[:, base + 3: base + 4], SMS[:, base + 3: base + 4],
                           SMS[:, base + 4: base + 5]))
            w(v.tensor_scalar(SMS[:, base + 3: base + 4],
                              SMS[:, base + 3: base + 4], 1.0, 1e-5,
                              ALU.mult, ALU.add), f"{P}.var")

        @b5.scalar
        def _(s, tt=tt, P=P, base=base):
            E.wait(s, f"{P}.var")
            E.mark(s.activation(SMS[:, base + 5: base + 6],
                                SMS[:, base + 3: base + 4], AF.Sqrt),
                   "so", f"{P}.sdv")

        @b5.vector
        def _(v, tt=tt, P=P, fsl=fsl, osl=osl, base=base):
            E.wait(v, f"{P}.sdv")
            rc = E.mark(v.reciprocal(SMS[:, base + 6: base + 7],
                                     SMS[:, base + 5: base + 6]), "vo")
            v.wait_ge(E.h["vo"], rc)
            if tt >= 2:
                E.wait(v, f"p5.{tt - 2}.o")
            E.mark(v.tensor_scalar(osl, fsl, SMS[:, base + 2: base + 3],
                                   SMS[:, base + 6: base + 7],
                                   ALU.subtract, ALU.mult), "vo", f"{P}.x")

        @b5.sync
        def _(e, tt=tt, P=P, osl=osl):
            E.wait(e, f"{P}.x")
            E.mark(e.dma_start(out=OUT[tt * 128:(tt + 1) * 128, :], in_=osl),
                   "sd", f"{P}.o", 16)

    bfin = blk()

    @bfin.sync
    def _(e):
        for snm in E.n:
            if snm != "sd":
                e.wait_ge(E.h[snm], E.n[snm])
        e.wait_ge(E.h["sd"], E.n["sd"])

    if _cur_blk[0] is not None:
        _cur_blk[0].__exit__(None, None, None)
        _cur_blk[0] = None
    mo4.close()
    mo2.close()
    st.close()
    return nc


def build_program():
    try:
        return _build_program_inner(STOP_AFTER)
    except _StopEmission as e:
        return e.args[0]


def kernel(**inputs):
    in_maps = make_in_maps(inputs)
    nc = build_program()
    declared = set(nc._lazy_in.declared)
    in_maps = [{k: v for k, v in m.items() if k in declared} for m in in_maps]
    res = run_bass_kernel_spmd(nc, in_maps, core_ids=list(range(8)))
    return assemble_output(res.results)


if __name__ == "__main__":
    build_program()
    print("built OK")


# version 04 (final)
# speedup vs baseline: 1.1537x; 1.0068x over previous
from contextlib import ExitStack

import numpy as np
import concourse.bass as bass
import concourse.mybir as mybir
from concourse.bass_utils import run_bass_kernel_spmd

F32 = mybir.dt.float32
BF16 = mybir.dt.bfloat16
F16 = mybir.dt.float16
U32 = mybir.dt.uint32
IOA = bass.IndirectOffsetOnAxis
FP16 = mybir.dt.float16
I16 = mybir.dt.int16
I32 = mybir.dt.int32
AF = mybir.ActivationFunctionType
ALU = mybir.AluOpType
AX = mybir.AxisListType

D = 1024
TC = 2048
NT = 16
HA, DK, MA = 4, 256, 64
HM, HD = 16, 64
FF = 2048
NE, SLOTS = 8, 256
CAP = 1280
SCALE_A = float(DK) ** -0.5

DEBUG_STAGE = 3
RACE_DETECT = True
PH0_MIN = 0
PROBE_SWITCH = None
DUMP = None
DUMPROWS = 0
STOP_AFTER = None


class _StopEmission(Exception):
    pass


def _chunks_of(core):
    g = core % 4
    return (g, 7 - g)


def _host_constants(core):
    cA, cB = _chunks_of(core)
    pos = np.concatenate([np.arange(cA * 1024, (cA + 1) * 1024),
                          np.arange(cB * 1024, (cB + 1) * 1024)])
    half = DK // 2
    inv = 1.0 / 10000.0 ** (np.arange(half, dtype=np.float64) / half)
    ang = pos[None, :] * inv[:, None]
    cosA = np.cos(ang).astype(np.float32)
    sinA = np.sin(ang).astype(np.float32)
    theta = 10000.0 ** (-2.0 * (np.arange(HD // 2, dtype=np.float64) / HD))
    ang2 = pos[None, :] * theta[:, None]
    mc_, ms_ = np.cos(ang2), np.sin(ang2)
    mcos = np.zeros((HD, TC), np.float32)
    msin = np.zeros((HD, TC), np.float32)
    mcos[0::2] = mc_
    mcos[1::2] = mc_
    msin[0::2] = -ms_
    msin[1::2] = ms_
    mcos2 = np.tile(mcos, (2, 1))
    msin2 = np.tile(msin, (2, 1))
    ii, jj = np.indices((128, 128))
    tri = (ii <= jj).astype(np.float32)
    ustrict = (ii < jj).astype(np.float32)
    negm = np.where(ii > jj, -1e30, 0.0).astype(np.float32)
    ident = np.eye(128, dtype=np.float32)
    perm = np.zeros((128, 128), np.float32)
    for i in range(64):
        perm[2 * i, 2 * i + 1] = 1.0
        perm[2 * i + 1, 2 * i] = 1.0
    iota256 = np.tile(np.arange(256, dtype=np.float32)[None, :], (128, 1))
    idp = np.arange(1, TC + 1, dtype=np.int64).reshape(NT, 128).T
    hib = (idp >> 8).astype(np.float32)
    lob = (idp & 255).astype(np.float32)
    konst = np.tile(np.array([[1e-5, 1.0 / 256.0, 0.0, 0.0]], np.float32), (128, 1))
    onesrow = np.ones((1, 128), np.float32)
    onescol = np.ones((128, 1), np.float32)
    maskA = np.zeros((128, 8), np.float32)
    maskAB = np.zeros((128, 8), np.float32)
    maskA[:, :cA] = 1.0
    maskAB[:, cA + 1:cB] = 1.0
    iotah = np.tile(np.arange(256, dtype=np.float16)[None, :], (128, 1))
    tloch = (np.arange(16)[None, :] * 128 + np.arange(128)[:, None]).astype(np.float16)
    dumy = (2048.0 + (np.arange(16)[None, :] % 2) * 128
            + np.arange(128)[:, None]).astype(np.float32)
    moemask = np.zeros((128, 1024), np.float32)
    moemask.reshape(128, 128, 8)[:, :, core] = 1.0
    ee = np.arange(16)[None, :] // 2
    yrows = (ee * 2048 + core * 256 + (np.arange(16)[None, :] % 2) * 128
             + np.arange(128)[:, None]).astype(np.uint32)
    pat8 = np.zeros((8, 128), np.float32)
    pat8.reshape(8, 16, 8)[np.arange(8), :, np.arange(8)] = 1.0
    return dict(iotah=iotah, tloch=tloch, dumy=dumy, moemask=moemask,
                yrows=yrows, pat8=pat8,
                cosA=cosA, sinA=sinA, mcos=mcos, msin=msin, tri=tri,
                ustrict=ustrict, negm=negm, ident=ident, perm=perm,
                iota256=iota256, hib=hib, lob=lob, onesrow=onesrow,
                onescol=onescol, maskA=maskA, maskAB=maskAB, konst=konst,
                mcos2=mcos2, msin2=msin2)


def make_in_maps(inputs):
    x = np.asarray(inputs["L0_attn_hidden_states"], np.float32)
    maps = []
    for core in range(8):
        b = core // 4
        cA, cB = _chunks_of(core)
        xt = np.concatenate([x[b, cA * 1024:(cA + 1) * 1024],
                             x[b, cB * 1024:(cB + 1) * 1024]], axis=0)
        m = {"xt": np.ascontiguousarray(xt), "xT": np.ascontiguousarray(xt.T)}
        ren = {"wq": "L0_attn_Wq", "wk": "L0_attn_Wk", "wv": "L0_attn_Wv",
               "wg": "L0_attn_Wg", "wo": "L0_attn_Wo", "skw1": "L0_attn_sk_w1",
               "skw2": "L0_attn_sk_w2", "svw1": "L0_attn_sv_w1",
               "svw2": "L0_attn_sv_w2", "wqkv": "L0_ffn_w_qkv",
               "wo2": "L0_ffn_w_o", "wrout": "L0_ffn_w_router",
               "w1": "L0_ffn_w1", "w2": "L0_ffn_w2"}
        for k, v in ren.items():
            m[k] = np.asarray(inputs[v], np.float32)
        m["skb2"] = np.asarray(inputs["L0_attn_sk_b2"], np.float32).reshape(1, 256)
        m["svb2"] = np.asarray(inputs["L0_attn_sv_b2"], np.float32).reshape(1, 256)
        m["gnwc"] = np.ascontiguousarray(np.asarray(inputs["L0_attn_gnorm_w"], np.float32).reshape(2, 128).T)
        m["g1c"] = np.ascontiguousarray(np.asarray(inputs["L0_ffn_g1"], np.float32).reshape(8, 128).T)
        m["g2c"] = np.ascontiguousarray(np.asarray(inputs["L0_ffn_g2"], np.float32).reshape(8, 128).T)
        m["w1c"] = np.ascontiguousarray(np.asarray(inputs["L0_ffn_w1"], np.float32)[core])
        m["w2c"] = np.ascontiguousarray(np.asarray(inputs["L0_ffn_w2"], np.float32)[core])
        for k, v in _host_constants(core).items():
            m["c_" + k] = v
        maps.append(m)
    return maps


def assemble_output(results):
    out = np.zeros((2, 8192, D), np.float32)
    for core in range(8):
        b = core // 4
        cA, cB = _chunks_of(core)
        o = np.asarray(results[core]["out"], np.float32)
        out[b, cA * 1024:(cA + 1) * 1024] = o[:1024]
        out[b, cB * 1024:(cB + 1) * 1024] = o[1024:]
    return out


class Ev:

    def __init__(self, nc, stack):
        names = ["sd", "hd", "gd", "tm", "vo", "so", "go", "cc", "zd", "fa", "gs"]
        self.h = {n: stack.enter_context(nc.semaphore(n)) for n in names}
        self.n = {n: 0 for n in names}
        self.ev = {}
        self.pre = set()

    def declare(self, name, sem, val):
        assert name not in self.ev, f"declare of existing event {name}"
        self.ev[name] = (sem, val)
        self.pre.add(name)

    def mark(self, inst, sem, name=None, k=1):
        inst.then_inc(self.h[sem], k)
        self.n[sem] += k
        if name is not None:
            if name in self.ev:
                assert getattr(self, "in_switch", False) or name in self.pre, \
                    f"duplicate event {name}"
                assert self.ev[name] == (sem, self.n[sem]), \
                    f"event value mismatch {name}: {self.ev[name]} vs {(sem, self.n[sem])}"
            self.ev[name] = (sem, self.n[sem])
        return self.n[sem]

    def switch_body_begin(self, snap):
        self.in_switch = True
        self.n = dict(snap)

    def switch_end(self, end_counts):
        assert all(e == end_counts[0] for e in end_counts), end_counts
        self.in_switch = False
        self.n = dict(end_counts[0])

    def wait(self, eng, name):
        sem, val = self.ev[name]
        eng.wait_ge(self.h[sem], val)

    def wait_all(self, eng, *names):
        for nm in names:
            self.wait(eng, nm)


def vchain(v, E):
    def w(inst, name=None):
        val = E.mark(inst, "vo", name)
        v.wait_ge(E.h["vo"], val)
        return inst
    return w


def mm_bank_group(t, mms):
    first = {}
    last = {}
    for i, (_, _, _, bk) in enumerate(mms):
        if bk not in first:
            first[bk] = i
        last[bk] = i
    inst = None
    for i, (o, l, r, bk) in enumerate(mms):
        inst = t.matmul(o, l, r, start=(first[bk] == i), stop=(last[bk] == i))
    return inst


def _build_program_inner(cut_point):
    nc = bass.Bass("TRN2", target_bir_lowering=False,
                   detect_race_conditions=RACE_DETECT)
    nc.cache_partition_id()
    st = ExitStack()

    def din(name, shape, dtype=F32):
        return nc.dram_tensor(name, list(shape), dtype, kind="ExternalInput").ap()

    class LazyIn:
        def __init__(self):
            self.shapes = {}
            self.dtypes = {}
            self.declared = {}

        def __getitem__(self, nm):
            if nm not in self.declared:
                self.declared[nm] = din(nm, self.shapes[nm],
                                        self.dtypes.get(nm, F32))
            return self.declared[nm]

    IN = LazyIn()
    for nm, shp in [
        ("xt", (TC, D)), ("xT", (D, TC)),
        ("wq", (D, D)), ("wk", (D, D)), ("wv", (D, D)), ("wg", (D, D)), ("wo", (D, D)),
        ("skw1", (D, 16)), ("skw2", (16, 256)), ("skb2", (1, 256)),
        ("svw1", (D, 16)), ("svw2", (16, 256)), ("svb2", (1, 256)),
        ("gnwc", (128, 2)), ("g1c", (128, 8)), ("wqkv", (D, 3 * D)), ("wo2", (D, D)),
        ("g2c", (128, 8)), ("wrout", (D, NE)), ("w1", (NE, D, FF)), ("w2", (NE, FF, D)),
        ("c_cosA", (128, TC)), ("c_sinA", (128, TC)),
        ("c_mcos", (HD, TC)), ("c_msin", (HD, TC)),
        ("c_tri", (128, 128)), ("c_ustrict", (128, 128)), ("c_negm", (128, 128)),
        ("c_ident", (128, 128)), ("c_perm", (128, 128)),
        ("c_iota256", (128, 256)), ("c_hib", (128, NT)), ("c_lob", (128, NT)),
        ("c_onesrow", (1, 128)), ("c_onescol", (128, 1)),
        ("c_maskA", (128, 8)), ("c_maskAB", (128, 8)), ("c_konst", (128, 4)),
        ("c_mcos2", (128, TC)), ("c_msin2", (128, TC)),
        ("c_iotah", (128, 256)), ("c_tloch", (128, 16)), ("c_dumy", (128, 16)),
        ("c_moemask", (128, 1024)), ("c_yrows", (128, 16)), ("c_pat8", (8, 128)),
        ("w1c", (D, FF)), ("w2c", (FF, D)),
    ]:
        IN.shapes[nm] = shp
    IN.dtypes["c_iotah"] = F16
    IN.dtypes["c_tloch"] = F16
    IN.dtypes["c_yrows"] = U32
    nc._lazy_in = IN

    OUT = nc.dram_tensor("out", [TC, D], F32, kind="ExternalOutput").ap()

    def dint(name, shape, dtype=F32):
        return nc.dram_tensor(name, list(shape), dtype, kind="Internal").ap()

    def dsh(name, shape, dtype=F32):
        return nc.dram_tensor(name, list(shape), dtype, kind="Internal",
                              addr_space="Shared").ap()

    DR = {} if PH0_MIN >= 1 else {
        "st_in": dint("st_in", (256, 2048)),
        "st_out": dint("st_out", (1024, 2048)),
        "vtimed": dint("vtimed", (TC, D), BF16),
        "ofinb": dint("ofinb", (TC, D), BF16),
        "x1b": dint("x1b", (TC, D), BF16),
        "gsild": dint("gsild", (TC, D), BF16),
        "qmhad": dint("qmhad", (D, TC), BF16),
        "kv_in": dint("kv_in", (128, 33024), BF16),
        "kv_out": dint("kv_out", (512, 33024), BF16),
        "ybT": dint("ybT", (D, TC), BF16),
        "PREd": dint("PREd", (TC, D)),
        "xsd": dint("xsd", (TC, D), F16),
        "gdram": dint("gdram", (TC, NE)),
        "gag": dsh("gag", (8 * TC, NE)),
        "taud": dint("taud", (1, 1)),
        "tau8d": dsh("tau8d", (8, 1)),
        "xct": dint("xct", (TC, D), F16),
        "xctg": dsh("xctg", (8 * TC, D), F16),
        "ycd": dint("ycd", (TC, D), F16),
        "ycg": dsh("ycg", (8 * TC, D), F16),
        "ystage": dint("ystage", (2304, D), F16),
    }

    E = Ev(nc, st)
    SEM = E.h

    CST = st.enter_context(nc.sbuf_tensor("CST", [128, 3200], BF16))
    CV = {
        "tri": CST[:, 0:128], "negm": CST[:, 128:256],
        "perm": CST[:, 256:384], "iota256": CST[:, 384:640],
        "identb": CST[:, 640:768],
        "mcos": CST[0:64, 768:2816], "msin": CST[64:128, 768:2816],
        "hib": CST[:, 2816:2832], "lob": CST[:, 2832:2848],
        "onesrowb": CST[0:1, 2848:2976], "onescolb": CST[:, 2976:2977],
    }
    CSTF = st.enter_context(nc.sbuf_tensor("CSTF", [128, 1312], F32))
    CF = {
        "ident": CSTF[:, 0:128], "ustrict": CSTF[:, 128:256],
        "onesrow": CSTF[0:1, 256:384],
        "gnwc": CSTF[:, 384:386], "g1c": CSTF[:, 386:394], "g2c": CSTF[:, 394:402],
        "skb2": CSTF[0:1, 402:658], "svb2": CSTF[0:1, 658:914],
        "trif": CSTF[:, 914:1042], "onescol": CSTF[:, 1042:1043],
        "maskA": CSTF[:, 1044:1052], "maskAB": CSTF[:, 1052:1060],
        "konst": CSTF[:, 1060:1064],
    }
    nc.const_aps.register(F32, 1e-5, CF["konst"][:, 0:1])
    nc.const_aps.register(F32, 1.0 / 256.0, CF["konst"][:, 1:2])
    SMS = st.enter_context(nc.sbuf_tensor("SMS", [128, 1024], F32))
    STATE = st.enter_context(nc.sbuf_tensor("STATE", [128, 2048], F32))
    STATEB = st.enter_context(nc.sbuf_tensor("STATEB", [128, 1536], BF16))

    psA = st.enter_context(nc.psum_tensor("psA", [128, 2048], F32))
    psB = st.enter_context(nc.psum_tensor("psB", [128, 1024], F32))
    psC = st.enter_context(nc.psum_tensor("psC", [128, 1024], BF16))
    psD = st.enter_context(nc.psum_tensor("psD", [128, 512], F32))

    _cur_blk = [None]
    _stacks = []

    def blk():
        if _cur_blk[0] is not None:
            _cur_blk[0].__exit__(None, None, None)
        bb = nc.Block()
        bb.__enter__()
        _cur_blk[0] = bb
        return bb

    def finish_and_stop():
        bf = blk()

        if DUMP is not None and DUMP.startswith("dr:"):
            dd = DR[DUMP[3:]]

            @bf.gpsimd
            def _(e):
                for snm in E.n:
                    if snm != "gd":
                        e.wait_ge(E.h[snm], E.n[snm])
                e.wait_ge(E.h["gd"], E.n["gd"])
                nr = DUMPROWS if DUMPROWS else dd.shape[0]
                E.mark(e.dma_start(out=OUT[0:nr, :], in_=dd[0:nr, :]), "gd", None, 16)
                e.wait_ge(E.h["gd"], E.n["gd"])
        elif DUMP is not None:
            dump_buf = nc._dumpables[DUMP]

            @bf.gpsimd
            def _(e):
                for snm in E.n:
                    if snm != "gd":
                        e.wait_ge(E.h[snm], E.n[snm])
                e.wait_ge(E.h["gd"], E.n["gd"])
                np_, nf = dump_buf.shape[0], min(dump_buf.shape[1], 16384)
                if nf >= 1024:
                    na = nf // 1024
                    E.mark(e.dma_start(
                        out=OUT[:, :].rearrange("(a p) d -> p a d", p=128)[0:np_, 0:na, :],
                        in_=dump_buf[0:np_, 0:na * 1024].rearrange("p (a d) -> p a d", a=na)),
                        "gd", None, 16)
                else:
                    E.mark(e.dma_start(out=OUT[0:np_, 0:nf],
                                       in_=dump_buf[0:np_, 0:nf]),
                           "gd", None, 16)
                e.wait_ge(E.h["gd"], E.n["gd"])

        @bf.sync
        def _(e):
            for snm in E.n:
                if snm != "sd" and not (PH0_MIN >= 2 and E.n[snm] == 0):
                    e.wait_ge(E.h[snm], E.n[snm])
            if DUMP is None:
                for tt in range(NT):
                    E.mark(e.dma_start(out=OUT[tt * 128:(tt + 1) * 128, :],
                                       in_=SMS[:, 0:1024]), "sd", None, 16)
            e.wait_ge(E.h["sd"], E.n["sd"])
        if _cur_blk[0] is not None:
            _cur_blk[0].__exit__(None, None, None)
            _cur_blk[0] = None
        for stk in reversed(_stacks):
            stk.close()
        st.close()
        raise _StopEmission(nc)

    def cut(name):
        if PROBE_SWITCH == name:
            bps = blk()

            @bps.tensor
            def _(t):
                rk = t.partition_id() % 4
                for g in nc.Switch(engines=[t], index=[rk], n=4):
                    t.matmul(psA[:, 0:128], CV["identb"], CV["negm"],
                             start=True, stop=True)
            print(f"PROBE_SWITCH at {name}: build OK")
            raise _StopEmission(nc)
        if cut_point == name:
            finish_and_stop()

    att = ExitStack()
    _stacks.append(att)
    QT = att.enter_context(nc.sbuf_tensor("QT", [128, 16384], BF16))
    KT = att.enter_context(nc.sbuf_tensor("KT", [128, 16384], BF16))
    WB = att.enter_context(nc.sbuf_tensor("WB", [128, 8192], BF16))
    WKV = att.enter_context(nc.sbuf_tensor("WKV", [128, 8192], BF16))
    SVT = att.enter_context(nc.sbuf_tensor("SVT", [64, 8192], BF16))
    VS = att.enter_context(nc.sbuf_tensor("VS", [128, 2048], BF16))
    SSC = att.enter_context(nc.sbuf_tensor("SSC", [128, 4864], BF16))
    _dumpables = {}
    nc._dumpables = _dumpables
    att2 = ExitStack()
    _stacks.append(att2)
    ROPET = att2.enter_context(nc.sbuf_tensor("ROPET", [128, 4096], BF16))
    XA = att2.enter_context(nc.sbuf_tensor("XA", [128, 16384], BF16))
    S1S = att2.enter_context(nc.sbuf_tensor("S1S", [16, 4096], BF16))

    b = blk()

    @b.sync
    def _(e):
        last = None
        for src, dst in [
            ("c_ident", CF["ident"]), ("c_ustrict", CF["ustrict"]),
            ("c_onesrow", CF["onesrow"]), ("skb2", CF["skb2"]),
            ("svb2", CF["svb2"]), ("c_tri", CF["trif"]),
            ("c_onescol", CF["onescol"]), ("c_maskA", CF["maskA"]),
            ("c_maskAB", CF["maskAB"]), ("c_konst", CF["konst"]),
        ]:
            last = e.dma_start(out=dst, in_=IN[src][:, :])
            E.mark(last, "sd", None, 16)
        E.mark(e.dma_start(out=CF["gnwc"], in_=IN["gnwc"][:, :]), "sd", None, 16)
        E.mark(e.dma_start(out=CF["g1c"], in_=IN["g1c"][:, :]), "sd", None, 16)
        E.mark(e.dma_start(out=CF["g2c"], in_=IN["g2c"][:, :]), "sd", "constsF", 16)

    @b.gpsimd
    def _(e):
        for src, dst in [
            ("c_cosA", ROPET[:, 0:2048]), ("c_sinA", ROPET[:, 2048:4096]),
            ("c_tri", CV["tri"]), ("c_negm", CV["negm"]), ("c_perm", CV["perm"]),
            ("c_iota256", CV["iota256"]), ("c_ident", CV["identb"]),
            ("c_mcos", CV["mcos"]), ("c_msin", CV["msin"]),
            ("c_hib", CV["hib"]), ("c_lob", CV["lob"]),
            ("c_onesrow", CV["onesrowb"]), ("c_onescol", CV["onescolb"]),
        ]:
            E.mark(e.dma_start(out=dst, in_=IN[src][:, :]), "gd", None, 16)
        E.mark(e.dma_start(out=XA[:, :].rearrange("p (a t) -> p a t", a=8),
                           in_=IN["xT"].rearrange("(a p) t -> p a t", p=128)),
               "gd", "xa", 16)

    cut("ph0")
    def load_w(tag, dram_ap, wait_evs=()):
        bw = blk()

        @bw.gpsimd
        def _(e):
            for evn in wait_evs:
                E.wait(e, evn)
            E.mark(e.dma_start(out=WB[:, 0:8192].rearrange("p (a o) -> p a o", a=8),
                               in_=dram_ap.rearrange("(a p) o -> p a o", p=128)),
                   "gd", f"w.{tag}", 16)
        return f"w.{tag}"

    def proj_dim(tag, dst_sb, w_ev, extra_evs=()):
        bq = blk()
        items = [(oc, ts) for oc in range(8) for ts in range(4)]
        for i, (oc, ts) in enumerate(items):
            @bq.tensor
            def _(t, i=i, oc=oc, ts=ts):
                if i == 0:
                    E.wait(t, w_ev)
                    for evn in extra_evs:
                        E.wait(t, evn)
                ps = psA[:, (i % 4) * 512:(i % 4 + 1) * 512]
                if i >= 4:
                    E.wait(t, f"{tag}.ev.{i - 4}")
                last = None
                for dc in range(8):
                    last = t.matmul(
                        ps,
                        WB[:, dc * 1024 + oc * 128: dc * 1024 + oc * 128 + 128],
                        XA[:, dc * 2048 + ts * 512: dc * 2048 + ts * 512 + 512],
                        start=(dc == 0), stop=(dc == 7))
                E.mark(last, "tm", f"{tag}.mm.{i}")

            @bq.vector
            def _(v, i=i, oc=oc, ts=ts):
                E.wait(v, f"{tag}.mm.{i}")
                ps = psA[:, (i % 4) * 512:(i % 4 + 1) * 512]
                E.mark(v.tensor_copy(
                    dst_sb[:, oc * 2048 + ts * 512: oc * 2048 + ts * 512 + 512], ps),
                    "vo", f"{tag}.ev.{i}")
        return f"{tag}.ev.{len(items) - 1}"

    wq_ev = load_w("wq", IN["wq"], wait_evs=("xa",))
    q_done = proj_dim("qproj", QT, wq_ev, extra_evs=("constsF", "xa"))
    wk_ev = load_w("wk", IN["wk"], wait_evs=(q_done,))
    k_done = proj_dim("kproj", KT, wk_ev)

    def rope_attn(tag, buf, ready_ev):
        br = blk()

        @br.vector
        def _(v):
            E.wait(v, ready_ev)
            last = None
            for h in range(HA):
                a = buf[:, (2 * h) * 2048:(2 * h + 1) * 2048]
                bb = buf[:, (2 * h + 1) * 2048:(2 * h + 2) * 2048]
                t1 = SSC[:, 0:2048]
                t2 = SSC[:, 2048:4096]
                v.tensor_mul(t1, a, ROPET[:, 0:2048])
                v.tensor_mul(t2, bb, ROPET[:, 2048:4096])
                v.tensor_mul(bb, bb, ROPET[:, 0:2048])
                v.tensor_mul(a, a, ROPET[:, 2048:4096])
                v.tensor_add(bb, bb, a)
                last = v.tensor_sub(a, t1, t2)
            E.mark(last, "vo", f"rope.{tag}")
        return f"rope.{tag}"

    _dumpables.update(QT=QT, KT=KT, WKV=WKV, SVT=SVT, STATE=STATE, GSILV=VS)
    cut("qk")
    ropeq_ev = rope_attn("q", QT, k_done)
    ropek_ev = rope_attn("k", KT, ropeq_ev)

    def proj_time(tag, w_ev, consumer):
        bt = blk()
        for tt in range(NT):
            @bt.tensor
            def _(t, tt=tt):
                if tt == 0:
                    E.wait(t, w_ev)
                if tt >= 2:
                    E.wait(t, f"{tag}.ev.{tt - 2}")
                ps = psB[:, :] if tt % 2 == 0 else psA[:, 1024:2048]
                last = None
                for dc in range(8):
                    for half in range(2):
                        last = t.matmul(
                            ps[:, half * 512:(half + 1) * 512],
                            XA[:, dc * 2048 + tt * 128: dc * 2048 + tt * 128 + 128],
                            WB[:, dc * 1024 + half * 512: dc * 1024 + (half + 1) * 512],
                            start=(dc == 0), stop=(dc == 7))
                E.mark(last, "tm", f"{tag}.mm.{tt}")

            if consumer == "v":
                @bt.vector
                def _(v, tt=tt):
                    E.wait(v, f"{tag}.mm.{tt}")
                    if tt >= 2:
                        E.wait(v, f"{tag}.st.{tt - 2}")
                    ps = psB[:, :] if tt % 2 == 0 else psA[:, 1024:2048]
                    E.mark(v.tensor_copy(VS[:, (tt % 2) * 1024:(tt % 2 + 1) * 1024], ps),
                           "vo", f"{tag}.ev.{tt}")

                @bt.scalar
                def _(s, tt=tt):
                    E.wait(s, f"{tag}.ev.{tt}")
                    E.mark(s.dma_start(out=DR["vtimed"][tt * 128:(tt + 1) * 128, :],
                                       in_=VS[:, (tt % 2) * 1024:(tt % 2 + 1) * 1024]),
                           "hd", f"{tag}.st.{tt}", 16)
            else:
                @bt.scalar
                def _(s, tt=tt):
                    E.wait(s, f"{tag}.mm.{tt}")
                    if tt >= 2:
                        E.wait(s, f"{tag}.st.{tt - 2}")
                    ps = psB[:, :] if tt % 2 == 0 else psA[:, 1024:2048]
                    E.mark(s.activation(VS[:, (tt % 2) * 1024:(tt % 2 + 1) * 1024],
                                        ps, AF.Silu), "so", f"{tag}.ev.{tt}")

                @bt.sync
                def _(e, tt=tt):
                    E.wait(e, f"{tag}.ev.{tt}")
                    E.mark(e.dma_start(out=DR["gsild"][tt * 128:(tt + 1) * 128, :],
                                       in_=VS[:, (tt % 2) * 1024:(tt % 2 + 1) * 1024]),
                           "sd", f"{tag}.st.{tt}", 16)
        return f"{tag}.ev.{NT - 1}", f"{tag}.st.{NT - 1}"

    cut("rope")
    wv_ev = load_w("wv", IN["wv"], wait_evs=(ropek_ev,))
    _, vt_done = proj_time("vproj", wv_ev, "v")
    wg_ev = load_w("wg", IN["wg"], wait_evs=("vproj.ev.15",))
    gs_done, gst_done = proj_time("gproj", wg_ev, "g")

    cut("ph1b")
    bw = blk()

    @bw.gpsimd
    def _(e):
        E.wait(e, gs_done)
        e1 = e.dma_start(out=WB[:, 0:128].rearrange("p (a o) -> p a o", a=8),
                         in_=IN["skw1"].rearrange("(a p) o -> p a o", p=128))
        E.mark(e1, "gd", None, 16)
        e2 = e.dma_start(out=WB[:, 128:256].rearrange("p (a o) -> p a o", a=8),
                         in_=IN["svw1"].rearrange("(a p) o -> p a o", p=128))
        E.mark(e2, "gd", None, 16)
        E.mark(e.dma_start(out=WB[0:16, 256:512], in_=IN["skw2"][:, :]), "gd", None, 16)
        E.mark(e.dma_start(out=WB[0:16, 512:768], in_=IN["svw2"][:, :]), "gd", "skw", 16)

    bs1 = blk()
    for i in range(8):
        which, ts = i // 4, i % 4

        @bs1.tensor
        def _(t, i=i, which=which, ts=ts):
            if i == 0:
                E.wait(t, "skw")
            if i >= 2:
                E.wait(t, f"s1.ev.{i - 2}")
            ps = psB[0:16, 0:512] if i % 2 == 0 else psA[0:16, 0:512]
            last = None
            for dc in range(8):
                last = t.matmul(
                    ps,
                    WB[:, which * 128 + dc * 16: which * 128 + (dc + 1) * 16],
                    XA[:, dc * 2048 + ts * 512: dc * 2048 + ts * 512 + 512],
                    start=(dc == 0), stop=(dc == 7))
            E.mark(last, "tm", f"s1.mm.{i}")

        @bs1.vector
        def _(v, i=i, which=which, ts=ts):
            E.wait(v, f"s1.mm.{i}")
            ps = psB[0:16, 0:512] if i % 2 == 0 else psA[0:16, 0:512]
            E.mark(v.tensor_copy(
                S1S[0:16, which * 2048 + ts * 512: which * 2048 + ts * 512 + 512], ps),
                "vo", f"s1.ev.{i}")

    bs2 = blk()
    for i in range(2 * NT):
        which, tt = i // NT, i % NT

        @bs2.tensor
        def _(t, i=i, which=which, tt=tt):
            if i == 0:
                E.wait(t, "s1.ev.7")
            if i >= 2:
                E.wait(t, f"s2.clip.{i - 2}")
            ps = psB[:, 0:256] if i % 2 == 0 else psA[:, 0:256]
            t.matmul(ps,
                     S1S[0:16, which * 2048 + tt * 128: which * 2048 + tt * 128 + 128],
                     WB[0:16, 256 + which * 256: 512 + which * 256],
                     start=True, stop=False)
            E.mark(t.matmul(ps, CF["onesrow"][0:1, 0:128],
                            (CF["skb2"] if which == 0 else CF["svb2"]),
                            start=False, stop=True), "tm", f"s2.mm.{i}")

        @bs2.vector
        def _(v, i=i):
            E.wait(v, f"s2.mm.{i}")
            if i >= 2:
                E.wait(v, f"s2.exp.{i - 2}")
            ps = psB[:, 0:256] if i % 2 == 0 else psA[:, 0:256]
            sl = SMS[:, (i % 2) * 256:(i % 2 + 1) * 256]
            v.tensor_scalar_min(sl, ps, 32.0)
            E.mark(v.tensor_scalar_max(sl, sl, -32.0), "vo", f"s2.clip.{i}")

        @bs2.scalar
        def _(s, i=i, which=which, tt=tt):
            E.wait(s, f"s2.clip.{i}")
            sl = SMS[:, (i % 2) * 256:(i % 2 + 1) * 256]
            E.mark(s.activation(
                WKV[:, which * 4096 + tt * 256: which * 4096 + (tt + 1) * 256],
                sl, AF.Exp), "so", f"s2.exp.{i}")

    bwt = blk()
    for bch in range(16):
        @bwt.tensor
        def _(t, bch=bch):
            if bch == 0:
                E.wait(t, f"s2.exp.{2 * NT - 1}")
            else:
                E.wait(t, f"svt.ev.{4 * bch - 1}")
            for j in range(4):
                i = 4 * bch + j
                tt, h = i // 4, i % 4
                E.mark(t.transpose(
                    psC[0:64, j * 128:(j + 1) * 128],
                    WKV[:, 4096 + tt * 256 + h * 64: 4096 + tt * 256 + (h + 1) * 64],
                    CV["identb"]), "tm", f"svt.mm.{i}")

        @bwt.vector
        def _(v, bch=bch):
            E.wait(v, f"svt.mm.{4 * bch + 3}")
            for j in range(4):
                i = 4 * bch + j
                tt, h = i // 4, i % 4
                E.mark(v.tensor_copy(
                    SVT[0:64, h * 2048 + tt * 128: h * 2048 + (tt + 1) * 128],
                    psC[0:64, j * 128:(j + 1) * 128]), "vo", f"svt.ev.{i}")

    cut("ph1c")
    att2.close()
    KTT = att.enter_context(nc.sbuf_tensor("KTT", [128, 2048], BF16))
    SF = att.enter_context(nc.sbuf_tensor("SF", [128, 2304], F32))
    CONT = att.enter_context(nc.sbuf_tensor("CONT", [128, 2048], F32))

    def emit_ktime(tag, b_, T, slot, pre_evs=()):
        @b_.tensor
        def _(t):
            for evn in pre_evs:
                E.wait(t, evn)
            for dc in range(8):
                E.mark(t.transpose(
                    psC[:, dc * 128:(dc + 1) * 128],
                    KT[:, dc * 2048 + T * 128: dc * 2048 + (T + 1) * 128],
                    CV["identb"]), "tm", f"{tag}.ktt.{T}.{dc}")

        @b_.vector
        def _(v):
            E.wait(v, f"{tag}.ktt.{T}.7")
            for dc in range(8):
                E.mark(v.tensor_copy(
                    KTT[:, slot * 1024 + dc * 128: slot * 1024 + (dc + 1) * 128],
                    psC[:, dc * 128:(dc + 1) * 128]), "vo", f"{tag}.ktc.{T}.{dc}")
        return f"{tag}.ktc.{T}.7"

    bcz = blk()

    @bcz.vector
    def _(v):
        E.mark(v.memset(CONT[:, :], 0.0), "vo", "cont0")

    for cc in range(2):
        for tt in range(8):
            T = cc * 8 + tt
            b_ = blk()

            @b_.gpsimd
            def _(e, T=T):
                if T == 0:
                    E.wait(e, "vproj.st.15")
                    E.wait(e, "gproj.st.15")
                if T >= 2:
                    E.wait(e, f"ca.bundle.{T - 2}")
                E.mark(e.dma_start(out=VS[:, (T % 2) * 1024:(T % 2 + 1) * 1024],
                                   in_=DR["vtimed"][T * 128:(T + 1) * 128, :]),
                       "gd", f"ca.v.{T}", 16)

            pre = ["svt.ev.63"] if T == 0 else [f"ca.ktc.{T - 1}.7"]
            kt_ev = emit_ktime("ca", b_, T, T % 2, pre_evs=pre)

            @b_.tensor
            def _(t, T=T, cc=cc, tt=tt):
                E.wait(t, f"ca.ktc.{T}.7")
                E.wait(t, f"ca.v.{T}")
                last = None
                for h in range(HA):
                    for dqc in range(2):
                        i = 2 * h + dqc
                        last = t.matmul(
                            psD[:, i * 64:(i + 1) * 64],
                            KTT[:, (T % 2) * 1024 + h * 256 + dqc * 128:
                                (T % 2) * 1024 + h * 256 + dqc * 128 + 128],
                            WKV[:, T * 256 + h * 64: T * 256 + (h + 1) * 64],
                            start=(tt == 0 and i == 0), stop=(tt == 7 and i == 7))
                for h in range(HA):
                    bank_first = (h % 2 == 0)
                    bank_last = (h % 2 == 1)
                    last = t.matmul(
                        psB[0:64, h * 256:(h + 1) * 256],
                        WKV[:, 4096 + T * 256 + h * 64: 4096 + T * 256 + (h + 1) * 64],
                        VS[:, (T % 2) * 1024 + h * 256:(T % 2) * 1024 + (h + 1) * 256],
                        start=(tt == 0 and bank_first), stop=(tt == 7 and bank_last))
                t.matmul(psA[0:1, 1024:1280], CV["onescolb"],
                         WKV[:, T * 256:(T + 1) * 256],
                         start=(tt == 0), stop=False)
                last = t.matmul(psA[0:1, 1280:1536], CV["onescolb"],
                                WKV[:, 4096 + T * 256: 4096 + (T + 1) * 256],
                                start=False, stop=(tt == 7))
                E.mark(last, "tm", f"ca.bundle.{T}")

        bce = blk()

        @bce.vector
        def _(v, cc=cc):
            E.wait(v, f"ca.bundle.{cc * 8 + 7}")
            E.wait(v, "cont0")
            if cc == 1:
                E.wait(v, "ca.st.0")
            v.tensor_copy(CONT[:, 0:512], psD[:, 0:512])
            v.tensor_copy(CONT[0:64, 512:1536], psB[0:64, 0:1024])
            v.tensor_copy(CONT[0:1, 1536:1792], psA[0:1, 1024:1280])
            E.mark(v.tensor_copy(CONT[0:1, 1792:2048], psA[0:1, 1280:1536]),
                   "vo", f"ca.cont.{cc}")

        @bce.scalar
        def _(s, cc=cc):
            E.wait(s, f"ca.cont.{cc}")
            E.mark(s.dma_start(out=DR["st_in"][cc * 128:(cc + 1) * 128, :],
                               in_=CONT[:, :]), "hd", f"ca.st.{cc}", 16)

    cut("ph2a")
    bag = blk()

    @bag.gpsimd
    def _(e):
        E.wait(e, "ca.st.1")
        E.mark(e.collective_compute(
            "AllGather", ALU.bypass,
            replica_groups=[[0, 1, 2, 3], [4, 5, 6, 7]],
            ins=[DR["st_in"].tensor.ap()],
            outs=[DR["st_out"].tensor.ap()],
        ), "cc", "ag.state")

    def prefix_add(tag, mask_ap, first_ev):
        for j in range(8):
            r, sl = (j, 0) if j < 4 else (7 - j, 1)
            row = 256 * r + 128 * sl
            bp = blk()

            @bp.gpsimd
            def _(e, j=j, row=row):
                if j == 0:
                    E.wait(e, first_ev)
                    if tag == "pfb":
                        E.wait(e, "w0.7.zrow.1")
                        E.wait(e, "w0.7.ptil")
                if j >= 2:
                    E.wait(e, f"{tag}.add.{j - 2}")
                E.mark(e.dma_start(out=(CONT[:, :] if j % 2 == 0 else SF[:, 0:2048]),
                                   in_=DR["st_out"][row:row + 128, :]),
                       "gd", f"{tag}.ld.{j}", 16)

            @bp.vector
            def _(v, j=j):
                E.wait(v, f"{tag}.ld.{j}")
                if tag == "pfa" and j == 0:
                    v.memset(STATE[:, :], 0.0)
                src = CONT[:, :] if j % 2 == 0 else SF[:, 0:2048]
                E.mark(v.scalar_tensor_tensor(
                    STATE[:, :], src, mask_ap[:, j:j + 1], STATE[:, :],
                    ALU.mult, ALU.add), "vo", f"{tag}.add.{j}")
        bp2 = blk()

        @bp2.vector
        def _(v):
            E.wait(v, f"{tag}.add.7")
            v.tensor_copy(STATEB[:, 0:512], STATE[:, 0:512])
            v.tensor_copy(STATEB[0:64, 512:1536], STATE[0:64, 512:1536])
            v.tensor_copy(SMS[0:1, 0:256], STATE[0:1, 1536:1792])
            E.mark(v.tensor_copy(SMS[0:1, 256:512], STATE[0:1, 1792:2048]),
                   "vo", f"{tag}.done")
        return f"{tag}.done"

    pfa_done = prefix_add("pfa", CF["maskA"], "ag.state")
    cut("pfa")

    def chunk_walk(cc, ready_ev):
        for tt in range(8):
            T = cc * 8 + tt
            P = f"w{cc}.{tt}"
            PM = f"w{cc}.{tt - 1}"
            b_ = blk()

            @b_.gpsimd
            def _(e, T=T, tt=tt, cc=cc, P=P):
                if tt == 0:
                    E.wait(e, ready_ev)
                if tt >= 2:
                    E.wait(e, f"w{cc}.{tt - 2}.oupd")
                E.mark(e.dma_start(out=VS[:, (tt % 2) * 1024:(tt % 2 + 1) * 1024],
                                   in_=DR["vtimed"][T * 128:(T + 1) * 128, :]),
                       "gd", f"{P}.v", 16)
                if tt >= 2:
                    E.wait(e, f"w{cc}.{tt - 2}.ofin")
                E.mark(e.dma_start(
                    out=SSC[:, 2816 + (tt % 2) * 1024: 2816 + (tt % 2 + 1) * 1024],
                    in_=DR["gsild"][T * 128:(T + 1) * 128, :]), "gd", f"{P}.gs", 16)

            pre = [ready_ev] if tt == 0 else [f"{PM}.ktc.7"]
            @b_.tensor
            def _(t, T=T, P=P, pre=pre, tt=tt, cc=cc):
                for evn in pre:
                    E.wait(t, evn)
                if tt > 0:
                    E.wait(t, f"w{cc}.{tt - 1}.ptc.3")
                for dc in range(8):
                    E.mark(t.transpose(
                        psC[:, dc * 128:(dc + 1) * 128],
                        KT[:, dc * 2048 + T * 128: dc * 2048 + (T + 1) * 128],
                        CV["identb"]), "tm", f"{P}.ktt.{dc}")

            @b_.vector
            def _(v, P=P, tt=tt):
                E.wait(v, f"{P}.ktt.7")
                for dc in range(8):
                    E.mark(v.tensor_copy(
                        KTT[:, (tt % 2) * 1024 + dc * 128:(tt % 2) * 1024 + (dc + 1) * 128],
                        psC[:, dc * 128:(dc + 1) * 128]), "vo", f"{P}.ktc.{dc}")

            @b_.tensor
            def _(t, T=T, P=P, tt=tt, cc=cc):
                if tt > 0:
                    E.wait(t, f"w{cc}.{tt - 1}.sevac")
                mms = []
                for h in range(HA):
                    for dqc in range(2):
                        mms.append((psA[:, h * 128:(h + 1) * 128],
                                    KT[:, (2 * h + dqc) * 2048 + T * 128:
                                       (2 * h + dqc) * 2048 + (T + 1) * 128],
                                    QT[:, (2 * h + dqc) * 2048 + T * 128:
                                       (2 * h + dqc) * 2048 + (T + 1) * 128], "b0"))
                E.mark(mm_bank_group(t, mms), "tm", f"{P}.sT")

            @b_.vector
            def _(v, P=P):
                E.wait(v, f"{P}.sT")
                last = None
                for h in range(HA):
                    last = v.scalar_tensor_tensor(
                        SSC[:, h * 128:(h + 1) * 128],
                        psA[:, h * 128:(h + 1) * 128], 1.0, CF["trif"],
                        ALU.mult, ALU.mult)
                E.mark(last, "vo", f"{P}.sevac")

            @b_.tensor
            def _(t, T=T, P=P, tt=tt, cc=cc):
                E.wait(t, f"{P}.sevac")
                if tt > 0:
                    E.wait(t, f"w{cc}.{tt - 1}.stb")
                    E.wait(t, f"w{cc}.{tt - 1}.zrow.1")
                    E.wait(t, f"w{cc}.{tt - 1}.praw")
                    E.wait(t, f"w{cc}.{tt - 1}.zcopy")
                mms = []
                for h in range(HA):
                    sl = psA[:, 512 + h * 64: 512 + (h + 1) * 64]
                    mms.append((sl, SSC[:, h * 128:(h + 1) * 128],
                                WKV[:, T * 256 + h * 64: T * 256 + (h + 1) * 64], "b1"))
                    mms.append((sl,
                                QT[:, 2 * h * 2048 + T * 128: 2 * h * 2048 + (T + 1) * 128],
                                STATEB[:, 2 * h * 64:(2 * h + 1) * 64], "b1"))
                    mms.append((sl,
                                QT[:, (2 * h + 1) * 2048 + T * 128:
                                   (2 * h + 1) * 2048 + (T + 1) * 128],
                                STATEB[:, (2 * h + 1) * 64:(2 * h + 2) * 64], "b1"))
                mms.append((psA[:, 1024:1280], CV["tri"],
                            WKV[:, T * 256:(T + 1) * 256], "b2"))
                mms.append((psA[:, 1024:1280], CF["onesrow"], SMS[0:1, 0:256], "b2"))
                mms.append((psA[:, 1280:1536], CV["tri"],
                            WKV[:, 4096 + T * 256: 4096 + (T + 1) * 256], "b2"))
                mms.append((psA[:, 1280:1536], CF["onesrow"], SMS[0:1, 256:512], "b2"))
                E.mark(mm_bank_group(t, mms), "tm", f"{P}.log")

            @b_.vector
            def _(v, P=P, tt=tt, cc=cc):
                E.wait(v, f"{P}.log")
                if tt > 0:
                    E.wait(v, f"w{cc}.{tt - 1}.zrow.1")
                    E.wait(v, f"w{cc}.{tt - 1}.exp")
                v.reciprocal(SF[:, 256:512], psA[:, 1024:1280])
                E.mark(v.scalar_tensor_tensor(
                    SF[:, 0:256], psA[:, 512:768], SCALE_A, SF[:, 256:512],
                    ALU.mult, ALU.mult), "vo", f"{P}.praw")
                v.reciprocal(SF[:, 512:768], psA[:, 1280:1536])
                v.tensor_copy(SF[:, 1040:1296], psA[:, 1024:1280])
                E.mark(v.tensor_copy(SF[:, 1296:1552], psA[:, 1280:1536]),
                       "vo", f"{P}.zcopy")
                E.mark(v.tensor_reduce(
                    SF[:, 1032:1036],
                    SF[:, 0:256].rearrange("p (a b) -> p a b", a=4),
                    AX.X, ALU.max, negate=True), "vo", f"{P}.nmax")

            @b_.scalar
            def _(s, P=P, tt=tt, cc=cc):
                E.wait(s, f"{P}.nmax")
                if tt > 0:
                    E.wait(s, f"w{cc}.{tt - 1}.ptil")
                last = None
                for h in range(HA):
                    last = s.activation(
                        SF[:, 768 + h * 64: 768 + (h + 1) * 64],
                        SF[:, h * 64:(h + 1) * 64],
                        AF.Exp, bias=SF[:, 1032 + h: 1033 + h],
                        accum_out=SF[:, 1024 + h: 1025 + h])
                E.mark(last, "so", f"{P}.exp")

            @b_.scalar
            def _(s, P=P):
                E.wait(s, f"{P}.zcopy")
                E.mark(s.dma_start(out=SMS[0:1, 0:256], in_=SF[127:128, 1040:1296]),
                       "zd", f"{P}.zrow.0", 16)
                E.mark(s.dma_start(out=SMS[0:1, 256:512], in_=SF[127:128, 1296:1552]),
                       "zd", f"{P}.zrow.1", 16)

            @b_.vector
            def _(v, P=P, tt=tt, cc=cc):
                E.wait(v, f"{P}.exp")
                if tt > 0:
                    E.wait(v, f"w{cc}.{tt - 1}.ptT.1")
                rc = E.mark(v.reciprocal(SF[:, 1028:1032], SF[:, 1024:1028]), "vo")
                v.wait_ge(E.h["vo"], rc)
                last = None
                for h in range(HA):
                    last = v.scalar_tensor_tensor(
                        SSC[:, 1536 + h * 64: 1536 + (h + 1) * 64],
                        SF[:, 768 + h * 64: 768 + (h + 1) * 64],
                        SF[:, 1028 + h: 1029 + h],
                        SF[:, 512 + h * 64: 512 + (h + 1) * 64],
                        ALU.mult, ALU.mult)
                E.mark(last, "vo", f"{P}.ptil")

            @b_.tensor
            def _(t, P=P):
                E.wait(t, f"{P}.ptil")
                E.wait(t, f"{P}.ktc.7")
                for h in range(HA):
                    E.mark(t.transpose(psC[0:64, h * 128:(h + 1) * 128],
                                       SSC[:, 1536 + h * 64: 1536 + (h + 1) * 64],
                                       CV["identb"]), "tm", f"{P}.ptT.{h}")

            @b_.vector
            def _(v, P=P):
                E.wait(v, f"{P}.ptT.3")
                for h in range(HA):
                    E.mark(v.tensor_copy(
                        SSC[0:64, 512 + h * 128: 512 + (h + 1) * 128],
                        psC[0:64, h * 128:(h + 1) * 128]), "vo", f"{P}.ptc.{h}")

            @b_.tensor
            def _(t, T=T, P=P, tt=tt, cc=cc):
                E.wait(t, f"{P}.ptc.3")
                if tt > 0:
                    E.wait(t, f"w{cc}.{tt - 1}.aevac")
                mms = []
                for h in range(HA):
                    mms.append((psA[:, 1536 + h * 128: 1536 + (h + 1) * 128],
                                SVT[0:64, h * 2048 + T * 128: h * 2048 + (T + 1) * 128],
                                SSC[0:64, 512 + h * 128: 512 + (h + 1) * 128], "b3"))
                E.mark(mm_bank_group(t, mms), "tm", f"{P}.aT")

            @b_.vector
            def _(v, P=P):
                E.wait(v, f"{P}.aT")
                last = None
                for h in range(HA):
                    last = v.scalar_tensor_tensor(
                        SSC[:, 1024 + h * 128: 1024 + (h + 1) * 128],
                        psA[:, 1536 + h * 128: 1536 + (h + 1) * 128], 1.0, CF["trif"],
                        ALU.mult, ALU.mult)
                E.mark(last, "vo", f"{P}.aevac")

            @b_.tensor
            def _(t, P=P, tt=tt, cc=cc):
                E.wait(t, f"{P}.aevac")
                E.wait(t, f"{P}.v")
                if tt > 0:
                    E.wait(t, f"w{cc}.{tt - 1}.ofin")
                mms = []
                for h in range(HA):
                    bk = f"b{4 + h // 2}"
                    mms.append((psB[:, h * 256:(h + 1) * 256],
                                SSC[:, 1024 + h * 128: 1024 + (h + 1) * 128],
                                VS[:, (tt % 2) * 1024 + h * 256:(tt % 2) * 1024 + (h + 1) * 256],
                                bk))
                    mms.append((psB[:, h * 256:(h + 1) * 256],
                                SSC[0:64, 512 + h * 128: 512 + (h + 1) * 128],
                                STATEB[0:64, 512 + h * 256: 512 + (h + 1) * 256],
                                bk))
                E.mark(mm_bank_group(t, mms), "tm", f"{P}.o")

            @b_.scalar
            def _(s, P=P):
                E.wait(s, f"{P}.o")
                for h in range(HA):
                    s.activation(SMS[:, 528:784], psB[:, h * 256:(h + 1) * 256],
                                 AF.Square, accum_out=SMS[:, 512 + h: 513 + h])
                E.mark(s.activation(SMS[:, 516:520], SMS[:, 512:516], AF.Sqrt,
                                    bias=1e-5, scale=1.0 / 256.0),
                       "so", f"{P}.gn")

            @b_.vector
            def _(v, P=P, T=T, tt=tt, cc=cc):
                E.wait(v, f"{P}.gn")
                if tt > 0:
                    E.wait(v, f"w{cc}.{tt - 1}.ost")
                rc = E.mark(v.reciprocal(SMS[:, 520:524], SMS[:, 516:520]), "vo")
                v.wait_ge(E.h["vo"], rc)
                E.wait(v, f"{P}.gs")
                last = None
                for h in range(HA):
                    last = v.scalar_tensor_tensor(
                        SSC[:, 1792 + h * 256: 1792 + (h + 1) * 256],
                        psB[:, h * 256:(h + 1) * 256],
                        SMS[:, 520 + h: 521 + h],
                        SSC[:, 2816 + (tt % 2) * 1024 + h * 256:
                            2816 + (tt % 2) * 1024 + (h + 1) * 256],
                        ALU.mult, ALU.mult)
                E.mark(last, "vo", f"{P}.ofin")

            @b_.scalar
            def _(s, P=P, T=T):
                E.wait(s, f"{P}.ofin")
                E.mark(s.dma_start(out=DR["ofinb"][T * 128:(T + 1) * 128, :],
                                   in_=SSC[:, 1792:2816]), "hd", f"{P}.ost", 16)

            @b_.tensor
            def _(t, P=P, T=T, tt=tt, cc=cc):
                E.wait(t, f"{P}.praw")
                E.wait(t, f"{P}.zcopy")
                if tt > 0:
                    E.wait(t, f"w{cc}.{tt - 1}.supd")
                mms = []
                for h in range(HA):
                    for dqc in range(2):
                        i = 2 * h + dqc
                        mms.append((psD[:, i * 64:(i + 1) * 64],
                                    KTT[:, (tt % 2) * 1024 + h * 256 + dqc * 128:
                                        (tt % 2) * 1024 + h * 256 + dqc * 128 + 128],
                                    WKV[:, T * 256 + h * 64: T * 256 + (h + 1) * 64],
                                    "b7"))
                for h in range(HA):
                    mms.append((psA[0:64, 512 + h * 256: 512 + (h + 1) * 256],
                                WKV[:, 4096 + T * 256 + h * 64: 4096 + T * 256 + (h + 1) * 64],
                                VS[:, (tt % 2) * 1024 + h * 256:(tt % 2) * 1024 + (h + 1) * 256],
                                f"b{1 + h // 2}"))
                E.mark(mm_bank_group(t, mms), "tm", f"{P}.oupd")

            @b_.vector
            def _(v, P=P):
                E.wait(v, f"{P}.oupd")
                v.tensor_add(STATE[:, 0:512], STATE[:, 0:512], psD[:, 0:512])
                E.mark(v.tensor_add(STATE[0:64, 512:1536], STATE[0:64, 512:1536],
                                    psA[0:64, 512:1536]), "vo", f"{P}.supd")
                v.tensor_copy(STATEB[:, 0:512], STATE[:, 0:512])
                E.mark(v.tensor_copy(STATEB[0:64, 512:1536], STATE[0:64, 512:1536]),
                       "vo", f"{P}.stb")
        return f"w{cc}.7.ost"

    w0_done = chunk_walk(0, pfa_done)
    cut("walk0")
    pfb_done = prefix_add("pfb", CF["maskAB"], "w0.7.supd")
    w1_done = chunk_walk(1, pfb_done)

    cut("walk1")
    wo_ev = load_w("wo", IN["wo"], wait_evs=(w1_done,))
    bfold = blk()

    @bfold.vector
    def _(v):
        E.wait(v, wo_ev)
        last = None
        for dvc in range(8):
            last = v.tensor_scalar(
                WB[:, dvc * 1024:(dvc + 1) * 1024],
                WB[:, dvc * 1024:(dvc + 1) * 1024],
                CF["gnwc"][:, dvc % 2: dvc % 2 + 1], None, ALU.mult)
        E.mark(last, "vo", "wofold")

    OFT = QT
    btr = blk()

    @btr.sync
    def _(e):
        E.wait(e, "w1.7.ost")
        for dvc in range(8):
            E.mark(e.dma_start(out=OFT[:, dvc * 2048:(dvc + 1) * 2048],
                               in_=DR["ofinb"][:, dvc * 128:(dvc + 1) * 128],
                               transpose=True), "sd", f"oft.{dvc}", 16)

    cut("oft")
    X1F = dint("x1f", (TC, D))

    for tt in range(NT):
        pi = tt % 2
        P = f"ln.{tt}"
        b_ = blk()

        @b_.gpsimd
        def _(e, tt=tt, pi=pi, P=P):
            if tt < 2:
                E.wait(e, "w1.7.supd")
                E.wait(e, "w1.7.zrow.1")
            if tt >= 2:
                E.wait(e, f"ln.{tt - 2}.r1")
            E.mark(e.dma_start(out=SF[:, pi * 1024:(pi + 1) * 1024],
                               in_=IN["xt"][tt * 128:(tt + 1) * 128, :]),
                   "gd", f"{P}.x0", 16)

        @b_.tensor
        def _(t, tt=tt, P=P):
            if tt == 0:
                E.wait(t, "wofold")
                E.wait(t, "oft.7")
            if tt >= 2:
                E.wait(t, f"ln.{tt - 2}.r1")
            ps = psB[:, :] if tt % 2 == 0 else psA[:, 0:1024]
            mms = []
            for dvc in range(8):
                for half in range(2):
                    mms.append((ps[:, half * 512:(half + 1) * 512],
                                OFT[:, dvc * 2048 + tt * 128: dvc * 2048 + tt * 128 + 128],
                                WB[:, dvc * 1024 + half * 512: dvc * 1024 + (half + 1) * 512],
                                f"h{half}"))
            E.mark(mm_bank_group(t, mms), "tm", f"{P}.mm")

        @b_.vector
        def _(v, tt=tt, pi=pi, P=P):
            E.wait(v, f"{P}.mm")
            E.wait(v, f"{P}.x0")
            if tt >= 2:
                E.wait(v, f"ln.{tt - 2}.x1done")
            ps = psB[:, :] if tt % 2 == 0 else psA[:, 0:1024]
            x0t = SF[:, pi * 1024:(pi + 1) * 1024]
            E.mark(v.tensor_add(CONT[:, pi * 1024:(pi + 1) * 1024], ps, x0t),
                   "vo", f"{P}.r1")
            E.mark(v.reduce_sum(SMS[:, 790 + pi * 16: 791 + pi * 16],
                                CONT[:, pi * 1024:(pi + 1) * 1024], AX.X),
                   "vo", f"{P}.sum")

        @b_.scalar
        def _(s, tt=tt, pi=pi, P=P):
            E.wait(s, f"{P}.r1")
            E.mark(s.activation(SSC[:, 0:1024],
                                CONT[:, pi * 1024:(pi + 1) * 1024], AF.Square,
                                accum_out=SMS[:, 791 + pi * 16: 792 + pi * 16]),
                   "so", f"{P}.sq")

        @b_.vector
        def _(v, tt=tt, pi=pi, P=P):
            E.wait(v, f"{P}.sq")
            base = 790 + pi * 16
            v.tensor_scalar(SMS[:, base + 2: base + 3], SMS[:, base: base + 1],
                            1.0 / 1024.0, None, ALU.mult)
            v.tensor_scalar(SMS[:, base + 3: base + 4], SMS[:, base + 1: base + 2],
                            1.0 / 1024.0, None, ALU.mult)
            v.tensor_mul(SMS[:, base + 4: base + 5], SMS[:, base + 2: base + 3],
                         SMS[:, base + 2: base + 3])
            v.tensor_sub(SMS[:, base + 3: base + 4], SMS[:, base + 3: base + 4],
                         SMS[:, base + 4: base + 5])
            E.mark(v.tensor_scalar(SMS[:, base + 3: base + 4],
                                   SMS[:, base + 3: base + 4], 1.0, 1e-5,
                                   ALU.mult, ALU.add), "vo", f"{P}.var")

        @b_.scalar
        def _(s, tt=tt, pi=pi, P=P):
            E.wait(s, f"{P}.var")
            base = 790 + pi * 16
            E.mark(s.activation(SMS[:, base + 5: base + 6],
                                SMS[:, base + 3: base + 4], AF.Sqrt),
                   "so", f"{P}.sd")

        @b_.vector
        def _(v, tt=tt, pi=pi, P=P):
            E.wait(v, f"{P}.sd")
            base = 790 + pi * 16
            v.reciprocal(SMS[:, base + 6: base + 7], SMS[:, base + 5: base + 6])
            v.tensor_mul(SMS[:, base + 7: base + 8], SMS[:, base + 2: base + 3],
                         SMS[:, base + 6: base + 7])
            E.mark(v.tensor_scalar(SMS[:, base + 7: base + 8],
                                   SMS[:, base + 7: base + 8], -1.0, None,
                                   ALU.mult), "vo", f"{P}.nmb")

        @b_.vector
        def _(v, tt=tt, pi=pi, P=P):
            E.wait(v, f"{P}.nmb")
            if tt >= 1:
                E.wait(v, f"ln.{tt - 1}.x1f")
                E.wait(v, f"ln.{tt - 1}.x1st")
            base = 790 + pi * 16
            E.mark(v.tensor_scalar(CONT[:, (1 - pi) * 1024:(2 - pi) * 1024],
                                   CONT[:, pi * 1024:(pi + 1) * 1024],
                                   SMS[:, base + 2: base + 3],
                                   SMS[:, base + 6: base + 7],
                                   ALU.subtract, ALU.mult),
                   "vo", f"{P}.x1")

        @b_.vector
        def _(v, tt=tt, pi=pi, P=P):
            E.wait(v, f"{P}.x1")
            if tt >= 2:
                E.wait(v, f"ln.{tt - 2}.x1st")
            E.mark(v.tensor_copy(
                SSC[:, 2816 + pi * 1024: 2816 + (pi + 1) * 1024],
                CONT[:, (1 - pi) * 1024:(2 - pi) * 1024]), "vo", f"{P}.x1b")

        @b_.sync
        def _(e, tt=tt, pi=pi, P=P):
            E.wait(e, f"{P}.x1b")
            E.mark(e.dma_start(out=X1F[tt * 128:(tt + 1) * 128, :],
                               in_=CONT[:, (1 - pi) * 1024:(2 - pi) * 1024]),
                   "sd", f"{P}.x1f", 16)

        @b_.scalar
        def _(s, tt=tt, pi=pi, P=P):
            E.wait(s, f"{P}.x1b")
            E.mark(s.dma_start(out=DR["x1b"][tt * 128:(tt + 1) * 128, :],
                               in_=SSC[:, 2816 + pi * 1024: 2816 + (pi + 1) * 1024]),
                   "hd", f"{P}.x1st", 16)

        @b_.vector
        def _(v, tt=tt, P=P):
            E.wait(v, f"{P}.x1b")
            E.mark(v.engine_nop(), "vo", f"{P}.x1done")

    att.close()
    mh = ExitStack()
    _stacks.append(mh)
    X1T = mh.enter_context(nc.sbuf_tensor("X1T", [128, 16384], BF16))
    WB3 = mh.enter_context(nc.sbuf_tensor("WB3", [128, 8192], BF16))
    SC3 = mh.enter_context(nc.sbuf_tensor("SC3", [128, 4096], BF16))
    MR = mh.enter_context(nc.sbuf_tensor("MR", [128, 4096], BF16))
    VSTG = mh.enter_context(nc.sbuf_tensor("VSTG", [128, 1040], BF16))
    KB = mh.enter_context(nc.sbuf_tensor("KB", [128, 8192], BF16))
    VB = mh.enter_context(nc.sbuf_tensor("VB", [128, 8320], BF16))
    QH = mh.enter_context(nc.sbuf_tensor("QH", [128, 2048], BF16))
    PB = mh.enter_context(nc.sbuf_tensor("PB", [128, 2048], BF16))
    YPAIR = mh.enter_context(nc.sbuf_tensor("YPAIR", [128, 2048], BF16))
    YTS = mh.enter_context(nc.sbuf_tensor("YTS", [128, 2048], BF16))
    FSC = mh.enter_context(nc.sbuf_tensor("FSC", [128, 64], F32))

    b3 = blk()

    @b3.sync
    def _(e):
        E.wait(e, "ln.15.x1st")
        for dc in range(8):
            E.mark(e.dma_start(out=X1T[:, dc * 2048:(dc + 1) * 2048],
                               in_=DR["x1b"][:, dc * 128:(dc + 1) * 128],
                               transpose=True), "sd", f"x1t.{dc}", 16)

    @b3.gpsimd
    def _(e):
        E.mark(e.dma_start(out=MR[:, 0:2048], in_=IN["c_mcos2"][:, :]), "gd", None, 16)
        E.mark(e.dma_start(out=MR[:, 2048:4096], in_=IN["c_msin2"][:, :]), "gd", "mr", 16)
        E.wait(e, "ln.15.x1st")
        E.mark(e.dma_start(out=WB3[:, :].rearrange("p (a o) -> p a o", a=8),
                           in_=IN["wqkv"][:, 0:1024].rearrange("(a p) o -> p a o", p=128)),
               "gd", "w3.q", 16)

    bfold = blk()

    @bfold.vector
    def _(v):
        E.wait(v, "w3.q")
        last = None
        for dc in range(8):
            last = v.tensor_scalar(WB3[:, dc * 1024:(dc + 1) * 1024],
                                   WB3[:, dc * 1024:(dc + 1) * 1024],
                                   CF["g1c"][:, dc:dc + 1], None, ALU.mult)
        E.mark(last, "vo", "w3.qf")

    def qk_proj(tag, dst_is_q):
        for oc in range(8):
            P = f"{tag}.{oc}"
            b_ = blk()
            for ts in range(4):
                @b_.tensor
                def _(t, oc=oc, P=P, tag=tag, ts=ts):
                    if oc == 0 and ts == 0:
                        E.wait(t, f"w3.{tag}f")
                        E.wait(t, "x1t.7")
                        E.wait(t, "mr")
                    if oc > 0:
                        E.wait(t, f"{tag}.{oc - 1}.ev.{ts}")
                    last = None
                    for dc in range(8):
                        last = t.matmul(
                            psA[:, ts * 512:(ts + 1) * 512],
                            WB3[:, dc * 1024 + oc * 128: dc * 1024 + oc * 128 + 128],
                            X1T[:, dc * 2048 + ts * 512: dc * 2048 + ts * 512 + 512],
                            start=(dc == 0), stop=(dc == 7))
                    E.mark(last, "tm", f"{P}.mm.{ts}")

                @b_.vector
                def _(v, oc=oc, P=P, ts=ts):
                    E.wait(v, f"{P}.mm.{ts}")
                    E.mark(v.tensor_copy(SC3[:, ts * 512:(ts + 1) * 512],
                                         psA[:, ts * 512:(ts + 1) * 512]),
                           "vo", f"{P}.ev.{ts}")

                @b_.tensor
                def _(t, oc=oc, P=P, ts=ts):
                    E.wait(t, f"{P}.ev.{ts}")
                    if ts >= 2:
                        E.wait(t, f"{P}.rp.{ts - 2}")
                    E.mark(t.matmul(psB[:, (ts % 2) * 512:(ts % 2 + 1) * 512],
                                    CV["perm"],
                                    SC3[:, ts * 512:(ts + 1) * 512],
                                    start=True, stop=True), "tm", f"{P}.pm.{ts}")

                @b_.vector
                def _(v, oc=oc, P=P, ts=ts, tag=tag):
                    E.wait(v, f"{P}.pm.{ts}")
                    if ts == 0:
                        if oc > 0:
                            E.wait(v, f"{tag}.{oc - 1}.st")
                        elif tag == "k":
                            E.wait(v, "q.7.st")
                    v.scalar_tensor_tensor(
                        SC3[:, 2048 + ts * 512: 2048 + (ts + 1) * 512],
                        psB[:, (ts % 2) * 512:(ts % 2 + 1) * 512], 1.0,
                        MR[:, 2048 + ts * 512: 2048 + (ts + 1) * 512],
                        ALU.mult, ALU.mult)
                    v.tensor_mul(SC3[:, ts * 512:(ts + 1) * 512],
                                 SC3[:, ts * 512:(ts + 1) * 512],
                                 MR[:, ts * 512:(ts + 1) * 512])
                    E.mark(v.tensor_add(SC3[:, 2048 + ts * 512: 2048 + (ts + 1) * 512],
                                        SC3[:, 2048 + ts * 512: 2048 + (ts + 1) * 512],
                                        SC3[:, ts * 512:(ts + 1) * 512]),
                           "vo", f"{P}.rp.{ts}")

            @b_.scalar
            def _(s, oc=oc, P=P, dst_is_q=dst_is_q):
                E.wait(s, f"{P}.rp.3")
                if dst_is_q:
                    dst = DR["qmhad"][oc * 128:(oc + 1) * 128, :]
                else:
                    dst = DR["kv_in"][:, oc * 2048:(oc + 1) * 2048]
                E.mark(s.dma_start(out=dst, in_=SC3[:, 2048:4096]), "hd", f"{P}.st", 16)

    qk_proj("q", True)

    bwk = blk()

    @bwk.gpsimd
    def _(e):
        E.wait(e, "q.7.rp.3")
        E.mark(e.dma_start(out=WB3[:, :].rearrange("p (a o) -> p a o", a=8),
                           in_=IN["wqkv"][:, 1024:2048].rearrange("(a p) o -> p a o", p=128)),
               "gd", "w3.kraw", 16)

    bfold2 = blk()

    @bfold2.vector
    def _(v):
        E.wait(v, "w3.kraw")
        last = None
        for dc in range(8):
            last = v.tensor_scalar(WB3[:, dc * 1024:(dc + 1) * 1024],
                                   WB3[:, dc * 1024:(dc + 1) * 1024],
                                   CF["g1c"][:, dc:dc + 1], None, ALU.mult)
        E.mark(last, "vo", "w3.kf")

    qk_proj("k", False)

    bwv = blk()

    @bwv.gpsimd
    def _(e):
        E.wait(e, "k.7.rp.3")
        E.mark(e.dma_start(out=WB3[:, :].rearrange("p (a o) -> p a o", a=8),
                           in_=IN["wqkv"][:, 2048:3072].rearrange("(a p) o -> p a o", p=128)),
               "gd", "w3.vraw", 16)

    bfold3 = blk()

    @bfold3.vector
    def _(v):
        E.wait(v, "w3.vraw")
        last = None
        for dc in range(8):
            last = v.tensor_scalar(WB3[:, dc * 1024:(dc + 1) * 1024],
                                   WB3[:, dc * 1024:(dc + 1) * 1024],
                                   CF["g1c"][:, dc:dc + 1], None, ALU.mult)
        E.mark(last, "vo", "w3.vf")
        E.mark(v.memset(VSTG[:, :], 1.0), "vo", "vstg1")

    for tt in range(NT):
        P = f"v3.{tt}"
        b_ = blk()

        @b_.tensor
        def _(t, tt=tt, P=P):
            if tt == 0:
                E.wait(t, "w3.vf")
            if tt >= 2:
                E.wait(t, f"v3.{tt - 2}.ev")
            ps = psB[:, :] if tt % 2 == 0 else psA[:, 1024:2048]
            last = None
            for dc in range(8):
                for half in range(2):
                    last = t.matmul(
                        ps[:, half * 512:(half + 1) * 512],
                        X1T[:, dc * 2048 + tt * 128: dc * 2048 + tt * 128 + 128],
                        WB3[:, dc * 1024 + half * 512: dc * 1024 + (half + 1) * 512],
                        start=(dc == 0), stop=(dc == 7))
            E.mark(last, "tm", f"{P}.mm")

        @b_.vector
        def _(v, tt=tt, P=P):
            E.wait(v, f"{P}.mm")
            if tt == 0:
                E.wait(v, "vstg1")
            if tt >= 1:
                E.wait(v, f"v3.{tt - 1}.st")
            ps = psB[:, :] if tt % 2 == 0 else psA[:, 1024:2048]
            last = None
            for h in range(HM):
                last = v.tensor_copy(VSTG[:, h * 65: h * 65 + 64],
                                     ps[:, h * 64:(h + 1) * 64])
            E.mark(last, "vo", f"{P}.ev")

        @b_.scalar
        def _(s, tt=tt, P=P):
            E.wait(s, f"{P}.ev")
            E.mark(s.dma_start(
                out=DR["kv_in"][:, 16384:33024].rearrange(
                    "p (h c) -> p h c", h=16)[:, :, tt * 65:(tt + 1) * 65],
                in_=VSTG[:, :].rearrange("p (h c) -> p h c", h=16)),
                "hd", f"{P}.st", 16)

    cut("qkv")

    bag3 = blk()

    @bag3.gpsimd
    def _(e):
        E.wait(e, "v3.15.st")
        E.wait(e, "k.7.st")
        E.mark(e.collective_compute(
            "AllGather", ALU.bypass,
            replica_groups=[[0, 1, 2, 3], [4, 5, 6, 7]],
            ins=[DR["kv_in"].tensor.ap()],
            outs=[DR["kv_out"].tensor.ap()],
        ), "cc", "ag.kv")

    cut("ag")

    LENA = {g: 8 * g + 8 for g in range(4)}

    def kv_loads(e, h):
        hp = h % 2
        if h == 0:
            E.wait(e, "ag.kv")
        if h >= 2:
            E.wait(e, f"fl{h - 2}.ev1")
        last = None
        for r in range(4):
            for sl in range(2):
                gc = r if sl == 0 else 7 - r
                last = e.dma_start(
                    out=KB[hp * 64:(hp + 1) * 64, gc * 1024:(gc + 1) * 1024],
                    in_=DR["kv_out"][128 * r + hp * 64: 128 * r + hp * 64 + 64,
                                     (h // 2) * 2048 + sl * 1024:
                                     (h // 2) * 2048 + (sl + 1) * 1024])
                E.mark(last, "sd", None, 16)
                last = e.dma_start(
                    out=VB[:, hp * 4160 + gc * 520: hp * 4160 + (gc + 1) * 520],
                    in_=DR["kv_out"][128 * r:128 * (r + 1),
                                     16384 + h * 1040 + sl * 520:
                                     16384 + h * 1040 + (sl + 1) * 520])
                E.mark(last, "sd", None, 16)
        last = e.dma_start(out=QH[hp * 64:(hp + 1) * 64, :],
                           in_=DR["qmhad"][h * 64:(h + 1) * 64, :])
        E.mark(last, "sd", f"kv{h}.ld", 16)

    TM0, SO0, VO0, GD0 = E.n["tm"], E.n["so"], E.n["vo"], E.n["gd"]
    base_tm, base_so, base_vo = TM0, SO0, VO0
    for h in range(HM):
        P = f"fl{h}"
        for i in range(72):
            E.declare(f"{P}.s{i}", "tm", base_tm + (i + 1 if i < 2 else 2 * i))
            E.declare(f"{P}.p{i}", "tm", base_tm + 3 + i + min(i, 70))
            E.declare(f"{P}.x{i}", "so", base_so + i + 1)
        E.declare(f"{P}.ev0", "vo", base_vo + 2)
        E.declare(f"{P}.ev1", "vo", base_vo + 4)
        E.declare(f"{P}.pA", "fa", h + 1)
        base_tm += 144
        base_so += 72
        base_vo += 4
        if h % 2 == 1:
            pair = h // 2
            for qt16 in range(16):
                E.declare(f"yt{pair}.t{qt16}", "tm", base_tm + qt16 + 1)
            base_tm += 16
            for bch in range(4):
                E.declare(f"yt{pair}.c{bch}", "vo", base_vo + bch + 1)
            base_vo += 4
            E.declare(f"yt{pair}.st", "gd", GD0 + 16 * (pair + 1))

    bfl = blk()

    @bfl.sync
    def _(e):
        for h in range(HM):
            kv_loads(e, h)

    @bfl.tensor
    def _(t):
        rk = t.partition_id() % 4
        snap = dict(E.n)
        ends = []
        for g in nc.Switch(engines=[t], index=[rk], n=4):
            E.switch_body_begin(snap)
            sched = ([(0, j) for j in range(8 * g + 8)]
                     + [(1, j) for j in range(64 - 8 * g)])
            assert len(sched) == 72
            cpos = {0: g, 1: 7 - g}
            chunk_start = {0: 0, 1: 8 * g + 8}
            for h in range(HM):
                hp = h % 2
                P = f"fl{h}"
                fa_pending = [False]

                def emit_s(i, h=h, hp=hp, P=P, fa_pending=fa_pending):
                    cs, j = sched[i]
                    c = cpos[cs]
                    diag = (j >= 8 * c)
                    d = j - 8 * c
                    buf = i % 2
                    if i == 0:
                        E.wait(t, f"kv{h}.ld")
                    if i < 2:
                        if h >= 1:
                            E.wait(t, f"fl{h - 1}.x{70 + i}")
                    else:
                        E.wait(t, f"{P}.x{i - 2}")
                    mms = []
                    for halfq in range(2):
                        mms.append((
                            psA[:, buf * 1024 + halfq * 512:
                                buf * 1024 + (halfq + 1) * 512],
                            KB[hp * 64:(hp + 1) * 64, j * 128:(j + 1) * 128],
                            QH[hp * 64:(hp + 1) * 64,
                               cs * 1024 + halfq * 512:
                               cs * 1024 + (halfq + 1) * 512],
                            "lo" if halfq == 0 else "hi"))
                    if diag:
                        mms.append((
                            psA[:, buf * 1024 + d * 128:
                                buf * 1024 + (d + 1) * 128],
                            CV["identb"], CV["negm"], "lo" if d < 4 else "hi"))
                    first = {}
                    last = {}
                    for k, (_, _, _, bk) in enumerate(mms):
                        first.setdefault(bk, k)
                        last[bk] = k
                    inst = None
                    for k, (o, l, r, bk) in enumerate(mms):
                        inst = t.matmul(o, l, r, start=(first[bk] == k),
                                        stop=(last[bk] == k))
                        if k == 0 and fa_pending[0]:
                            E.mark(inst, "fa", f"{P}.pA")
                            fa_pending[0] = False
                    E.mark(inst, "tm", f"{P}.s{i}")

                emit_s(0)
                emit_s(1)
                for i in range(72):
                    cs, j = sched[i]
                    c = cpos[cs]
                    diag = (j >= 8 * c)
                    d = j - 8 * c if diag else -1
                    buf = i % 2
                    S0 = chunk_start[cs]
                    E.wait(t, f"{P}.x{i}")
                    if i == S0:
                        if cs == 1:
                            E.wait(t, f"{P}.ev0")
                        elif h >= 1:
                            E.wait(t, f"fl{h - 1}.ev1")
                    b4_last = S0 + 8 * c + 3
                    b5_last = S0 + 8 * c + 7
                    inst = None
                    for qt in range(8):
                        if diag and qt < d:
                            continue
                        start = (i == S0) and (qt == 0 or qt == 4)
                        stop = ((i == b4_last and qt == 3)
                                or (i == b5_last and qt == 7))
                        inst = t.matmul(
                            psB[:, qt * 128: qt * 128 + 65],
                            PB[:, buf * 1024 + qt * 128:
                               buf * 1024 + (qt + 1) * 128],
                            VB[:, hp * 4160 + j * 65: hp * 4160 + (j + 1) * 65],
                            start=start, stop=stop)
                    E.mark(inst, "tm", f"{P}.p{i}")
                    if cs == 0 and i == S0 + 8 * c + 7:
                        fa_pending[0] = True
                    if i + 2 < 72:
                        emit_s(i + 2)
                assert not fa_pending[0]
                if h % 2 == 1:
                    pair = h // 2
                    for bch in range(4):
                        if bch == 0:
                            E.wait(t, f"fl{h}.ev1")
                            if pair > 0:
                                E.wait(t, f"yt{pair - 1}.c3")
                        else:
                            E.wait(t, f"yt{pair}.c{bch - 1}")
                        for jj in range(4):
                            qt16 = bch * 4 + jj
                            E.mark(t.transpose(
                                psC[:, jj * 256: jj * 256 + 128],
                                YPAIR[:, qt16 * 128:(qt16 + 1) * 128],
                                CV["identb"]), "tm", f"yt{pair}.t{qt16}")
            ends.append(dict(E.n))
        E.switch_end(ends)

    @bfl.scalar
    def _(s):
        for h in range(HM):
            P = f"fl{h}"
            for i in range(72):
                E.wait(s, f"{P}.s{i}")
                if i >= 2:
                    E.wait(s, f"{P}.p{i - 2}")
                elif h >= 1:
                    E.wait(s, f"fl{h - 1}.p{70 + i}")
                buf = i % 2
                E.mark(s.activation(PB[:, buf * 1024:(buf + 1) * 1024],
                                    psA[:, buf * 1024:(buf + 1) * 1024],
                                    AF.Exp, scale=0.125), "so", f"{P}.x{i}")

    @bfl.vector
    def _(v):
        for h in range(HM):
            hp = h % 2
            P = f"fl{h}"
            for cs in range(2):
                if cs == 0:
                    E.wait(v, f"{P}.pA")
                    if h % 2 == 0 and h >= 2:
                        E.wait(v, f"yt{h // 2 - 1}.st")
                else:
                    E.wait(v, f"{P}.p71")
                rc = E.mark(v.reciprocal(
                    FSC[:, 0:8].rearrange("p (a b) -> p a b", b=1),
                    psB[:, :].rearrange("p (a b) -> p a b", a=8)[:, :, 64:65]),
                    "vo")
                v.wait_ge(E.h["vo"], rc)
                last = None
                for qt in range(8):
                    last = v.tensor_scalar(
                        YPAIR[:, (cs * 8 + qt) * 128 + hp * 64:
                              (cs * 8 + qt) * 128 + hp * 64 + 64],
                        psB[:, qt * 128: qt * 128 + 64],
                        FSC[:, qt:qt + 1], None, ALU.mult)
                E.mark(last, "vo", f"{P}.ev{cs}")
            if h % 2 == 1:
                pair = h // 2
                for bch in range(4):
                    E.wait(v, f"yt{pair}.t{bch * 4 + 3}")
                    last = None
                    for jj in range(4):
                        qt16 = bch * 4 + jj
                        last = v.tensor_copy(
                            YTS[:, qt16 * 128:(qt16 + 1) * 128],
                            psC[:, jj * 256: jj * 256 + 128])
                    E.mark(last, "vo", f"yt{pair}.c{bch}")

    @bfl.gpsimd
    def _(e):
        for pair in range(8):
            E.wait(e, f"yt{pair}.c3")
            E.mark(e.dma_start(out=DR["ybT"][pair * 128:(pair + 1) * 128, :],
                               in_=YTS[:, :]), "gd", f"yt{pair}.st", 16)

    cut("flash")

    mh.close()
    mo2 = ExitStack()
    _stacks.append(mo2)
    GSB = mo2.enter_context(nc.sbuf_tensor("GSB", [128, 128], F32))
    GEB = mo2.enter_context(nc.sbuf_tensor("GEB", [128, 128], F32))
    MSKB = mo2.enter_context(nc.sbuf_tensor("MSKB", [128, 128], F32))
    TAUL = mo2.enter_context(nc.sbuf_tensor("TAUL", [128, 128], F32))
    KEPTB = mo2.enter_context(nc.sbuf_tensor("KEPTB", [128, 128], F32))
    RKS = mo2.enter_context(nc.sbuf_tensor("RKS", [128, 128], F32))
    TIDG = mo2.enter_context(nc.sbuf_tensor("TIDG", [128, 32], F32))
    PXA = mo2.enter_context(nc.sbuf_tensor("PXA", [128, 16], F32))
    PXB = mo2.enter_context(nc.sbuf_tensor("PXB", [128, 16], F32))
    DMY = mo2.enter_context(nc.sbuf_tensor("DMY", [128, 16], F32))
    BIS = mo2.enter_context(nc.sbuf_tensor("BIS", [1, 16], F32))
    MIDC = mo2.enter_context(nc.sbuf_tensor("MIDC", [128, 1], F32))
    CNTR = mo2.enter_context(nc.sbuf_tensor("CNTR", [1, 128], F32))
    TROW = mo2.enter_context(nc.sbuf_tensor("TROW", [1, 128], F32))
    TAU8 = mo2.enter_context(nc.sbuf_tensor("TAU8", [8, 1], F32))
    PAT8S = mo2.enter_context(nc.sbuf_tensor("PAT8S", [8, 128], F32))
    RTOT = mo2.enter_context(nc.sbuf_tensor("RTOT", [128, 1], F32))
    CARR = mo2.enter_context(nc.sbuf_tensor("CARR", [128, 1], F32))
    IOTH = mo2.enter_context(nc.sbuf_tensor("IOTH", [128, 256], F16))
    TLC = mo2.enter_context(nc.sbuf_tensor("TLC", [128, 16], F16))
    PPB = mo2.enter_context(nc.sbuf_tensor("PPB", [128, 512], F16))
    TGE = mo2.enter_context(nc.sbuf_tensor("TGE", [128, 32], F16))
    XGB = mo2.enter_context(nc.sbuf_tensor("XGB", [128, 2048], F16))
    XIDX = mo2.enter_context(nc.sbuf_tensor("XIDX", [128, 16], U32))
    YIDX = mo2.enter_context(nc.sbuf_tensor("YIDX", [128, 16], U32))
    YRW = mo2.enter_context(nc.sbuf_tensor("YRW", [128, 16], U32))
    mh2p = ExitStack()
    _stacks.append(mh2p)
    PRE = mh2p.enter_context(nc.sbuf_tensor("PRE", [128, 16384], F32))
    mh2 = ExitStack()
    _stacks.append(mh2)
    WO2 = mh2.enter_context(nc.sbuf_tensor("WO2", [128, 8192], BF16))
    YTT = mh2.enter_context(nc.sbuf_tensor("YTT", [128, 2048], BF16))
    XF2 = mh2.enter_context(nc.sbuf_tensor("XF2", [128, 2048], F32))
    _dumpables["PRE"] = PRE

    bwo = blk()

    @bwo.gpsimd
    def _(e):
        E.wait(e, "yt7.st")
        E.mark(e.dma_start(out=WO2[:, :].rearrange("p (a o) -> p a o", a=8),
                           in_=IN["wo2"].rearrange("(a p) o -> p a o", p=128)),
               "gd", "wo2.w", 16)

    b_ = blk()
    for tt in range(NT):
        P = f"wo2.{tt}"

        @b_.sync
        def _(e, tt=tt, P=P):
            if tt == 0:
                E.wait(e, "yt7.st")
            if tt >= 2:
                E.wait(e, f"wo2.{tt - 2}.pre")
            E.mark(e.dma_start(
                out=YTT[:, (tt % 2) * 1024:(tt % 2 + 1) * 1024].rearrange(
                    "p (a c) -> p a c", a=8),
                in_=DR["ybT"][:, tt * 128:(tt + 1) * 128].rearrange(
                    "(a p) c -> p a c", p=128)), "sd", f"{P}.yt", 16)

        @b_.gpsimd
        def _(e, tt=tt, P=P):
            if tt >= 2:
                E.wait(e, f"wo2.{tt - 2}.pre")
            E.wait(e, f"ln.{tt}.x1f")
            E.mark(e.dma_start(out=XF2[:, (tt % 2) * 1024:(tt % 2 + 1) * 1024],
                               in_=X1F[tt * 128:(tt + 1) * 128, :]),
                   "gd", f"{P}.x1", 16)

        @b_.tensor
        def _(t, tt=tt, P=P):
            if tt == 0:
                E.wait(t, "wo2.w")
            E.wait(t, f"{P}.yt")
            if tt >= 2:
                E.wait(t, f"wo2.{tt - 2}.pre")
            ps = psB[:, :] if tt % 2 == 0 else psA[:, 0:1024]
            mms = []
            for dc in range(8):
                for half in range(2):
                    mms.append((ps[:, half * 512:(half + 1) * 512],
                                YTT[:, (tt % 2) * 1024 + dc * 128:
                                    (tt % 2) * 1024 + (dc + 1) * 128],
                                WO2[:, dc * 1024 + half * 512: dc * 1024 + (half + 1) * 512],
                                f"h{half}"))
            E.mark(mm_bank_group(t, mms), "tm", f"{P}.mm")

        @b_.vector
        def _(v, tt=tt, P=P):
            E.wait(v, f"{P}.mm")
            E.wait(v, f"{P}.x1")
            ps = psB[:, :] if tt % 2 == 0 else psA[:, 0:1024]
            E.mark(v.tensor_add(PRE[:, tt * 1024:(tt + 1) * 1024], ps,
                                XF2[:, (tt % 2) * 1024:(tt % 2 + 1) * 1024]),
                   "vo", f"{P}.pre")

    if DEBUG_STAGE == 2:
        bd2 = blk()

        @bd2.sync
        def _(e):
            for tt in range(NT):
                E.wait(e, f"wo2.{tt}.pre")
            for tt in range(NT):
                E.mark(e.dma_start(out=OUT[tt * 128:(tt + 1) * 128, :],
                                   in_=PRE[:, tt * 1024:(tt + 1) * 1024]),
                       "sd", f"dbg2.{tt}", 16)
            E.wait(e, f"dbg2.{NT - 1}")
        if _cur_blk[0] is not None:
            _cur_blk[0].__exit__(None, None, None)
            _cur_blk[0] = None
        mh2.close()
        mh2p.close()
        st.close()
        return nc

    if DEBUG_STAGE == 1:
        bdbg = blk()

        @bdbg.sync
        def _(e):
            for tt in range(NT):
                E.wait(e, f"ln.{tt}.x1f")
            for tt in range(NT):
                E.mark(e.dma_start(out=OUT[tt * 128:(tt + 1) * 128, :],
                                   in_=X1F[tt * 128:(tt + 1) * 128, :]),
                       "sd", f"dbg.{tt}", 16)
            E.wait(e, f"dbg.{NT - 1}")
        if _cur_blk[0] is not None:
            _cur_blk[0].__exit__(None, None, None)
            _cur_blk[0] = None
        mh2.close()
        mh2p.close()
        mo2.close()
        st.close()
        return nc

    mh2.close()
    mo1 = ExitStack()
    _stacks.append(mo1)
    XH = mo1.enter_context(nc.sbuf_tensor("XH", [128, 16384], F16))
    XTR = mo1.enter_context(nc.sbuf_tensor("XTR", [128, 16384], F16))
    JNK = mo1.enter_context(nc.sbuf_tensor("JNK", [128, 1024], F32))
    WRS = mo1.enter_context(nc.sbuf_tensor("WRS", [128, 64], F16))
    GAB = mo1.enter_context(nc.sbuf_tensor("GAB", [128, 1024], F32))
    MSB = mo1.enter_context(nc.sbuf_tensor("MSB", [128, 1024], F32))
    ROUT = mo1.enter_context(nc.sbuf_tensor("ROUT", [128, 64], F32))

    b4a = blk()

    @b4a.sync
    def _(e):
        for tt in range(NT):
            E.wait(e, f"wo2.{tt}.pre")
            E.mark(e.dma_start(out=DR["PREd"][tt * 128:(tt + 1) * 128, :],
                               in_=PRE[:, tt * 1024:(tt + 1) * 1024]),
                   "sd", f"p4.pred{tt}", 16)

    @b4a.scalar
    def _(s):
        for tt in range(NT):
            E.wait(s, f"wo2.{tt}.pre")
            E.mark(s.activation(JNK[:, :], PRE[:, tt * 1024:(tt + 1) * 1024],
                                AF.Square,
                                accum_out=SMS[:, 528 + tt: 529 + tt]),
                   "so", f"p4.sq{tt}")

    @b4a.vector
    def _(v):
        E.wait(v, "p4.sq15")
        E.mark(v.tensor_scalar(SMS[:, 544:560], SMS[:, 528:544],
                               1.0 / 1024.0, 1e-6, ALU.mult, ALU.add),
               "vo", "p4.vv")

    @b4a.scalar
    def _(s):
        E.wait(s, "p4.vv")
        E.mark(s.activation(SMS[:, 560:576], SMS[:, 544:560], AF.Sqrt),
               "so", "p4.sd2")

    @b4a.vector
    def _(v):
        E.wait(v, "p4.sd2")
        rc = E.mark(v.reciprocal(SMS[:, 576:592], SMS[:, 560:576]), "vo")
        v.wait_ge(E.h["vo"], rc)
        for tt in range(NT):
            E.mark(v.tensor_scalar(XH[:, tt * 1024:(tt + 1) * 1024],
                                   PRE[:, tt * 1024:(tt + 1) * 1024],
                                   SMS[:, 576 + tt: 577 + tt], None, ALU.mult),
                   "vo", f"p4.xh{tt}")

    @b4a.scalar
    def _(s):
        for tt in range(NT):
            E.wait(s, f"p4.xh{tt}")
            E.mark(s.dma_start(out=DR["xsd"][tt * 128:(tt + 1) * 128, :],
                               in_=XH[:, tt * 1024:(tt + 1) * 1024]),
                   "hd", f"p4.xsd{tt}", 16)

    @b4a.gpsimd
    def _(e):
        E.mark(e.dma_start(out=IOTH[:, :], in_=IN["c_iotah"][:, :]), "gd", None, 16)
        E.mark(e.dma_start(out=TLC[:, :], in_=IN["c_tloch"][:, :]), "gd", None, 16)
        E.mark(e.dma_start(out=DMY[:, :], in_=IN["c_dumy"][:, :]), "gd", None, 16)
        E.mark(e.dma_start(out=MSB[:, :], in_=IN["c_moemask"][:, :]), "gd", "p4.msb", 16)
        E.mark(e.dma_start(out=YRW[:, :], in_=IN["c_yrows"][:, :]), "gd", "p4.yrw", 16)
        E.mark(e.dma_start(out=PAT8S[:, :], in_=IN["c_pat8"][:, :]), "gd", "p4.p8", 16)
        E.mark(e.dma_start(out=WRS[:, :].rearrange("p (a j) -> p a j", a=8),
                           in_=IN["wrout"].rearrange("(a p) j -> p a j", p=128)),
               "gd", "p4.wrs", 16)
        E.wait(e, "p4.sq15")
        E.mark(e.memset(JNK[:, :], 0.0), "go", "p4.jz")
        E.wait(e, "p4.jz")
        for zi in range(18):
            E.mark(e.dma_start(out=DR["ystage"][zi * 128:(zi + 1) * 128, :],
                               in_=JNK[:, :]),
                   "gd", "p4.yz" if zi == 17 else None, 16)

    @b4a.vector
    def _(v):
        E.wait(v, "p4.wrs")
        last = None
        for a in range(8):
            last = v.tensor_scalar(WRS[:, a * 8:(a + 1) * 8],
                                   WRS[:, a * 8:(a + 1) * 8],
                                   CF["g2c"][:, a:a + 1], None, ALU.mult)
        E.mark(last, "vo", "p4.wrf")

    @b4a.sync
    def _(e):
        for tt in range(NT):
            E.wait(e, f"p4.xsd{tt}")
        for dc in range(8):
            E.mark(e.dma_start(out=XTR[:, dc * 2048:(dc + 1) * 2048],
                               in_=DR["xsd"][:, dc * 128:(dc + 1) * 128],
                               transpose=True), "sd", f"p4.xtr{dc}", 16)

    cut("p4a")
    b4b = blk()
    for ts_ in range(NT):
        @b4b.tensor
        def _(t, ts_=ts_):
            if ts_ == 0:
                E.wait(t, "p4.xtr7")
                E.wait(t, "p4.wrf")
            if ts_ >= 2:
                E.wait(t, f"p4.lge{ts_ - 2}")
            last = None
            for dc in range(8):
                last = t.matmul(
                    psB[:, (ts_ % 2) * 512:(ts_ % 2) * 512 + 8],
                    XTR[:, dc * 2048 + ts_ * 128: dc * 2048 + (ts_ + 1) * 128],
                    WRS[:, dc * 8:(dc + 1) * 8],
                    start=(dc == 0), stop=(dc == 7))
            E.mark(last, "tm", f"p4.lg{ts_}")

        @b4b.vector
        def _(v, ts_=ts_):
            w = vchain(v, E)
            E.wait(v, f"p4.lg{ts_}")
            w(v.tensor_copy(ROUT[:, 0:8],
                            psB[:, (ts_ % 2) * 512:(ts_ % 2) * 512 + 8]),
              f"p4.lge{ts_}")
            w(v.max(ROUT[:, 8:16], ROUT[:, 0:8]), f"p4.nm{ts_}")

        @b4b.scalar
        def _(s, ts_=ts_):
            E.wait(s, f"p4.nm{ts_}")
            s.activation(ROUT[:, 16:24], ROUT[:, 0:8], AF.Exp)
            E.mark(s.activation(ROUT[:, 25:26], ROUT[:, 9:10], AF.Exp),
                   "so", f"p4.ex{ts_}")

        @b4b.vector
        def _(v, ts_=ts_):
            w = vchain(v, E)
            E.wait(v, f"p4.ex{ts_}")
            w(v.reduce_sum(ROUT[:, 26:27],
                           ROUT[:, 16:24].rearrange("p (x a) -> p x a", x=1),
                           AX.X))
            w(v.reciprocal(ROUT[:, 27:28], ROUT[:, 26:27]))
            w(v.tensor_scalar(ROUT[:, 16:24], ROUT[:, 16:24], ROUT[:, 27:28],
                              None, ALU.mult))
            w(v.tensor_mul(ROUT[:, 28:29], ROUT[:, 25:26], ROUT[:, 27:28]))
            w(v.tensor_scalar(ROUT[:, 29:37], ROUT[:, 16:24], ROUT[:, 28:29],
                              None, ALU.is_ge))
            w(v.tensor_mul(GSB[:, ts_ * 8:(ts_ + 1) * 8], ROUT[:, 29:37],
                           ROUT[:, 16:24]), f"p4.g{ts_}")

    @b4b.sync
    def _(e):
        for ts_ in range(NT):
            E.wait(e, f"p4.g{ts_}")
        E.mark(e.dma_start(
            out=DR["gdram"][:, :].rearrange("(a p) j -> p a j", p=128),
            in_=GSB[:, :].rearrange("p (a j) -> p a j", a=16)),
            "sd", "p4.gst", 16)

    @b4b.gpsimd
    def _(e):
        E.wait(e, "p4.gst")
        E.mark(e.collective_compute("AllGather", ALU.bypass,
                                    replica_groups=[[0, 1, 2, 3, 4, 5, 6, 7]],
                                    ins=[DR["gdram"].tensor.ap()],
                                    outs=[DR["gag"].tensor.ap()]), "cc", "p4.gag")

    @b4b.sync
    def _(e):
        E.wait(e, "p4.gag")
        for r in range(8):
            E.mark(e.dma_start(
                out=GAB[:, r * 128:(r + 1) * 128].rearrange(
                    "p (a j) -> p a j", a=16),
                in_=DR["gag"][r * 2048:(r + 1) * 2048, :].rearrange(
                    "(a p) j -> p a j", p=128)),
                "sd", f"p4.ga{r}", 16)

    @b4b.vector
    def _(v):
        for r in range(8):
            E.wait(v, f"p4.ga{r}")
        E.wait(v, "p4.msb")
        v.tensor_mul(GAB[:, :], GAB[:, :], MSB[:, :])
        E.mark(v.reduce_sum(GEB[:, :],
                            GAB[:, :].rearrange("p (a j) -> p a j", j=8), AX.X),
               "vo", "p4.ge")

    _dumpables["GSB"] = GSB
    _dumpables["GEB"] = GEB
    _dumpables["KEPTB"] = KEPTB
    _dumpables["RKS"] = RKS
    _dumpables["TIDG"] = TIDG
    _dumpables["BIS"] = BIS
    _dumpables["TAUL"] = TAUL
    cut("p4b")
    b4c = blk()

    @b4c.vector
    def _(v):
        E.wait(v, "p4.ge")
        v.memset(BIS[0:1, 0:1], 0.0)
        E.mark(v.memset(BIS[0:1, 1:2], 1.0), "vo", "p4.bis0")

    NBIS = 40
    for it in range(NBIS):
        P = f"p4.b{it}"

        @b4c.vector
        def _(v, it=it, P=P):
            w = vchain(v, E)
            E.wait(v, "p4.bis0" if it == 0 else f"p4.b{it - 1}.upd")
            w(v.tensor_add(BIS[0:1, 2:3], BIS[0:1, 0:1], BIS[0:1, 1:2]))
            w(v.tensor_scalar(BIS[0:1, 2:3], BIS[0:1, 2:3], 0.5, None,
                              ALU.mult), f"{P}.mid")

        @b4c.tensor
        def _(t, it=it, P=P):
            E.wait(t, f"{P}.mid")
            E.mark(t.matmul(psD[:, 0:1], CF["onesrow"], BIS[0:1, 2:3],
                            start=True, stop=True), "tm", f"{P}.rep")

        @b4c.vector
        def _(v, it=it, P=P):
            w = vchain(v, E)
            E.wait(v, f"{P}.rep")
            w(v.tensor_copy(MIDC[:, :], psD[:, 0:1]))
            w(v.tensor_scalar(MSKB[:, :], GEB[:, :], MIDC[:, 0:1], None,
                              ALU.is_ge), f"{P}.msk")

        @b4c.tensor
        def _(t, it=it, P=P):
            E.wait(t, f"{P}.msk")
            E.mark(t.matmul(psD[0:1, 256:384], CF["onescol"], MSKB[:, :],
                            start=True, stop=True), "tm", f"{P}.cnt")

        @b4c.vector
        def _(v, it=it, P=P):
            w = vchain(v, E)
            E.wait(v, f"{P}.cnt")
            w(v.tensor_copy(CNTR[:, :], psD[0:1, 256:384]))
            w(v.reduce_sum(BIS[0:1, 3:4],
                           CNTR[0:1, :].rearrange("p (x a) -> p x a", x=1), AX.X))
            w(v.tensor_scalar(BIS[0:1, 4:5], BIS[0:1, 3:4], float(CAP), None,
                              ALU.is_ge))
            w(v.tensor_sub(BIS[0:1, 5:6], BIS[0:1, 2:3], BIS[0:1, 0:1]))
            w(v.tensor_mul(BIS[0:1, 5:6], BIS[0:1, 4:5], BIS[0:1, 5:6]))
            w(v.tensor_add(BIS[0:1, 0:1], BIS[0:1, 0:1], BIS[0:1, 5:6]))
            w(v.tensor_sub(BIS[0:1, 6:7], BIS[0:1, 1:2], BIS[0:1, 2:3]))
            w(v.tensor_mul(BIS[0:1, 6:7], BIS[0:1, 4:5], BIS[0:1, 6:7]))
            w(v.tensor_add(BIS[0:1, 1:2], BIS[0:1, 2:3], BIS[0:1, 6:7]),
              f"{P}.upd")

    cut("p4c")
    b4d = blk()

    @b4d.sync
    def _(e):
        E.wait(e, f"p4.b{NBIS - 1}.upd")
        E.mark(e.dma_start(out=DR["taud"][:, :], in_=BIS[0:1, 0:1]),
               "sd", "p4.tau", 16)

    @b4d.gpsimd
    def _(e):
        E.wait(e, "p4.tau")
        E.mark(e.collective_compute("AllGather", ALU.bypass,
                                    replica_groups=[[0, 1, 2, 3, 4, 5, 6, 7]],
                                    ins=[DR["taud"].tensor.ap()],
                                    outs=[DR["tau8d"].tensor.ap()]), "cc", "p4.tag")

    @b4d.sync
    def _(e):
        E.wait(e, "p4.tag")
        E.mark(e.dma_start(out=TAU8[:, :], in_=DR["tau8d"][:, :]),
               "sd", "p4.t8", 16)

    @b4d.tensor
    def _(t):
        E.wait(t, "p4.t8")
        E.wait(t, "p4.p8")
        E.mark(t.matmul(psD[0:1, 384:512], TAU8[:, :], PAT8S[:, :],
                        start=True, stop=True), "tm", "p4.trow")

    @b4d.vector
    def _(v):
        E.wait(v, "p4.trow")
        E.mark(v.tensor_copy(TROW[:, :], psD[0:1, 384:512]), "vo", "p4.trc")

    @b4d.tensor
    def _(t):
        E.wait(t, "p4.trc")
        E.mark(t.matmul(psD[:, 128:256], CF["onesrow"], TROW[0:1, :],
                        start=True, stop=True), "tm", "p4.tfull")

    @b4d.vector
    def _(v):
        w = vchain(v, E)
        E.wait(v, "p4.tfull")
        w(v.tensor_copy(TAUL[:, :], psD[:, 128:256]))
        w(v.tensor_tensor(KEPTB[:, :], GSB[:, :], TAUL[:, :], ALU.is_ge))
        w(v.tensor_scalar(MSKB[:, :], GSB[:, :], 0.0, None, ALU.is_gt))
        w(v.tensor_mul(KEPTB[:, :], KEPTB[:, :], MSKB[:, :]), "p4.kept")

    KEPT3 = KEPTB[:, :].rearrange("p (a j) -> p a j", j=8)
    for j in range(NE):
        P = f"p4.rk{j}"

        @b4d.vector
        def _(v, j=j, P=P):
            if j == 0:
                E.wait(v, "p4.kept")
            E.mark(v.reduce_sum(
                RTOT[:, :],
                KEPT3[:, :, j].rearrange("p (x a) -> p x a", x=1), AX.X),
                "vo", f"{P}.rt")

        @b4d.tensor
        def _(t, j=j, P=P):
            E.wait(t, f"{P}.rt")
            E.mark(t.matmul(psD[:, 0:1], CF["ustrict"], RTOT[:, :],
                            start=True, stop=True), "tm", f"{P}.ca")

        @b4d.vector
        def _(v, j=j, P=P):
            w = vchain(v, E)
            E.wait(v, f"{P}.ca")
            w(v.tensor_copy(CARR[:, :], psD[:, 0:1]))
            KJ = KEPT3[:, :, j]
            v.memset(PXA[:, 0:1], 0.0)
            w(v.tensor_copy(PXA[:, 1:16], KJ[:, 0:15]))
            cur, oth = PXA, PXB
            for k in (1, 2, 4, 8):
                v.tensor_copy(oth[:, 0:k], cur[:, 0:k])
                w(v.tensor_add(oth[:, k:16], cur[:, k:16], cur[:, 0:16 - k]))
                cur, oth = oth, cur
            w(v.tensor_scalar(cur[:, :], cur[:, :], CARR[:, 0:1], 512.0,
                              ALU.add, ALU.add))
            w(v.scalar_tensor_tensor(RKS[:, j * 16:(j + 1) * 16], KJ,
                                     -512.0, cur[:, :], ALU.mult, ALU.add),
              f"{P}.rk")

    for j in range(NE):
        P = f"p4.tg{j}"

        @b4d.vector
        def _(v, j=j, P=P):
            if j == 0:
                for jj in range(NE):
                    E.wait(v, f"p4.rk{jj}.rk")
            v.tensor_copy(
                TGE[:, :].rearrange("p (a x) -> p a x", x=2)[:, :, 0], TLC[:, :])
            E.mark(v.tensor_copy(
                TGE[:, :].rearrange("p (a x) -> p a x", x=2)[:, :, 1],
                GSB[:, :].rearrange("p (a j) -> p a j", j=8)[:, :, j]),
                "vo", f"{P}.tge")

        for a in range(16):
            @b4d.vector
            def _(v, j=j, a=a, P=P):
                if a == 0:
                    E.wait(v, f"{P}.tge")
                if a >= 2:
                    E.wait(v, f"{P}.mm{a - 2}")
                E.mark(v.tensor_scalar(PPB[:, (a % 2) * 256:(a % 2) * 256 + 256],
                                       IOTH[:, :],
                                       RKS[:, j * 16 + a: j * 16 + a + 1],
                                       None, ALU.is_equal),
                       "vo", f"{P}.pp{a}")

            @b4d.tensor
            def _(t, j=j, a=a, P=P):
                E.wait(t, f"{P}.pp{a}")
                if a == 0 and j > 0:
                    E.wait(t, f"p4.tg{j - 1}.ev")
                t.matmul(psD[:, 0:2], PPB[:, (a % 2) * 256:(a % 2) * 256 + 128],
                         TGE[:, 2 * a: 2 * a + 2],
                         start=(a == 0), stop=(a == 15))
                E.mark(t.matmul(psB[:, 0:2],
                                PPB[:, (a % 2) * 256 + 128:(a % 2) * 256 + 256],
                                TGE[:, 2 * a: 2 * a + 2],
                                start=(a == 0), stop=(a == 15)),
                       "tm", f"{P}.mm{a}")

        @b4d.vector
        def _(v, j=j, P=P):
            E.wait(v, f"{P}.mm15")
            v.tensor_copy(TIDG[:, j * 4: j * 4 + 2], psD[:, 0:2])
            E.mark(v.tensor_copy(TIDG[:, j * 4 + 2: j * 4 + 4], psB[:, 0:2]),
                   "vo", f"{P}.ev")

    @b4d.vector
    def _(v):
        w = vchain(v, E)
        for j in range(NE):
            E.wait(v, f"p4.tg{j}.ev")
        for c in range(16):
            base = (c // 2) * 4 + (c % 2) * 2
            tidc = TIDG[:, base: base + 1]
            gc = TIDG[:, base + 1: base + 2]
            v.tensor_copy(XIDX[:, c:c + 1], tidc)
            w(v.tensor_scalar(ROUT[:, 0:1], gc, 0.0, None, ALU.is_equal))
            w(v.tensor_sub(ROUT[:, 1:2], DMY[:, c:c + 1], tidc))
            w(v.tensor_mul(ROUT[:, 1:2], ROUT[:, 0:1], ROUT[:, 1:2]))
            w(v.tensor_add(ROUT[:, 2:3], tidc, ROUT[:, 1:2]))
            w(v.tensor_copy(YIDX[:, c:c + 1], ROUT[:, 2:3]),
              "p4.idx" if c == 15 else None)

    cut("p4d")
    b4e = blk()
    for c in range(16):
        @b4e.gpsimd
        def _(e, c=c):
            if c == 0:
                E.wait(e, "p4.idx")
                for tt in range(NT):
                    E.wait(e, f"p4.xsd{tt}")
            if c >= 2:
                E.wait(e, f"p4.xc{c - 2}")
            E.mark(e.indirect_dma_start(
                XGB[:, (c % 2) * 1024:(c % 2 + 1) * 1024], None,
                DR["xsd"][:, :], IOA(XIDX[:, c:c + 1], 0)),
                "gs", f"p4.xg{c}", 16)

        @b4e.vector
        def _(v, c=c):
            E.wait(v, f"p4.xg{c}")
            if c == 0:
                E.wait(v, "p4.idx")
            base = (c // 2) * 4 + (c % 2) * 2
            E.mark(v.tensor_scalar(XGB[:, (c % 2) * 1024:(c % 2 + 1) * 1024],
                                   XGB[:, (c % 2) * 1024:(c % 2 + 1) * 1024],
                                   TIDG[:, base + 1: base + 2], None, ALU.mult),
                   "vo", f"p4.xs{c}")

        @b4e.scalar
        def _(s, c=c):
            E.wait(s, f"p4.xs{c}")
            E.mark(s.dma_start(out=DR["xct"][c * 128:(c + 1) * 128, :],
                               in_=XGB[:, (c % 2) * 1024:(c % 2 + 1) * 1024]),
                   "hd", f"p4.xc{c}", 16)

    @b4e.gpsimd
    def _(e):
        for c in range(16):
            E.wait(e, f"p4.xc{c}")
        E.mark(e.collective_compute("AllGather", ALU.bypass,
                                    replica_groups=[[0, 1, 2, 3, 4, 5, 6, 7]],
                                    ins=[DR["xct"].tensor.ap()],
                                    outs=[DR["xctg"].tensor.ap()]), "cc", "p4.xag")

    cut("p4e")
    mo1.close()
    mh2p.close()
    mo3 = ExitStack()
    _stacks.append(mo3)
    XET = mo3.enter_context(nc.sbuf_tensor("XET", [128, 16384], F16))
    W1S = mo3.enter_context(nc.sbuf_tensor("W1S", [128, 16384], F16))
    W2S = mo3.enter_context(nc.sbuf_tensor("W2S", [128, 16384], F16))
    HTB = mo3.enter_context(nc.sbuf_tensor("HTB", [128, 16384], F16))
    YCB = mo3.enter_context(nc.sbuf_tensor("YCB", [128, 2048], F16))
    YB = mo3.enter_context(nc.sbuf_tensor("YB", [128, 2048], F16))
    YBF = mo3.enter_context(nc.sbuf_tensor("YBF", [128, 2048], F32))

    b4f = blk()

    @b4f.gpsimd
    def _(e):
        E.mark(e.dma_start(out=W1S[:, :].rearrange("p (a f) -> p a f", a=8),
                           in_=IN["w1c"].rearrange("(a p) f -> p a f", p=128)),
               "gd", "p4.w1", 16)
        E.mark(e.dma_start(out=W2S[:, :].rearrange("p (a f) -> p a f", a=16),
                           in_=IN["w2c"].rearrange("(a p) f -> p a f", p=128)),
               "gd", "p4.w2", 16)

    @b4f.vector
    def _(v):
        E.wait(v, "p4.w1")
        last = None
        for dc in range(8):
            last = v.tensor_scalar(W1S[:, dc * 2048:(dc + 1) * 2048],
                                   W1S[:, dc * 2048:(dc + 1) * 2048],
                                   CF["g2c"][:, dc:dc + 1], None, ALU.mult)
        E.mark(last, "vo", "p4.w1f")

    @b4f.scalar
    def _(s):
        E.wait(s, "p4.xag")
        rk = s.partition_id() % 8
        snap = dict(E.n)
        ends = []
        for g8 in nc.Switch(engines=[s], index=[rk], n=8):
            E.switch_body_begin(snap)
            for r in range(8):
                for dc in range(8):
                    nm = "p4.xet" if (r == 7 and dc == 7) else None
                    E.mark(s.dma_start(
                        out=XET[:, dc * 2048 + r * 256: dc * 2048 + (r + 1) * 256],
                        in_=DR["xctg"][r * 2048 + g8 * 256:
                                       r * 2048 + (g8 + 1) * 256,
                                       dc * 128:(dc + 1) * 128],
                        transpose=True), "hd", nm, 16)
            ends.append(dict(E.n))
        E.switch_end(ends)

    for Jh in range(2):
        for fc in range(16):
            @b4f.tensor
            def _(t, Jh=Jh, fc=fc):
                if Jh == 0 and fc == 0:
                    E.wait(t, "p4.xet")
                    E.wait(t, "p4.w1f")
                if Jh == 1 and fc == 0:
                    E.wait(t, "p4.y0.7")
                if fc >= 2:
                    E.wait(t, f"p4.r{Jh}.{fc - 2}")
                elif Jh == 1:
                    E.wait(t, f"p4.r0.{14 + fc}")
                last = None
                for dc in range(8):
                    for hv in range(2):
                        last = t.matmul(
                            psA[:, (fc % 2) * 1024 + hv * 512:
                                (fc % 2) * 1024 + (hv + 1) * 512],
                            W1S[:, dc * 2048 + fc * 128: dc * 2048 + (fc + 1) * 128],
                            XET[:, dc * 2048 + Jh * 1024 + hv * 512:
                                dc * 2048 + Jh * 1024 + (hv + 1) * 512],
                            start=(dc == 0), stop=(dc == 7))
                E.mark(last, "tm", f"p4.h{Jh}.{fc}")

            @b4f.scalar
            def _(s, Jh=Jh, fc=fc):
                E.wait(s, f"p4.h{Jh}.{fc}")
                E.mark(s.activation(HTB[:, fc * 1024:(fc + 1) * 1024],
                                    psA[:, (fc % 2) * 1024:(fc % 2 + 1) * 1024],
                                    AF.Relu), "so", f"p4.r{Jh}.{fc}")

        for Js in range(8):
            @b4f.tensor
            def _(t, Jh=Jh, Js=Js):
                if Js == 0:
                    E.wait(t, f"p4.r{Jh}.15")
                    if Jh == 0:
                        E.wait(t, "p4.w2")
                if Jh > 0 or Js > 0:
                    pj, ps_ = (Jh, Js - 1) if Js > 0 else (0, 7)
                    E.wait(t, f"p4.ye{pj}.{ps_}")
                last = None
                for fc in range(16):
                    for hv in range(2):
                        last = t.matmul(
                            psB[:, hv * 512:(hv + 1) * 512],
                            HTB[:, fc * 1024 + Js * 128: fc * 1024 + (Js + 1) * 128],
                            W2S[:, fc * 1024 + hv * 512: fc * 1024 + (hv + 1) * 512],
                            start=(fc == 0), stop=(fc == 15))
                E.mark(last, "tm", f"p4.y{Jh}.{Js}")

            @b4f.vector
            def _(v, Jh=Jh, Js=Js):
                E.wait(v, f"p4.y{Jh}.{Js}")
                c = Jh * 8 + Js
                if c >= 2:
                    E.wait(v, f"p4.yc{c - 2}")
                E.mark(v.tensor_copy(YCB[:, (Js % 2) * 1024:(Js % 2 + 1) * 1024],
                                     psB[:, 0:1024]), "vo", f"p4.ye{Jh}.{Js}")

            @b4f.sync
            def _(e, Jh=Jh, Js=Js):
                E.wait(e, f"p4.ye{Jh}.{Js}")
                c = Jh * 8 + Js
                E.mark(e.dma_start(out=DR["ycd"][c * 128:(c + 1) * 128, :],
                                   in_=YCB[:, (Js % 2) * 1024:(Js % 2 + 1) * 1024]),
                       "sd", f"p4.yc{c}", 16)

    @b4f.gpsimd
    def _(e):
        for c in range(16):
            E.wait(e, f"p4.yc{c}")
        E.mark(e.collective_compute("AllGather", ALU.bypass,
                                    replica_groups=[[0, 1, 2, 3, 4, 5, 6, 7]],
                                    ins=[DR["ycd"].tensor.ap()],
                                    outs=[DR["ycg"].tensor.ap()]), "cc", "p4.yag")

    cut("p4f")
    for c in range(16):
        @b4f.gpsimd
        def _(e, c=c):
            if c == 0:
                E.wait(e, "p4.yag")
                E.wait(e, "p4.yrw")
            if c >= 2:
                E.wait(e, f"p4.ysc{c - 2}")
            E.mark(e.indirect_dma_start(
                YB[:, (c % 2) * 1024:(c % 2 + 1) * 1024], None,
                DR["ycg"][:, :], IOA(YRW[:, c:c + 1], 0)),
                "gs", f"p4.yg{c}", 16)
            E.wait(e, f"p4.yg{c}")
            if c == 0:
                E.wait(e, "p4.yz")
            else:
                E.wait(e, f"p4.ysc{c - 1}")
            E.mark(e.indirect_dma_start(
                DR["ystage"][:, :], IOA(YIDX[:, c:c + 1], 0),
                YB[:, (c % 2) * 1024:(c % 2 + 1) * 1024], None,
                oob_is_err=True, compute_op=ALU.add),
                "gs", f"p4.ysc{c}", 16)

    cut("p4g")
    mo3.close()
    mo4 = ExitStack()
    _stacks.append(mo4)
    F1 = mo4.enter_context(nc.sbuf_tensor("F1", [128, 2048], F32))
    F2 = mo4.enter_context(nc.sbuf_tensor("F2", [128, 2048], F32))
    F3 = mo4.enter_context(nc.sbuf_tensor("F3", [128, 2048], F32))
    OSB = mo4.enter_context(nc.sbuf_tensor("OSB", [128, 2048], F32))
    JNK2 = mo4.enter_context(nc.sbuf_tensor("JNK2", [128, 1024], F32))

    b5 = blk()
    for tt in range(NT):
        P = f"p5.{tt}"
        buf = tt % 2
        fsl = F1[:, buf * 1024:(buf + 1) * 1024]
        osl = OSB[:, buf * 1024:(buf + 1) * 1024]
        base = 592 + buf * 16

        f2sl = F2[:, buf * 1024:(buf + 1) * 1024]
        f3sl = F3[:, buf * 1024:(buf + 1) * 1024]

        @b5.sync
        def _(e, tt=tt, P=P, fsl=fsl):
            if tt == 0:
                E.wait(e, "p4.ysc15")
            if tt >= 2:
                E.wait(e, f"p5.{tt - 2}.x")
            E.mark(e.dma_start(out=fsl, in_=X1F[tt * 128:(tt + 1) * 128, :]),
                   "sd", f"{P}.l0", 16)

        @b5.scalar
        def _(s, tt=tt, P=P, f2sl=f2sl):
            if tt == 0:
                E.wait(s, "p4.ysc15")
            if tt >= 2:
                E.wait(s, f"p5.{tt - 2}.x")
            E.mark(s.dma_start(out=f2sl,
                               in_=DR["PREd"][tt * 128:(tt + 1) * 128, :]),
                   "hd", f"{P}.l1", 16)

        @b5.gpsimd
        def _(e, tt=tt, P=P, f3sl=f3sl):
            if tt == 0:
                E.wait(e, "p4.ysc15")
            if tt >= 2:
                E.wait(e, f"p5.{tt - 2}.x")
            E.mark(e.dma_start(out=f3sl,
                               in_=DR["ystage"][tt * 128:(tt + 1) * 128, :]),
                   "gd", f"{P}.l2", 16)

        @b5.vector
        def _(v, tt=tt, P=P, fsl=fsl, f2sl=f2sl, f3sl=f3sl, base=base):
            w = vchain(v, E)
            E.wait(v, f"{P}.l0")
            E.wait(v, f"{P}.l1")
            E.wait(v, f"{P}.l2")
            w(v.tensor_add(fsl, fsl, f2sl))
            w(v.tensor_add(fsl, fsl, f3sl), f"{P}.r")
            E.mark(v.reduce_sum(SMS[:, base:base + 1],
                                fsl.rearrange("p (x a) -> p x a", x=1), AX.X),
                   "vo", f"{P}.sum")

        @b5.scalar
        def _(s, tt=tt, P=P, fsl=fsl, base=base):
            E.wait(s, f"{P}.sum")
            E.mark(s.activation(JNK2[:, :], fsl, AF.Square,
                                accum_out=SMS[:, base + 1: base + 2]),
                   "so", f"{P}.ssq")

        @b5.vector
        def _(v, tt=tt, P=P, base=base):
            w = vchain(v, E)
            E.wait(v, f"{P}.ssq")
            v.tensor_scalar(SMS[:, base + 2: base + 3], SMS[:, base: base + 1],
                            1.0 / 1024.0, None, ALU.mult)
            w(v.tensor_scalar(SMS[:, base + 3: base + 4],
                              SMS[:, base + 1: base + 2], 1.0 / 1024.0, None,
                              ALU.mult))
            w(v.tensor_mul(SMS[:, base + 4: base + 5], SMS[:, base + 2: base + 3],
                           SMS[:, base + 2: base + 3]))
            w(v.tensor_sub(SMS[:, base + 3: base + 4], SMS[:, base + 3: base + 4],
                           SMS[:, base + 4: base + 5]))
            w(v.tensor_scalar(SMS[:, base + 3: base + 4],
                              SMS[:, base + 3: base + 4], 1.0, 1e-5,
                              ALU.mult, ALU.add), f"{P}.var")

        @b5.scalar
        def _(s, tt=tt, P=P, base=base):
            E.wait(s, f"{P}.var")
            E.mark(s.activation(SMS[:, base + 5: base + 6],
                                SMS[:, base + 3: base + 4], AF.Sqrt),
                   "so", f"{P}.sdv")

        @b5.vector
        def _(v, tt=tt, P=P, fsl=fsl, osl=osl, base=base):
            E.wait(v, f"{P}.sdv")
            rc = E.mark(v.reciprocal(SMS[:, base + 6: base + 7],
                                     SMS[:, base + 5: base + 6]), "vo")
            v.wait_ge(E.h["vo"], rc)
            if tt >= 2:
                E.wait(v, f"p5.{tt - 2}.o")
            E.mark(v.tensor_scalar(osl, fsl, SMS[:, base + 2: base + 3],
                                   SMS[:, base + 6: base + 7],
                                   ALU.subtract, ALU.mult), "vo", f"{P}.x")

        @b5.sync
        def _(e, tt=tt, P=P, osl=osl):
            E.wait(e, f"{P}.x")
            E.mark(e.dma_start(out=OUT[tt * 128:(tt + 1) * 128, :], in_=osl),
                   "sd", f"{P}.o", 16)

    bfin = blk()

    @bfin.sync
    def _(e):
        for snm in E.n:
            if snm != "sd":
                e.wait_ge(E.h[snm], E.n[snm])
        e.wait_ge(E.h["sd"], E.n["sd"])

    if _cur_blk[0] is not None:
        _cur_blk[0].__exit__(None, None, None)
        _cur_blk[0] = None
    mo4.close()
    mo2.close()
    st.close()
    return nc


def build_program():
    try:
        return _build_program_inner(STOP_AFTER)
    except _StopEmission as e:
        return e.args[0]


def kernel(**inputs):
    in_maps = make_in_maps(inputs)
    nc = build_program()
    declared = set(nc._lazy_in.declared)
    in_maps = [{k: v for k, v in m.items() if k in declared} for m in in_maps]
    res = run_bass_kernel_spmd(nc, in_maps, core_ids=list(range(8)))
    return assemble_output(res.results)


if __name__ == "__main__":
    build_program()
    print("built OK")
